# version 21
# baseline (speedup 1.0000x reference)
import numpy as np
import ml_dtypes
import concourse.bass as bass
import concourse.mybir as mybir
from concourse.bass_utils import run_bass_kernel_spmd

F32 = mybir.dt.float32
BF16 = mybir.dt.bfloat16
AF = mybir.ActivationFunctionType
ALU = mybir.AluOpType
AX = mybir.AxisListType
bfdt = ml_dtypes.bfloat16

V, A, T, G, S = "vector", "scalar", "tensor", "gpsimd", "sync"
sV, sA, sT, sG, sS = "sV", "sA", "sT", "sGc", "sS"

SEGL = 1024
NSEG = 2
DM = 2048
DI = 4096
HFF = 5632
N_ECH = 67
N_HCH = 44
FFN_R = 8
EPS_LN = 1e-5
EPS_RMS = 1e-5
GAMMA_EPS = 1e-8
NW1 = float(HFF * DM)


class Pipe:
    ENGINES = ("sync", "scalar", "vector", "tensor", "gpsimd")
    CHAINED = {"vector": "sV", "scalar": "sA", "gpsimd": "sGc"}

    def __init__(self, nc):
        self.nc = nc
        self.ops = {e: [] for e in self.ENGINES}
        self.sem_names = []
        self.counts = {}
        self._last_compute = {}

    def new_sem(self, name):
        if name in self.counts:
            return name
        self.sem_names.append(name)
        self.counts[name] = 0
        return name

    def add(self, engine, fn, waits=(), inc=None, amt=1, _is_dma=False):
        ev = None
        waits = [w for w in waits if w is not None]
        if not _is_dma and engine in self.CHAINED:
            if inc is None:
                inc = self.CHAINED[engine]
            prev = self._last_compute.get(engine)
            if prev is not None:
                waits = waits + [prev]
        if engine == "tensor" and inc is None:
            inc = "sT"
        if inc is not None:
            self.new_sem(inc)
            self.counts[inc] += amt
            ev = (inc, self.counts[inc])
        if not _is_dma and engine in self.CHAINED:
            self._last_compute[engine] = ev
        self.ops[engine].append((fn, tuple(waits), inc, amt))
        return ev

    def dma(self, engine, out, in_, waits=(), sem=None, **kw):
        assert sem is not None
        self.new_sem(sem)
        return self.add(engine,
                        lambda eng, o=out, i=in_: eng.dma_start(out=o, in_=i, **kw),
                        waits=waits, inc=sem, amt=16, _is_dma=True)

    def last(self, engine):
        if engine == "tensor":
            n = self.counts.get("sT", 0)
            return ("sT", n) if n else None
        return self._last_compute.get(engine)

    def barrier(self):
        return [e for e in (self.last(V), self.last(A), self.last(T), self.last(G))
                if e is not None]

    def check(self):
        idx = {e: 0 for e in self.ENGINES}
        cnt = {s: 0 for s in self.sem_names}
        progress = True
        while progress:
            progress = False
            for e in self.ENGINES:
                while idx[e] < len(self.ops[e]):
                    fn, waits, inc, amt = self.ops[e][idx[e]]
                    if all(cnt[s] >= v for s, v in waits):
                        if inc is not None:
                            cnt[inc] += amt
                        idx[e] += 1
                        progress = True
                    else:
                        break
        stuck = {e: (idx[e], len(self.ops[e])) for e in self.ENGINES
                 if idx[e] < len(self.ops[e])}
        if stuck:
            detail = {}
            for e in stuck:
                fn, waits, inc, amt = self.ops[e][idx[e]]
                detail[e] = [(s, v, cnt[s]) for s, v in waits if cnt[s] < v]
            raise RuntimeError(f"Pipe deadlock: stuck={stuck} waits={detail}")

    def emit(self):
        self.check()
        nc = self.nc
        sems = {}
        import contextlib
        with contextlib.ExitStack() as stck:
            for s in self.sem_names:
                sems[s] = stck.enter_context(nc.semaphore(s))
            block = stck.enter_context(nc.Block())
            for e in self.ENGINES:
                if not self.ops[e]:
                    continue
                ops = self.ops[e]

                def section(eng, ops=ops, ename=e):
                    watermark = {}
                    for fn, waits, inc, amt in ops:
                        agg = {}
                        for s, v in waits:
                            if v > agg.get(s, -1):
                                agg[s] = v
                        for s, v in agg.items():
                            if watermark.get(s, -1) < v:
                                eng.wait_ge(sems[s], v)
                                watermark[s] = v
                        ins = fn(eng)
                        if inc is not None:
                            assert ins is not None, f"None instr on {ename}"
                            ins.then_inc(sems[inc], amt)
                getattr(block, e)(section)


class Overlay:
    def __init__(self, nc, base, size, tag):
        self.nc, self.base, self.size, self.tag = nc, base, size, tag
        self.off = 0
        self.n = 0

    def reset(self):
        self.off = 0

    def alloc(self, name, shape, dtype=F32):
        per_part = int(np.prod(shape[1:])) * mybir.dt.size(dtype)
        per_part = (per_part + 31) // 32 * 32
        assert self.off + per_part <= self.size, \
            f"overlay {self.tag} overflow: {name} {self.off}+{per_part}>{self.size}"
        self.n += 1
        t = self.nc.alloc_sbuf_tensor_at(f"{self.tag}{self.n}_{name}", shape, dtype,
                                         offset=self.base + self.off)
        self.off += per_part
        return t



def ssd_init(P, B, ev_consts):
    ev_dd = P.add(V, lambda v: v.tensor_copy(B["Ddiag"][:, :, :], B["Ddiag32"][:, :, :]),
                  waits=list(ev_consts))
    ev_s0 = P.add(V, lambda v: v.memset(B["STATES_bf"][:, :], 0.0))
    st = {
        "stage_ok": [ev_dd],
        "states_bf_ready": [ev_s0],
        "psh_free": None, "pY_free": [None, None], "pdiff_free": [None, None],
        "stage_free": [None, None], "dmin_free": [None, None], "E_free": [None, None],
        "ST_free": [None] * 4, "CE_free": [None] * 4,
        "pxT_free": None, "psmall_free": None, "cdrow_free": None,
        "xdt_freeT": None, "xdtd_freeT": None, "dtbf_freeT": None,
        "decaybf_freeT": None, "Btm_freeT": None, "cdb_freeT": None,
        "ST_CE_init": ev_dd,
    }
    return st


def emit_ssd_chunk(P, B, c, ev_ready, st):
    lo = c * 128
    sl = slice(lo, lo + 128)

    P.add(V, lambda v: v.tensor_scalar_mul(B["dA"][:, :], B["dt_hm"][:, sl], B["A_col"][:, 0:1]),
          waits=list(ev_ready["dt"]))
    ev_acs = P.add(V, lambda v: v.tensor_tensor_scan(B["Acs"][:, :], B["dA"][:, :], B["z64"][:, :], 0.0, ALU.add, ALU.add))
    ev_negacs = P.add(V, lambda v: v.tensor_scalar_mul(B["negAcs"][:, :], B["Acs"][:, :], -1.0))
    ev_tdec = P.add(V, lambda v: v.tensor_scalar(B["tdec"][:, :], B["Acs"][:, :], B["Acs"][:, 127:128], None, ALU.subtract))
    ev_dtbf = P.add(V, lambda v: v.tensor_copy(B["dtbf"][:, :], B["dt_hm"][:, sl]),
                    waits=[st["dtbf_freeT"]])
    ev_dec = P.add(A, lambda a: a.activation(B["decay"][:, :], B["tdec"][:, :], AF.Exp, scale=-1.0),
                   waits=[ev_tdec])
    ev_eacs = P.add(A, lambda a: a.activation(B["eacs"][:, :], B["Acs"][:, :], AF.Exp),
                    waits=[ev_acs])
    ev_cd = P.add(A, lambda a: a.activation(B["cd"][:, 0:1], B["Acs"][:, 127:128], AF.Exp),
                  waits=[ev_acs])
    ev_decbf = P.add(V, lambda v: v.tensor_copy(B["decaybf"][:, :], B["decay"][:, :]),
                     waits=[ev_dec, st["decaybf_freeT"]])

    ev_t1 = P.add(T, lambda t: t.transpose(B["p_xT"][:, 0:64], B["dtbf"][:, :], B["idn"][0:64, 0:64]),
                  waits=[st["pxT_free"], ev_dtbf])
    P.add(T, lambda t: t.transpose(B["p_xT"][:, 64:128], B["decaybf"][:, :], B["idn"][0:64, 0:64]),
          waits=[ev_decbf])
    ev_t3 = P.add(T, lambda t: t.transpose(B["p_xT"][:, 128:256], B["B_fm"][:, sl], B["idn"][:, :]),
                  waits=list(ev_ready["BC"]) + [st["Btm_freeT"]])
    ev_dtT = P.add(V, lambda v: v.tensor_copy(B["dt_tm"][:, :], B["p_xT"][:, 0:64]), waits=[ev_t3])
    ev_decT = P.add(V, lambda v: v.tensor_copy(B["decay_tm"][:, :], B["p_xT"][:, 64:128]))
    ev_BT = P.add(V, lambda v: v.tensor_copy(B["B_tm"][:, :], B["p_xT"][:, 128:256]))

    ev_g = P.add(T, lambda t: t.matmul(B["p_small"][:, 0:128], B["B_fm"][:, sl], B["C_fm"][:, sl],
                                       start=True, stop=True),
                 waits=list(ev_ready["BC"]) + [st["psmall_free"]])
    ev_Gm = P.add(V, lambda v: v.tensor_tensor(B["Gm"][:, :], B["p_small"][:, 0:128], B["tril"][:, :], ALU.mult),
                  waits=[ev_g, st["ST_CE_init"]])

    ev_cdrow = P.dma(G, B["cd_row"][0:1, :], B["cd"][:, 0:1],
                     waits=[ev_cd, st.get("cdrow_free")], sem="cdr")
    ev_cdmm = P.add(T, lambda t: t.matmul(B["p_small"][:, 192:256], B["ones1_f32"][0:1, :], B["cd_row"][0:1, :],
                                          start=True, stop=True),
                    waits=[ev_cdrow, ev_Gm])
    ev_cdb = P.add(V, lambda v: v.tensor_copy(B["cdb"][:, :], B["p_small"][:, 192:256]),
                   waits=[ev_cdmm, st["cdb_freeT"]])

    prev_evac = [ev_dtT, ev_decT, ev_BT]
    ev_x1 = ev_x2 = None
    for r in range(4):
        ev_tr = None
        for k in range(8):
            pc = 8 * r + k
            w = list(ev_ready["x"]) + prev_evac if k == 0 else ()
            ev_tr = P.add(
                T, lambda t, pc=pc, k=k: t.transpose(
                    B["p_xT"][:, 128 * k:128 * k + 128], B["xy"][:, pc, sl], B["idn"][:, :]),
                waits=w)

        def xdt_fn(v, r=r):
            dt_ap = B["dt_tm"][:, 16 * r:16 * r + 16]
            dt_ap = dt_ap.rearrange("p (pc par) -> p pc par", par=2).unsqueeze(3).broadcast_to([128, 8, 2, 64])
            return v.tensor_tensor(B["xdt"][:, 1024 * r:1024 * r + 1024], B["p_xT"][:, 0:1024], dt_ap, ALU.mult)
        w = [ev_tr, ev_dtT] + ([st["xdt_freeT"]] if r == 0 else [])
        ev_x1 = P.add(V, xdt_fn, waits=w)

        def xdtd_fn(v, r=r):
            de_ap = B["decay_tm"][:, 16 * r:16 * r + 16]
            de_ap = de_ap.rearrange("p (pc par) -> p pc par", par=2).unsqueeze(3).broadcast_to([128, 8, 2, 64])
            return v.tensor_tensor(B["xdtd"][:, 1024 * r:1024 * r + 1024], B["xdt"][:, 1024 * r:1024 * r + 1024], de_ap, ALU.mult)
        w = [ev_decT] + ([st["xdtd_freeT"]] if r == 0 else [])
        ev_x2 = P.add(V, xdtd_fn, waits=w)
        prev_evac = [ev_x1]
    ev_xdt, ev_xdtd = ev_x1, ev_x2
    st["pxT_free"] = ev_x1

    ev_ST = [None] * 16
    ev_CE = [None] * 16
    ev_y_evacs = []
    last_pair_mm = None
    st["eacsb_free"] = [st["psh_free"], st["psmall_free"]]

    def emit_group(g):
        p = g % 2
        q4 = g % 4
        rs = slice(4 * g, 4 * g + 4)
        wst = [st["stage_free"][p], ev_negacs, ev_eacs] + (st["stage_ok"] if g < 2 else [])
        P.dma(G, B["stageL"][p][1:2, :], B["negAcs"][rs, :], waits=wst, sem=f"stg{p}")
        P.dma(G, B["stageR"][p][0:1, :], B["Acs"][rs, :], sem=f"stg{p}")
        e3 = P.dma(G, B["eacsrow"][p][0:1, :], B["eacs"][rs, :], sem=f"stg{p}")
        evd = None
        for j in range(4):
            w = [st["pdiff_free"][p], e3] if j == 0 else ()
            evd = P.add(
                T, lambda t, p=p, j=j: t.matmul(
                    B["p_diff"][p][:, 128 * j:128 * j + 128],
                    B["stageL"][p][0:2, 128 * j:128 * j + 128],
                    B["stageR"][p][0:2, 128 * j:128 * j + 128],
                    start=True, stop=True),
                waits=w)
        ebank = B["p_sh"] if (g % 2 == 0) else B["p_small"]
        wb = [st["eacsb_free"][g % 2]]
        if g == 1:
            wb.append(ev_cdb)
        ev_eb = P.add(T, lambda t, p=p, ebank=ebank: t.matmul(ebank[:, :], B["ones1_bf"][0:1, :], B["eacsrow"][p][0:1, :],
                                                 start=True, stop=True),
                      waits=wb)
        ev_min = P.add(A, lambda a, p=p: a.activation(B["dmin"][p][:, :], B["p_diff"][p][:, :], AF.Relu, scale=-1.0),
                       waits=[st["dmin_free"][p], evd])
        st["pdiff_free"][p] = ev_min
        st["stage_free"][p] = ev_eb
        ev_E = P.add(A, lambda a, p=p: a.activation(B["dmin"][p][:, :], B["dmin"][p][:, :], AF.Exp, scale=-1.0))

        def st_fn(v, p=p, q4=q4):
            gm_ap = B["Gm"][:, :].unsqueeze(1).broadcast_to([128, 4, 128])
            return v.tensor_tensor(B["ST"][q4][:, :], B["dmin"][p][:, :], gm_ap, ALU.mult)
        ev_st_ = P.add(V, st_fn, waits=[st["ST_free"][q4], ev_E, ev_Gm])
        st["dmin_free"][p] = ev_st_
        ev_ST[g] = ev_st_

        def ce_fn(v, q4=q4, sl=sl, ebank=ebank):
            c_ap = B["C_fm"][:, sl].unsqueeze(1).broadcast_to([128, 4, 128])
            return v.tensor_tensor(B["CE"][q4][:, :], c_ap, ebank[:, :], ALU.mult)
        ev_ce = P.add(V, ce_fn, waits=[st["CE_free"][q4], ev_eb])
        st["eacsb_free"][g % 2] = ev_ce
        ev_CE[g] = ev_ce

    def emit_yfill(q):
        nonlocal last_pair_mm
        b = q % 2
        last = None
        for m in range(4):
            i = 4 * q + m
            h0, h1 = 2 * i, 2 * i + 1
            g0, j0 = h0 // 4, h0 % 4
            g1, j1 = h1 // 4, h1 % 4
            col = slice(128 * m, 128 * m + 128)
            w0 = [ev_ST[g0], ev_ST[g1], ev_CE[g0], ev_CE[g1], ev_xdt]
            if m == 0:
                w0 = w0 + [st["pY_free"][b]] + list(st["states_bf_ready"])
            P.add(T, lambda t, b=b, col=col, h0=h0, g0=g0, j0=j0: t.matmul(
                B["p_Y"][b][0:64, col], B["xdt"][:, 64 * h0:64 * h0 + 64],
                B["ST"][g0 % 4][:, 128 * j0:128 * j0 + 128],
                start=True, stop=False, tile_position=(0, 0)),
                waits=w0)
            P.add(T, lambda t, b=b, col=col, h0=h0, g0=g0, j0=j0: t.matmul(
                B["p_Y"][b][0:64, col], B["STATES_bf"][:, 64 * h0:64 * h0 + 64],
                B["CE"][g0 % 4][:, 128 * j0:128 * j0 + 128],
                start=False, stop=False, tile_position=(0, 0)))
            P.add(T, lambda t, b=b, col=col, i=i, sl=sl: t.matmul(
                B["p_Y"][b][0:64, col], B["Ddiag"][:, i, 0:64], B["xy"][:, i, sl],
                start=False, stop=True, tile_position=(0, 0)))
            P.add(T, lambda t, b=b, col=col, h1=h1, g1=g1, j1=j1: t.matmul(
                B["p_Y"][b][64:128, col], B["xdt"][:, 64 * h1:64 * h1 + 64],
                B["ST"][g1 % 4][:, 128 * j1:128 * j1 + 128],
                start=True, stop=False, tile_position=(0, 64)))
            P.add(T, lambda t, b=b, col=col, h1=h1, g1=g1, j1=j1: t.matmul(
                B["p_Y"][b][64:128, col], B["STATES_bf"][:, 64 * h1:64 * h1 + 64],
                B["CE"][g1 % 4][:, 128 * j1:128 * j1 + 128],
                start=False, stop=False, tile_position=(0, 64)))
            last = P.add(T, lambda t, b=b, col=col, i=i, sl=sl: t.matmul(
                B["p_Y"][b][64:128, col], B["Ddiag"][:, i, 64:128], B["xy"][:, i, sl],
                start=False, stop=True, tile_position=(0, 64)))
        last_pair_mm = last

        def yevac_fn(a, b=b, q=q, sl=sl):
            return a.copy(B["xy"][:, 4 * q:4 * q + 4, sl], B["p_Y"][b][:, :])
        ev_ye = P.add(A, yevac_fn, waits=[last])
        st["pY_free"][b] = ev_ye
        ev_y_evacs.append(ev_ye)
        for g in (2 * q, 2 * q + 1):
            st["ST_free"][g % 4] = last
            st["CE_free"][g % 4] = last

    for q in range(9):
        if q < 8:
            emit_group(2 * q)
            emit_group(2 * q + 1)
        if q >= 1:
            emit_yfill(q - 1)

    ev_states = []
    sbf_ready = []
    psh_free = st["eacsb_free"][0]
    last_smm = None
    for j in range(8):
        colj = slice(512 * j, 512 * j + 512)
        wmm = [ev_xdtd, ev_BT, psh_free] + ([last_pair_mm] if j == 0 else [])
        ev_smm = P.add(T, lambda t, colj=colj: t.matmul(B["p_sh"][:, :], B["B_tm"][:, :], B["xdtd"][:, colj],
                                                        start=True, stop=True),
                       waits=wmm)
        last_smm = ev_smm

        def tmul_fn(v, j=j, colj=colj):
            cdb_ap = B["cdb"][:, 8 * j:8 * j + 8].unsqueeze(2).broadcast_to([128, 8, 64])
            return v.tensor_tensor(B["tscr"][:, :], B["STATES_bf"][:, colj], cdb_ap, ALU.mult)
        ev_tm = P.add(V, tmul_fn, waits=[ev_cdb, last_pair_mm])
        ev_up = P.add(V, lambda v, colj=colj: v.tensor_tensor(B["STATES_bf"][:, colj], B["tscr"][:, :], B["p_sh"][:, :], ALU.add),
                      waits=[ev_smm, ev_tm])
        sbf_ready.append(ev_up)
        psh_free = ev_up
        ev_states.append(ev_up)

    st["states_bf_ready"] = sbf_ready
    st["psh_free"] = psh_free
    st["psmall_free"] = st["eacsb_free"][1]
    st["cdrow_free"] = ev_cdmm
    st["xdt_freeT"] = last_pair_mm
    st["xdtd_freeT"] = last_smm
    st["dtbf_freeT"] = ev_t1
    st["decaybf_freeT"] = ev_t1
    st["Btm_freeT"] = last_smm
    st["cdb_freeT"] = ev_states[-1]
    return {"y": ev_y_evacs, "states": ev_states}



def build_program():
    nc = bass.Bass("TRN2", target_bir_lowering=False)

    d_u = nc.dram_tensor("u_fm", [DM, 2048], F32, kind="ExternalInput").ap()
    d_win = nc.dram_tensor("winT_t", [N_ECH, 128, 16, 128], F32, kind="ExternalInput").ap()
    d_wout = nc.dram_tensor("woutT_t", [16, 128, 32, 128], F32, kind="ExternalInput").ap()
    d_w1 = nc.dram_tensor("w1T_t", [N_HCH, 128, 16, 128], F32, kind="ExternalInput").ap()
    d_w2 = nc.dram_tensor("w2T_t", [N_HCH, 128, 16, 128], F32, kind="ExternalInput").ap()
    d_w3 = nc.dram_tensor("w3T_t", [N_HCH, 128, 2048], F32, kind="ExternalInput").ap()
    d_cw = nc.dram_tensor("cw_pp", [128, 34, 4], F32, kind="ExternalInput").ap()
    d_cb = nc.dram_tensor("cb_pp", [128, 34], F32, kind="ExternalInput").ap()
    d_nw = nc.dram_tensor("nw_pp", [128, 32], F32, kind="ExternalInput").ap()
    d_dtb = nc.dram_tensor("dtb", [64, 1], F32, kind="ExternalInput").ap()
    d_alog = nc.dram_tensor("alog", [64, 1], F32, kind="ExternalInput").ap()
    d_dd = nc.dram_tensor("dd32", [128, 32, 128], F32, kind="ExternalInput").ap()
    d_idn = nc.dram_tensor("idn", [128, 128], BF16, kind="ExternalInput").ap()
    d_tril = nc.dram_tensor("trilT", [128, 128], BF16, kind="ExternalInput").ap()
    d_o1f = nc.dram_tensor("o1f", [1, 128], F32, kind="ExternalInput").ap()
    d_o1b = nc.dram_tensor("o1b", [1, 128], BF16, kind="ExternalInput").ap()
    d_o128b = nc.dram_tensor("o128b", [128, 128], BF16, kind="ExternalInput").ap()
    d_o128f = nc.dram_tensor("o128f", [128, 128], F32, kind="ExternalInput").ap()
    d_o128f1 = nc.dram_tensor("o128f1", [128, 1], F32, kind="ExternalInput").ap()
    d_out = nc.dram_tensor("out_fm", [DM, 2048], F32, kind="ExternalOutput").ap()
    d_q1s = nc.dram_tensor("q1s", [N_HCH, 128, 2048], BF16, kind="Internal").ap()
    d_q2s = nc.dram_tensor("q2s", [N_HCH, 128, 2048], BF16, kind="Internal").ap()
    d_q3s = nc.dram_tensor("q3s", [N_HCH, 128, 2048], BF16, kind="Internal").ap()

    TOTAL = 212800
    BASE = 16512
    AR_A_SZ = 65536
    AR_H_SZ = 32768
    AR_X_SZ = 32768
    AR_W_SZ = 53696
    top = BASE + TOTAL
    a_base = top - AR_A_SZ
    h_base = a_base - AR_H_SZ
    x_base = h_base - AR_X_SZ
    w_base = x_base - AR_W_SZ

    Al = lambda n, s, d=F32: nc.alloc_sbuf_tensor(n, s, d)
    B = {}
    B["STATES_bf"] = Al("STATES_bf", [128, 4096], BF16)
    B["Ddiag"] = Al("Ddiag", [128, 32, 128], BF16)
    B["idn"] = Al("c_idn", [128, 128], BF16)
    B["tril"] = Al("c_tril", [128, 128], BF16)
    B["ones1_f32"] = Al("c_o1f", [1, 128])
    B["ones1_bf"] = Al("c_o1b", [1, 128], BF16)
    B["o128b"] = Al("c_o128b", [128, 128], BF16)
    B["o128f"] = Al("c_o128f", [128, 128])
    B["o128f1"] = Al("c_o128f1", [128, 1])
    B["nw"] = Al("c_nw", [128, 32])
    B["cw"] = Al("c_cw", [128, 34, 4])
    B["cb"] = Al("c_cb", [128, 34])
    B["dtbias"] = Al("c_dtb", [64, 1])
    B["A_col"] = Al("A_col", [64, 1])
    B["stash"] = Al("stash", [128, 34, 3])
    B["tpp"] = Al("tpp", [128, 4])
    B["tneg"] = Al("tneg", [128, 4])
    B["B_fm"] = Al("B_fm", [128, SEGL], BF16)
    B["C_fm"] = Al("C_fm", [128, SEGL], BF16)
    B["dt_hm"] = Al("dt_hm", [64, SEGL])
    used = TOTAL - nc.sbuf_bytes_remaining
    assert BASE + used <= w_base, f"auto overflow: used={used}, limit={w_base - BASE}"

    arA = Overlay(nc, a_base, AR_A_SZ, "aA")
    arH = Overlay(nc, h_base, AR_H_SZ, "aH")
    arX = Overlay(nc, x_base, AR_X_SZ, "aX")
    arW = Overlay(nc, w_base, AR_W_SZ, "aW")

    B["Ddiag32"] = arA.alloc("dd32", [128, 32, 128], F32)
    arA.reset()
    B["u32"] = arA.alloc("u32", [128, 16, SEGL], F32)
    arA.reset()
    B["xy"] = arA.alloc("xy", [128, 32, SEGL], BF16)
    arA.reset()
    B["hid"] = arA.alloc("hid", [128, FFN_R, SEGL], BF16)
    B["q3l"] = arA.alloc("q3l", [128, FFN_R, 2048], BF16)

    B["h1"] = arH.alloc("h", [128, 16, SEGL], BF16)

    B["X1"] = arX.alloc("X1", [128, 16, SEGL], BF16)
    arX.reset()
    B["gslab"] = [arX.alloc(f"gslab{i}", [128, 2048], F32) for i in range(2)]
    B["gscr"] = arX.alloc("gscr", [128, 264])
    B["g3"] = arX.alloc("g3", [128, 4])
    B["grow"] = arX.alloc("grow", [1, 4])
    B["wslab3"] = arX.alloc("wslab3", [128, 2048], BF16)
    B["sqbX"] = arX.alloc("sqbX", [128, SEGL], BF16)

    B["dA"] = arW.alloc("dA", [64, 128])
    B["Acs"] = arW.alloc("Acs", [64, 128])
    B["negAcs"] = arW.alloc("negAcs", [64, 128])
    B["tdec"] = arW.alloc("tdec", [64, 128])
    B["decay"] = arW.alloc("decay", [64, 128])
    B["eacs"] = arW.alloc("eacs", [64, 128])
    B["cd"] = arW.alloc("cd", [64, 1])
    B["dtbf"] = arW.alloc("dtbf", [64, 128], BF16)
    B["decaybf"] = arW.alloc("decaybf", [64, 128], BF16)
    B["z64"] = arW.alloc("z64", [64, 128])
    B["ones512"] = arW.alloc("ones512", [1, 512])
    B["dt_tm"] = arW.alloc("dt_tm", [128, 64], BF16)
    B["decay_tm"] = arW.alloc("decay_tm", [128, 64], BF16)
    B["B_tm"] = arW.alloc("B_tm", [128, 128], BF16)
    B["Gm"] = arW.alloc("Gm", [128, 128], BF16)
    B["cdb"] = arW.alloc("cdb", [128, 64])
    B["cd_row"] = arW.alloc("cd_row", [1, 64])
    B["xdt"] = arW.alloc("xdt", [128, 4096], BF16)
    B["xdtd"] = arW.alloc("xdtd", [128, 4096], BF16)
    B["tscr"] = arW.alloc("tscr", [128, 512])
    B["stageL"] = [arW.alloc(f"stageL{i}", [2, 512]) for i in range(2)]
    B["stageR"] = [arW.alloc(f"stageR{i}", [2, 512]) for i in range(2)]
    B["eacsrow"] = [arW.alloc(f"eacsrow{i}", [1, 512], BF16) for i in range(2)]
    B["dmin"] = [arW.alloc(f"dmin{i}", [128, 512]) for i in range(2)]
    B["ST"] = [arW.alloc(f"ST{i}", [128, 512], BF16) for i in range(4)]
    B["CE"] = [arW.alloc(f"CE{i}", [128, 512], BF16) for i in range(4)]
    assert arW.off <= AR_W_SZ

    arW.reset()
    B["wslab@m1"] = [arW.alloc(f"wslab{i}", [128, 2048], BF16) for i in range(2)]
    B["dtraw"] = arW.alloc("dtraw", [64, SEGL])
    B["dts1"] = arW.alloc("dts1", [64, SEGL])
    B["dts2"] = B["dtraw"]
    B["convtmp"] = arW.alloc("convtmp", [128, SEGL])
    B["xbcraw"] = [arW.alloc(f"xbcraw{i}", [128, 1028]) for i in range(2)]
    B["convo@m0"] = [arW.alloc(f"convo{i}", [128, SEGL]) for i in range(2)]
    B["sqb@m0"] = arW.alloc("sqb0", [128, SEGL], BF16)
    B["sz@m0"] = arW.alloc("sz0", [128, SEGL])
    B["stat1@m0"] = arW.alloc("stat10", [128, SEGL])
    B["stat2@m0"] = arW.alloc("stat20", [128, SEGL])
    assert arW.off <= AR_W_SZ, f"W2 {arW.off}"

    arW.reset()
    B["wslab@m3"] = [arW.alloc(f"wslab3_{i}", [128, 2048], BF16) for i in range(2)]
    B["sz@m3"] = arW.alloc("sz3", [128, SEGL])
    assert arW.off <= AR_W_SZ - 6272
    arW.off = AR_W_SZ - 6272
    B["sqb@m3"] = arW.alloc("sqb3", [128, SEGL], BF16)
    B["stat2@m3"] = arW.alloc("stat23", [128, SEGL])

    arW.reset()
    B["woslab"] = [arW.alloc(f"woslab{i}", [128, 4096], BF16) for i in range(2)]
    B["uscr"] = [arW.alloc(f"uscr{i}", [128, 1024]) for i in range(2)]
    B["tmpX"] = arW.alloc("tmpX", [128, 1024])
    assert arW.off <= AR_W_SZ - 4160

    arW.reset()
    B["sqb@m5"] = arW.alloc("sqb5", [128, SEGL], BF16)
    B["sz@m5"] = arW.alloc("sz5", [128, SEGL])
    B["stat1@m5"] = arW.alloc("stat15", [128, SEGL])
    B["stat2@m5"] = arW.alloc("stat25", [128, SEGL])
    B["convo@m5"] = [arW.alloc(f"convo5_{i}", [128, SEGL]) for i in range(2)]
    assert arW.off <= AR_W_SZ

    arW.reset()
    _w1 = arW.alloc("w1slab", [128, 2048], F32)
    B["w1slab"] = [_w1, _w1]
    _w2 = arW.alloc("w2slab", [128, 2048], F32)
    B["w2slab"] = [_w2, _w2]
    B["q1"] = [arW.alloc(f"q1_{i}", [128, 2048], BF16) for i in range(2)]
    B["q2"] = [arW.alloc(f"q2_{i}", [128, 2048], BF16) for i in range(2)]
    B["qtmp"] = arW.alloc("qtmp", [128, 2048], BF16)
    B["sig"] = arW.alloc("sig", [128, SEGL])
    B["outstg"] = [arW.alloc(f"outstg{i}", [128, 512]) for i in range(2)]
    B["w3slab"] = B["w2slab"]
    assert arW.off <= AR_W_SZ - 6272, f"W6 {arW.off}"

    B["p_xT"] = nc.alloc_psum_tensor("p_xT", [128, 1024], BF16)
    p_ip = [nc.alloc_psum_tensor(f"p_ip{i}", [128, 1024], F32) for i in range(3)]
    B["p_ip"] = p_ip
    B["p_Y"] = [p_ip[0][:, 0:512], p_ip[0][:, 512:1024]]
    B["p_diff"] = [p_ip[1][:, 0:512], p_ip[1][:, 512:1024]]
    B["p_sh"] = p_ip[2][:, 0:512]
    B["p_small"] = p_ip[2][:, 512:1024]
    B["p_w3"] = nc.alloc_psum_tensor("p_w3", [128, 512], F32)

    P = Pipe(nc)
    for s in (sV, sA, "sT", "sGc", "stg0", "stg1", "cdr"):
        P.new_sem(s)

    ld = []
    for dst, src in ((B["idn"], d_idn), (B["tril"], d_tril), (B["ones1_f32"], d_o1f),
                     (B["ones1_bf"], d_o1b), (B["o128b"], d_o128b), (B["o128f"], d_o128f),
                     (B["o128f1"], d_o128f1),
                     (B["nw"], d_nw), (B["cw"], d_cw), (B["cb"], d_cb),
                     (B["dtbias"], d_dtb), (B["A_col"], d_alog), (B["Ddiag32"], d_dd)):
        ld.append(P.dma(S, dst[:], src, sem="dini"))
    ev_ini = [ld[-1]]
    e = P.add(A, lambda a: a.activation(B["A_col"][:, 0:1], B["A_col"][:, 0:1], AF.Exp),
              waits=ev_ini)
    P.add(V, lambda v: v.tensor_scalar_mul(B["A_col"][:, 0:1], B["A_col"][:, 0:1], -1.0), waits=[e])
    ev_stash0 = P.add(V, lambda v: v.memset(B["stash"][:, :, :], 0.0))

    sst = ssd_init(P, B, ev_consts=ev_ini)

    pfree = {"p_Y0": None, "p_Y1": None, "p_diff0": None, "p_diff1": None,
             "p_sh": None, "p_small": None, "p_w3": None}

    def pf_pair(names):
        return [pfree[n] for n in names]

    ev_gamma_done = [None]
    qcache_done = {}
    wfree3 = [None, None, None]
    out_dma_evs = []

    gamma_state = {"col": 0, "gfree": [None, None], "last": None}

    def emit_gamma_batch(n):
        srcs = []
        for dw in (d_w1, d_w2, d_w3):
            for hc in range(N_HCH):
                srcs.append((dw, hc))
        while n > 0 and gamma_state["col"] < len(srcs):
            col = gamma_state["col"]
            dw, hc = srcs[col]
            par = col % 2
            if dw is d_w3:
                s_ = dw[hc, :, :]
            else:
                s_ = dw[hc, :, :, :].rearrange("p a b -> p (a b)")
            gf = gamma_state["gfree"][par]
            evd = P.dma(S, B["gslab"][par][:, :], s_,
                        waits=list(gf) if isinstance(gf, tuple) else [gf],
                        sem=f"dg{par}")
            ev_ra = P.add(A, lambda a, par=par, col=col: a.activation(
                B["gslab"][par][:, 0:1024], B["gslab"][par][:, 0:1024], AF.Abs,
                accum_out=B["gscr"][:, 2 * col:2 * col + 1]), waits=[evd])
            ev_rb = P.add(A, lambda a, par=par, col=col: a.activation(
                B["gslab"][par][:, 1024:2048], B["gslab"][par][:, 1024:2048], AF.Abs,
                accum_out=B["gscr"][:, 2 * col + 1:2 * col + 2]))
            gamma_state["gfree"][par] = (ev_ra, ev_rb)
            gamma_state["last"] = (ev_ra, ev_rb)
            gamma_state["col"] = col + 1
            n -= 1

    def emit_gamma_fold():
        last_red = list(gamma_state["last"])
        assert gamma_state["col"] == 3 * N_HCH
        evs = []
        for idx in range(3):
            evs.append(P.add(V, lambda v, idx=idx: v.reduce_sum(
                B["g3"][:, idx:idx + 1], B["gscr"][:, 88 * idx:88 * idx + 88], AX.X),
                waits=last_red))
        ev_z = P.add(V, lambda v: v.memset(B["g3"][:, 3:4], 0.0))
        ev_gmm = P.add(T, lambda t: t.matmul(B["p_w3"][0:1, 0:4], B["o128f1"][:, 0:1], B["g3"][:, 0:4],
                                             start=True, stop=True),
                       waits=[ev_z] + evs + [pfree["p_w3"]])
        ev_gr = P.add(V, lambda v: v.tensor_copy(B["grow"][0:1, 0:4], B["p_w3"][0:1, 0:4]),
                      waits=[ev_gmm])
        ev_gb = P.add(T, lambda t: t.matmul(B["p_w3"][:, 4:8], B["ones1_f32"][0:1, :], B["grow"][0:1, 0:4],
                                            start=True, stop=True), waits=[ev_gr])
        ev_tpp = P.add(V, lambda v: v.tensor_scalar(B["tpp"][:, 0:4], B["p_w3"][:, 4:8],
                                                    0.5 / NW1, 0.5 * GAMMA_EPS, ALU.mult, ALU.add),
                       waits=[ev_gb])
        ev_tn = P.add(V, lambda v: v.tensor_scalar_mul(B["tneg"][:, 0:4], B["tpp"][:, 0:4], -1.0))
        pfree["p_w3"] = ev_tpp
        ev_gamma_done[0] = ev_tn

    for seg in range(NSEG):
        l0 = seg * SEGL
        wu = (P.barrier() + list(qcache_done.values()) + out_dma_evs[-2:]) if seg else [sst["ST_CE_init"]]
        evu = []
        for half in range(2):
            src = d_u[1024 * half:1024 * (half + 1), l0:l0 + SEGL]
            src = src.rearrange("(k p) l -> p k l", p=128)
            evu.append(P.dma(S, B["u32"][:, 8 * half:8 * (half + 1), :], src,
                             waits=wu, sem="du"))
        ev_smm = ev_umm = None
        smm_by_k = []
        for k in range(16):
            sqbuf = B["sqb@m0"] if k % 2 == 0 else B["sqbX"]
            evq = P.add(A, lambda a, k=k, sqbuf=sqbuf: a.activation(sqbuf[:, :], B["u32"][:, k, :], AF.Square),
                        waits=evu + (wu if k < 2 else []) + ([smm_by_k[k - 2]] if k >= 2 else []))
            for lh in range(2):
                ev_smm = P.add(T, lambda t, k=k, lh=lh, sqbuf=sqbuf: t.matmul(
                    B["p_diff"][lh][:, :], B["o128b"][:, :], sqbuf[:, 512 * lh:512 * lh + 512],
                    start=(k == 0), stop=(k == 15)),
                    waits=[evq] + ([pfree[f"p_diff{lh}"]] if k == 0 else []))
            smm_by_k.append(ev_smm)
        for k in range(16):
            for lh in range(2):
                ev_umm = P.add(T, lambda t, k=k, lh=lh: t.matmul(
                    B["p_Y"][lh][:, :], B["o128f"][:, :], B["u32"][:, k, 512 * lh:512 * lh + 512],
                    start=(k == 0), stop=(k == 15)),
                    waits=(evu + [pfree["p_Y0"], pfree["p_Y1"]]) if k == 0 else ())
        P.add(A, lambda a: a.mul(B["stat1@m0"][:, :], B["p_ip"][0][:, :], 1.0 / DM),
              waits=[ev_umm, ev_smm] + (wu if seg else []))
        ev_v2 = P.add(A, lambda a: a.mul(B["stat2@m0"][:, :], B["p_ip"][1][:, :], 1.0 / DM))
        pfree["p_Y0"] = pfree["p_Y1"] = pfree["p_diff0"] = pfree["p_diff1"] = ev_v2
        evm = P.add(A, lambda a: a.activation(B["sz@m0"][:, :], B["stat1@m0"][:, :], AF.Square),
                    waits=[ev_v2])
        evv = P.add(V, lambda v: v.scalar_tensor_tensor(B["stat2@m0"][:, :], B["stat2@m0"][:, :], float(EPS_LN), B["sz@m0"][:, :], ALU.add, ALU.subtract),
                    waits=[evm])
        evl = P.add(A, lambda a: a.activation(B["stat2@m0"][:, :], B["stat2@m0"][:, :], AF.Ln),
                    waits=[evv])
        ev_rstd = P.add(A, lambda a: a.activation(B["stat2@m0"][:, :], B["stat2@m0"][:, :], AF.Exp, scale=-0.5))
        ev_h1k = []
        for k in range(16):
            evs_ = P.add(V, lambda v, k=k: v.tensor_tensor(B["convo@m0"][k % 2][:, :], B["u32"][:, k, :], B["stat1@m0"][:, :], ALU.subtract),
                         waits=[ev_rstd])
            ev_h1k.append(P.add(V, lambda v, k=k: v.tensor_tensor(B["h1"][:, k, :], B["convo@m0"][k % 2][:, :], B["stat2@m0"][:, :], ALU.mult)))
        ev_h1done = [ev_h1k[-1]]


        wfree = wfree3 if seg else [None, None, None]
        slabs3 = [B["wslab@m1"][0], B["wslab@m1"][1], B["wslab3"]]
        psum3 = [(B["p_Y"][0], B["p_Y"][1]), (B["p_diff"][0], B["p_diff"][1]),
                 (B["p_sh"], B["p_small"])]
        slot_free = {0: [pfree["p_Y0"], pfree["p_Y1"]],
                     1: [pfree["p_diff0"], pfree["p_diff1"]],
                     2: [pfree["p_sh"], pfree["p_small"]]}
        xbf = [None, None]
        cvf = [None, None]
        ev_x_ready = []
        ev_dt_ready = []
        for ci, ec in enumerate(range(32, 67)):
            par = ci % 3
            p3 = ci % 3
            if seg == 0:
                emit_gamma_batch(2)
            src = d_win[ec, :, :, :].rearrange("p a b -> p (a b)")
            evd = P.dma(G, slabs3[par][:, :], src, waits=[wfree[par]] + (P.barrier() if ci < 3 and seg else []),
                        sem=f"dwin{par}")
            pb = psum3[p3]
            ev_mm = None
            for lh in range(2):
                for k in range(16):
                    w = [ev_h1k[k]]
                    if lh == 0 and k == 0:
                        w = w + [evd] + slot_free[p3]
                    ev_mm = P.add(T, lambda t, sl3=slabs3[par], lh=lh, k=k, pb=pb: t.matmul(
                        pb[lh][:, :], sl3[:, 128 * k:128 * k + 128],
                        B["h1"][:, k, 512 * lh:512 * lh + 512],
                        start=(k == 0), stop=(k == 15)), waits=w)
            wfree[par] = ev_mm
            if ec < 66:
                m = ec - 32
                par = ci % 2
                pfull = p_ip[p3]
                co = B["convo@m0"][par]
                e1_ = P.add(V, lambda v, co=co, m=m: v.tensor_scalar(
                    co[:, 0:3], B["stash"][:, m, 0:3], B["cw"][:, m, 0:1], None, ALU.mult),
                    waits=[ev_mm, ev_stash0, cvf[par]])
                e2_ = P.add(V, lambda v, co=co, m=m, pfull=pfull: v.tensor_scalar(
                    co[:, 3:1024], pfull[:, 0:1021], B["cw"][:, m, 0:1], None, ALU.mult))
                P.add(V, lambda v, co=co, m=m: v.scalar_tensor_tensor(
                    co[:, 0:2], B["stash"][:, m, 1:3], B["cw"][:, m, 1:2], co[:, 0:2],
                    ALU.mult, ALU.add))
                P.add(V, lambda v, co=co, m=m, pfull=pfull: v.scalar_tensor_tensor(
                    co[:, 2:1024], pfull[:, 0:1022], B["cw"][:, m, 1:2], co[:, 2:1024],
                    ALU.mult, ALU.add))
                P.add(V, lambda v, co=co, m=m: v.scalar_tensor_tensor(
                    co[:, 0:1], B["stash"][:, m, 2:3], B["cw"][:, m, 2:3], co[:, 0:1],
                    ALU.mult, ALU.add))
                P.add(V, lambda v, co=co, m=m, pfull=pfull: v.scalar_tensor_tensor(
                    co[:, 1:1024], pfull[:, 0:1023], B["cw"][:, m, 2:3], co[:, 1:1024],
                    ALU.mult, ALU.add))
                evc = P.add(V, lambda v, co=co, m=m, pfull=pfull: v.scalar_tensor_tensor(
                    co[:, 0:1024], pfull[:, 0:1024], B["cw"][:, m, 3:4], co[:, 0:1024],
                    ALU.mult, ALU.add))
                ev_sth = P.add(V, lambda v, m=m, pfull=pfull: v.tensor_copy(
                    B["stash"][:, m, :], pfull[:, 1021:1024]))
                slot_free[p3] = [evc, ev_sth]
                if m < 32:
                    dst = B["xy"][:, m, :]
                elif m == 32:
                    dst = B["B_fm"][:, :]
                else:
                    dst = B["C_fm"][:, :]
                evx = P.add(A, lambda a, dst=dst, m=m, co=co: a.activation(
                    dst, co[:, :], AF.Silu, bias=B["cb"][:, m:m + 1]),
                    waits=[evc])
                cvf[par] = evx
                ev_x_ready = [evx]
            else:
                eva = P.add(A, lambda a, pb=pb: a.copy(B["dtraw"][0:64, 0:512], pb[0][0:64, :]),
                            waits=[ev_mm])
                eva2 = P.add(A, lambda a, pb=pb: a.copy(B["dtraw"][0:64, 512:1024], pb[1][0:64, :]))
                slot_free[p3] = [eva2]
                ev1 = P.add(V, lambda v: v.tensor_scalar(B["dts1"][0:64, :], B["dtraw"][0:64, :],
                                                         B["dtbias"][:, 0:1], None, ALU.add),
                            waits=[eva2])
                P.add(A, lambda a: a.activation(B["dts2"][0:64, :], B["dts1"][0:64, :], AF.Abs),
                      waits=[ev1])
                P.add(A, lambda a: a.activation(B["dts2"][0:64, :], B["dts2"][0:64, :], AF.Exp, scale=-1.0))
                ev4 = P.add(A, lambda a: a.activation(B["dts2"][0:64, :], B["dts2"][0:64, :], AF.Ln, bias=1.0))
                P.add(V, lambda v: v.tensor_scalar_max(B["dts1"][0:64, :], B["dts1"][0:64, :], 0.0),
                      waits=[ev4])
                ev6 = P.add(V, lambda v: v.tensor_tensor(B["dt_hm"][0:64, :], B["dts1"][0:64, :], B["dts2"][0:64, :], ALU.add))
                ev_dt_ready = [ev6]

        m1bar = P.barrier()
        P.add(V, lambda v: v.memset(B["z64"][:, :], 0.0), waits=m1bar)
        sevs = []
        for i2 in range(2):
            sevs.append(P.add(V, lambda v, i2=i2: v.memset(B["stageL"][i2][0:1, :], 1.0)))
        ev_o2 = P.add(V, lambda v: v.memset(B["ones512"][0:1, :], 1.0))
        for i2 in range(2):
            sevs.append(P.dma(G, B["stageR"][i2][1:2, :], B["ones512"][0:1, :],
                              waits=[ev_o2, sst["stage_free"][i2]], sem=f"stg{i2}"))
        sst["stage_ok"] = sevs + [P.last(V)]
        sst["stage_free"] = [sevs[-2], sevs[-1]]
        sst["pY_free"] = [slot_free[0][0], slot_free[0][0]]
        sst["pdiff_free"] = [slot_free[1][0], slot_free[1][0]]
        sst["psh_free"] = slot_free[2][0]
        sst["psmall_free"] = slot_free[2][0]
        r = None
        for c in range(8):
            if seg == 0:
                emit_gamma_batch(2)
            r = emit_ssd_chunk(P, B, c, {"x": ev_x_ready, "BC": ev_x_ready,
                                         "dt": ev_dt_ready}, sst)
        ev_ssd_done = r["y"] + r["states"]

        m2bar = P.barrier()
        slot_free = {0: [sst["pY_free"][0]], 1: [sst["pdiff_free"][0]]}
        ev_rms_mm = None
        ev_yg_prev = None
        ev_rms_prev = None
        ev_ygs = []
        ev_sqs = []
        slabs3b = [B["wslab@m3"][0], B["wslab@m3"][1], B["wslab3"]]
        for ec in range(32):
            par = ec % 3
            par2 = ec % 2
            if seg == 0:
                emit_gamma_batch(2)
            src = d_win[ec, :, :, :].rearrange("p a b -> p (a b)")
            evd = P.dma(G, slabs3b[par][:, :], src,
                        waits=[wfree[par]] + (m2bar if ec < 3 else []), sem=f"dwin{par}")
            pb = B["p_Y"] if par2 == 0 else B["p_diff"]
            ev_mm = None
            for lh in range(2):
                for k in range(16):
                    w = []
                    if lh == 0 and k == 0:
                        w = [evd] + slot_free[par2] + (ev_ssd_done if ec < 2 else [])
                    ev_mm = P.add(T, lambda t, sl3=slabs3b[par], lh=lh, k=k, pb=pb: t.matmul(
                        pb[lh][:, :], sl3[:, 128 * k:128 * k + 128],
                        B["h1"][:, k, 512 * lh:512 * lh + 512],
                        start=(k == 0), stop=(k == 15)), waits=w)
            wfree[par] = ev_mm
            pfull = B["p_ip"][par2]
            evs2 = P.add(A, lambda a, pfull=pfull: a.activation(B["sz@m3"][:, :], pfull[:, :], AF.Silu),
                         waits=[ev_mm, ev_yg_prev])
            slot_free[par2] = [evs2]
            evyg = P.add(V, lambda v, ec=ec: v.tensor_tensor(B["xy"][:, ec, :], B["xy"][:, ec, :], B["sz@m3"][:, :], ALU.mult),
                         waits=[evs2] + (ev_ssd_done if ec < 2 else []))
            ev_yg_prev = evyg
            ev_ygs.append(evyg)
            if ec >= 2:
                ecl = ec - 2
                sqb_l = B["sqb@m3"] if ecl % 2 == 0 else B["sqb@m0"]
                for lh in range(2):
                    ev_rms_mm = P.add(T, lambda t, ecl=ecl, lh=lh, sqb_l=sqb_l: t.matmul(
                        (B["p_sh"] if lh == 0 else B["p_small"])[:, :], B["o128b"][:, :],
                        sqb_l[:, 512 * lh:512 * lh + 512],
                        start=(ecl == 0), stop=False),
                        waits=[ev_sqs[ecl]] + ([sst["psh_free"], sst["psmall_free"]] if ecl == 0 else []))
                ev_rms_prev = ev_rms_mm
            sqbuf = B["sqb@m3"] if ec % 2 == 0 else B["sqb@m0"]
            ev_sqs.append(P.add(A, lambda a, ec=ec, sqbuf=sqbuf: a.activation(
                sqbuf[:, :], B["xy"][:, ec, :], AF.Square),
                waits=[evyg, ev_rms_prev]))
        m3bar = P.barrier()
        for ecl in (30, 31):
            sqb_l = B["sqb@m3"] if ecl % 2 == 0 else B["sqb@m0"]
            for lh in range(2):
                ev_rms_mm = P.add(T, lambda t, ecl=ecl, lh=lh, sqb_l=sqb_l: t.matmul(
                    (B["p_sh"] if lh == 0 else B["p_small"])[:, :], B["o128b"][:, :],
                    sqb_l[:, 512 * lh:512 * lh + 512],
                    start=False, stop=(ecl == 31)),
                    waits=[ev_sqs[ecl]])
        ev1 = P.add(V, lambda v: v.tensor_scalar(B["stat2@m3"][:, 0:512], B["p_sh"][:, :], 1.0 / DI, float(EPS_RMS), ALU.mult, ALU.add),
                    waits=[ev_rms_mm])
        ev2 = P.add(V, lambda v: v.tensor_scalar(B["stat2@m3"][:, 512:1024], B["p_small"][:, :], 1.0 / DI, float(EPS_RMS), ALU.mult, ALU.add))
        pfree["p_sh"] = ev2
        pfree["p_small"] = ev2
        P.add(A, lambda a: a.activation(B["stat2@m3"][:, :], B["stat2@m3"][:, :], AF.Ln), waits=[ev2])
        ev_rms = P.add(A, lambda a: a.activation(B["stat2@m3"][:, :], B["stat2@m3"][:, :], AF.Exp, scale=-0.5))
        ev_yn_done = [ev_ygs[-1]]

        wfree3 = list(wfree)
        if seg == 0:
            emit_gamma_fold()

        wofree = [None, None]
        ufree = [None, None]
        ev_x1k_m4 = []
        slot_free = {0: [slot_free[0][0]], 1: [slot_free[1][0]],
                     2: [pfree["p_sh"], pfree["p_small"]]}
        ev_x1w = None
        for dc in range(16):
            par = dc % 2
            src = d_wout[dc, :, :, :].rearrange("p a b -> p (a b)")
            evd = P.dma(G, B["woslab"][par][:, :], src,
                        waits=[wofree[par]] + (m3bar if dc < 2 else []), sem=f"dwo{par}")
            nw_ap = B["nw"][:, :].unsqueeze(2).broadcast_to([128, 32, 128])
            evd = P.add(V, lambda v, par=par, nw_ap=nw_ap: v.tensor_tensor(
                B["woslab"][par][:, :].rearrange("p (a b) -> p a b", b=128), B["woslab"][par][:, :].rearrange("p (a b) -> p a b", b=128),
                nw_ap, ALU.mult), waits=[evd])
            uev = P.dma(S, B["uscr"][par][:, :], d_u[128 * dc:128 * dc + 128, l0:l0 + SEGL],
                        waits=[ufree[par]] + (m3bar if dc < 2 else [])
                        + ([ev_gamma_done[0]] if dc < 2 and seg == 0 else []),
                        sem=f"dus{par}")
            p3w = dc % 3
            pfullw = B["p_ip"][p3w]
            pb = [pfullw[:, 0:512], pfullw[:, 512:1024]]
            ev_mm = None
            for lh in range(2):
                for ec in range(32):
                    w = []
                    if lh == 0 and ec == 0:
                        w = [evd] + slot_free[p3w] + ev_yn_done
                    ev_mm = P.add(T, lambda t, par=par, lh=lh, ec=ec, pb=pb: t.matmul(
                        pb[lh][:, :], B["woslab"][par][:, 128 * ec:128 * ec + 128],
                        B["xy"][:, ec, 512 * lh:512 * lh + 512],
                        start=(ec == 0), stop=(ec == 31)), waits=w)
            wofree[par] = ev_mm
            et1 = P.add(V, lambda v, pfullw=pfullw: v.tensor_tensor(
                B["tmpX"][:, :], pfullw[:, :], B["stat2@m3"][:, :], ALU.mult),
                waits=[ev_mm, ev_rms] + ([ev_gamma_done[0]] if dc < 2 and seg == 0 else []))
            e2 = P.add(V, lambda v, dc=dc, par=par: v.tensor_tensor(
                B["X1"][:, dc, :], B["tmpX"][:, :], B["uscr"][par][:, :], ALU.add),
                waits=[uev])
            slot_free[p3w] = [e2]
            ufree[par] = e2
            ev_x1k_m4.append(e2)
            ev_x1w = e2
        pfree["p_Y0"] = pfree["p_Y1"] = slot_free[0][0]
        pfree["p_diff0"] = pfree["p_diff1"] = slot_free[1][0]
        pfree["p_sh"] = pfree["p_small"] = slot_free[2][0]

        ev_rms_mm_guard = [ev_rms_mm] + P.barrier()
        ev_smm = ev_umm = None
        smm_by_k = []
        for k in range(16):
            sqbuf = B["sqb@m5"] if k % 2 == 0 else B["sqb@m0"]
            evq = P.add(A, lambda a, k=k, sqbuf=sqbuf: a.activation(sqbuf[:, :], B["X1"][:, k, :], AF.Square),
                        waits=[ev_x1k_m4[k]] + ([smm_by_k[k - 2]] if k >= 2 else [])
                        + (ev_rms_mm_guard if k < 2 else []))
            for lh in range(2):
                ev_smm = P.add(T, lambda t, k=k, lh=lh, sqbuf=sqbuf: t.matmul(
                    B["p_diff"][lh][:, :], B["o128b"][:, :], sqbuf[:, 512 * lh:512 * lh + 512],
                    start=(k == 0), stop=(k == 15)),
                    waits=[evq] + ([pfree[f"p_diff{lh}"]] if k == 0 else []))
            smm_by_k.append(ev_smm)
        for k in range(16):
            for lh in range(2):
                ev_umm = P.add(T, lambda t, k=k, lh=lh: t.matmul(
                    B["p_Y"][lh][:, :], B["o128b"][:, :], B["X1"][:, k, 512 * lh:512 * lh + 512],
                    start=(k == 0), stop=(k == 15)),
                    waits=[ev_x1k_m4[k]] + ([pfree["p_Y0"], pfree["p_Y1"]] if k == 0 else []))
        P.add(A, lambda a: a.mul(B["stat1@m5"][:, :], B["p_ip"][0][:, :], 1.0 / DM),
              waits=[ev_umm, ev_smm])
        ev_v2 = P.add(A, lambda a: a.mul(B["stat2@m5"][:, :], B["p_ip"][1][:, :], 1.0 / DM))
        pfree["p_Y0"] = pfree["p_Y1"] = pfree["p_diff0"] = pfree["p_diff1"] = ev_v2
        evm = P.add(A, lambda a: a.activation(B["sz@m5"][:, :], B["stat1@m5"][:, :], AF.Square),
                    waits=[ev_v2])
        evv = P.add(V, lambda v: v.scalar_tensor_tensor(B["stat2@m5"][:, :], B["stat2@m5"][:, :], float(EPS_LN), B["sz@m5"][:, :], ALU.add, ALU.subtract),
                    waits=[evm])
        P.add(A, lambda a: a.activation(B["stat2@m5"][:, :], B["stat2@m5"][:, :], AF.Ln),
              waits=[evv])
        ev_rstd = P.add(A, lambda a: a.activation(B["stat2@m5"][:, :], B["stat2@m5"][:, :], AF.Exp, scale=-0.5))
        wh2 = list(wfree) + [wofree[0], wofree[1]]
        ev_h2k = []
        for k in range(16):
            evs_ = P.add(V, lambda v, k=k: v.tensor_tensor(B["convo@m5"][k % 2][:, :], B["X1"][:, k, :], B["stat1@m5"][:, :], ALU.subtract),
                         waits=[ev_rstd] + (wh2 if k == 0 else []))
            ev_h2k.append(P.add(V, lambda v, k=k: v.tensor_tensor(B["h1"][:, k, :], B["convo@m5"][k % 2][:, :], B["stat2@m5"][:, :], ALU.mult)))
        ev_h2done = [ev_h2k[-1]]

        m5bar = P.barrier()
        rounds = [list(range(i, min(i + FFN_R, N_HCH))) for i in range(0, N_HCH, FFN_R)]
        w1f = [None, None]
        w2f = [None, None]
        w3f = [None, None]
        q1f = [None, None]
        q2f = [None, None]
        q3lf = [None] * FFN_R
        hidf = [None] * FFN_R
        x1pf = {0: pfree["p_Y0"], 1: pfree["p_Y1"]}
        x2pf = {0: pfree["p_diff0"], 1: pfree["p_diff1"]}
        w3pf = [pfree["p_w3"], pfree["p_sh"]]
        sigf = None
        qtmpf = [None]
        outf = [None, None]
        mam_bar = P.barrier()
        ev_last_w3 = None
        for ri, rnd in enumerate(rounds):
            hid_ready = [None] * len(rnd)
            q3_ready = [None] * len(rnd)
            for li, hc in enumerate(rnd):
                par = li % 2

                def quant(dslab, wslabs, qbufs, tcol, wf, qf, extra, dcache):
                    if seg != 0:
                        dq = P.dma(S, qbufs[par][:, :], dcache[hc, :, :],
                                   waits=[qf[0], qf[1], qcache_done.get(tcol)] + extra,
                                   sem=f"dq{tcol}{par}")
                        wf[par] = dq
                        return dq
                    d_ = P.dma(S, wslabs[par][:, :], dslab,
                               waits=[wf[0], wf[1]] + extra, sem=f"dwf{tcol}{par}")
                    if tcol == 1:
                        eb = P.add(V, lambda v, q=qbufs[par], w=wslabs[par], tc=tcol: v.tensor_scalar(
                            q[:, :], w[:, :], B["tneg"][:, tc:tc + 1], None, ALU.is_lt),
                            waits=[d_, qf[0], qf[1], qcache_done.get(tcol), ev_gamma_done[0]])
                        ea2 = P.add(V, lambda v, w=wslabs[par], tc=tcol: v.tensor_scalar(
                            B["qtmp"][:, :], w[:, :], B["tpp"][:, tc:tc + 1], None, ALU.is_gt),
                            waits=[qtmpf[0]])
                        eq = P.add(V, lambda v, q=qbufs[par]: v.tensor_tensor(
                            q[:, :], B["qtmp"][:, :], q[:, :], ALU.subtract))
                    else:
                        ea = P.add(A, lambda a, w=wslabs[par], tc=tcol: a.activation(
                            B["qtmp"][:, :], w[:, :], AF.Sign, bias=B["tneg"][:, tc:tc + 1]),
                            waits=[d_, qtmpf[0], ev_gamma_done[0]])
                        eb = P.add(A, lambda a, q=qbufs[par], w=wslabs[par], tc=tcol: a.activation(
                            q[:, :], w[:, :], AF.Sign, bias=B["tpp"][:, tc:tc + 1]),
                            waits=[qf[0], qf[1], qcache_done.get(tcol)])
                        eq = P.add(V, lambda v, q=qbufs[par]: v.tensor_tensor(
                            q[:, :], q[:, :], B["qtmp"][:, :], ALU.add),
                            waits=[ea, eb])
                    qtmpf[0] = eq
                    wf[par] = eq
                    qcache_done[tcol] = P.dma(G, dcache[hc, :, :], qbufs[par][:, :],
                                              waits=[eq], sem=f"dqc{tcol}")
                    return eq
                eq1 = quant(d_w1[hc, :, :, :].rearrange("p a b -> p (a b)"), B["w1slab"],
                            B["q1"], 0, w1f, q1f, (m5bar if ri == 0 and li < 2 else []),
                            d_q1s)
                eq2 = quant(d_w2[hc, :, :, :].rearrange("p a b -> p (a b)"), B["w2slab"],
                            B["q2"], 1, w2f, q2f, [w3f[0], w3f[1]] + (m5bar if ri == 0 and li < 2 else []),
                            d_q2s)
                ex1 = ex2 = None
                for lh in range(2):
                    for k in range(16):
                        w = [ev_h2k[k]]
                        if lh == 0 and k == 0:
                            w = w + [eq1, x1pf[0], x1pf[1]]
                        ex1 = P.add(T, lambda t, par=par, lh=lh, k=k: t.matmul(
                            B["p_Y"][lh][:, :], B["q1"][par][:, 128 * k:128 * k + 128],
                            B["h1"][:, k, 512 * lh:512 * lh + 512],
                            start=(k == 0), stop=(k == 15)), waits=w)
                for lh in range(2):
                    for k in range(16):
                        w = []
                        if lh == 0 and k == 0:
                            w = [eq2, x2pf[0], x2pf[1]]
                        ex2 = P.add(T, lambda t, par=par, lh=lh, k=k: t.matmul(
                            B["p_diff"][lh][:, :], B["q2"][par][:, 128 * k:128 * k + 128],
                            B["h1"][:, k, 512 * lh:512 * lh + 512],
                            start=(k == 0), stop=(k == 15)), waits=w)
                q1f[0] = q1f[1] = ex1
                q2f[0] = q2f[1] = ex2
                es2 = P.add(A, lambda a: a.activation(B["sig"][:, :], B["p_ip"][0][:, :], AF.Sigmoid, scale=0.5),
                            waits=[ex1, sigf])
                x1pf = {0: es2, 1: es2}
                eh2 = P.add(V, lambda v, li=li: v.tensor_tensor(
                    B["hid"][:, li, :], B["sig"][:, :], B["p_ip"][1][:, :], ALU.mult),
                    waits=[es2, ex2, hidf[li]] + (mam_bar if ri == 0 and li < 2 else []))
                x2pf = {0: eh2, 1: eh2}
                sigf = eh2
                hid_ready[li] = eh2
                if seg == 0:
                    d3 = P.dma(S, B["w3slab"][par][:, :], d_w3[hc, :, :],
                               waits=[w3f[0], w3f[1], eq2], sem=f"dw3{par}")
                    ec_ = P.add(A, lambda a, par=par: a.activation(
                        B["qtmp"][:, :], B["w3slab"][par][:, :], AF.Sign, bias=B["tneg"][:, 2:3]),
                        waits=[d3, qtmpf[0]])
                    eb3 = P.add(A, lambda a, par=par, li=li: a.activation(
                        B["q3l"][:, li, :], B["w3slab"][par][:, :], AF.Sign, bias=B["tpp"][:, 2:3]),
                        waits=[q3lf[li], qcache_done.get(2)] + (mam_bar if ri == 0 and li < 2 else []))
                    eq3 = P.add(V, lambda v, li=li: v.tensor_tensor(
                        B["q3l"][:, li, :], B["q3l"][:, li, :], B["qtmp"][:, :], ALU.add),
                        waits=[ec_, eb3])
                    qtmpf[0] = eq3
                    w3f[par] = eq3
                    qcache_done[2] = P.dma(G, d_q3s[hc, :, :], B["q3l"][:, li, :],
                                           waits=[eq3], sem="dqc2")
                else:
                    eq3 = P.dma(S, B["q3l"][:, li, :], d_q3s[hc, :, :],
                                waits=[q3lf[li], qcache_done.get(2)] + (mam_bar if ri == 0 and li < 2 else []),
                                sem=f"dq3l{par}")
                q3_ready[li] = eq3
            for dc in range(16):
                for lh in range(2):
                    pwb = B["p_w3"] if (dc * 2 + lh) % 2 == 0 else B["p_sh"]
                    pw_i = (dc * 2 + lh) % 2
                    ev_mm = None
                    for li in range(len(rnd)):
                        w = []
                        if li == 0:
                            w = [w3pf[pw_i], q3_ready[-1], hid_ready[-1]] + (
                                [q3_ready[-2]] if len(rnd) > 1 else [])
                        ev_mm = P.add(T, lambda t, dc=dc, lh=lh, li=li, pwb=pwb,
                                      st_=(li == 0), sp_=(li == len(rnd) - 1): t.matmul(
                            pwb[:, :], B["q3l"][:, li, 128 * dc:128 * dc + 128],
                            B["hid"][:, li, 512 * lh:512 * lh + 512],
                            start=st_, stop=sp_), waits=w)
                    ev_last_w3 = ev_mm
                    if ri < len(rounds) - 1:
                        evac = P.add(V, lambda v, dc=dc, lh=lh, pwb=pwb: v.scalar_tensor_tensor(
                            B["X1"][:, dc, 512 * lh:512 * lh + 512], pwb[:, :], 0.5,
                            B["X1"][:, dc, 512 * lh:512 * lh + 512], ALU.mult, ALU.add),
                            waits=[ev_mm])
                    else:
                        sp = (dc * 2 + lh) % 2
                        evac = P.add(V, lambda v, dc=dc, lh=lh, pwb=pwb, sp=sp: v.scalar_tensor_tensor(
                            B["outstg"][sp][:, :], pwb[:, :], 0.5,
                            B["X1"][:, dc, 512 * lh:512 * lh + 512], ALU.mult, ALU.add),
                            waits=[ev_mm, outf[sp]])
                        odst = d_out[128 * dc:128 * dc + 128, l0 + 512 * lh:l0 + 512 * lh + 512]
                        oev = P.dma(S, odst, B["outstg"][sp][:, :], waits=[evac], sem=f"dout{sp}")
                        outf[sp] = oev
                        out_dma_evs.append(oev)
                    w3pf[pw_i] = evac
            for li in range(len(rnd)):
                hidf[li] = ev_last_w3
                q3lf[li] = ev_last_w3
        pfree["p_w3"] = w3pf[0]
        pfree["p_sh"] = w3pf[1]
        pfree["p_Y0"] = pfree["p_Y1"] = x1pf[0]
        pfree["p_diff0"] = pfree["p_diff1"] = x2pf[0]

    P.add(G, lambda g: g.engine_nop(), waits=out_dma_evs)
    P.emit()
    return nc


_NC_CACHE = [None]


def _host_layout(inputs):
    f = np.float32
    W_in = np.asarray(inputs["L0_attn_W_in"], f)
    W_out = np.asarray(inputs["L0_attn_W_out"], f)
    w1 = np.asarray(inputs["L0_ffn_w1"], f)
    w2 = np.asarray(inputs["L0_ffn_w2"], f)
    w3 = np.asarray(inputs["L0_ffn_w3"], f)
    conv_w = np.asarray(inputs["L0_attn_conv_w"], f)
    conv_b = np.asarray(inputs["L0_attn_conv_b"], f)
    nw = np.asarray(inputs["L0_attn_norm_weight"], f)
    dtb = np.asarray(inputs["L0_attn_dt_bias"], f)
    alog = np.asarray(inputs["L0_attn_A_log"], f)
    Dv = np.asarray(inputs["L0_attn_D"], f)

    WinT = np.ascontiguousarray(W_in.T)
    WinT_p = np.zeros((DM, N_ECH * 128), f)
    WinT_p[:, :8512] = WinT
    winT_t = np.ascontiguousarray(
        WinT_p.reshape(16, 128, N_ECH, 128).transpose(2, 1, 0, 3))
    woutT_t = np.ascontiguousarray(
        W_out.T.reshape(32, 128, 16, 128).transpose(2, 1, 0, 3))
    w1T_t = np.ascontiguousarray(
        w1.T.reshape(16, 128, N_HCH, 128).transpose(2, 1, 0, 3))
    w2T_t = np.ascontiguousarray(
        w2.T.reshape(16, 128, N_HCH, 128).transpose(2, 1, 0, 3))
    w3T_t = np.ascontiguousarray(w3.T.reshape(N_HCH, 128, 2048))
    cw_pp = np.ascontiguousarray(conv_w.reshape(34, 128, 4).transpose(1, 0, 2))
    cb_pp = np.ascontiguousarray(conv_b.reshape(34, 128).T)
    nw_pp = np.ascontiguousarray(nw.reshape(32, 128).T)
    dd32 = np.zeros((128, 32, 128), f)
    for i in range(32):
        for p_ in range(64):
            dd32[p_, i, p_] = Dv[2 * i]
            dd32[64 + p_, i, 64 + p_] = Dv[2 * i + 1]
    consts = dict(
        winT_t=winT_t, woutT_t=woutT_t, w1T_t=w1T_t, w2T_t=w2T_t, w3T_t=w3T_t,
        cw_pp=cw_pp, cb_pp=cb_pp, nw_pp=nw_pp,
        dtb=dtb.reshape(64, 1), alog=alog.reshape(64, 1), dd32=dd32,
        idn=np.eye(128, dtype=bfdt),
        trilT=(np.arange(128)[:, None] <= np.arange(128)[None, :]).astype(bfdt),
        o1f=np.ones((1, 128), f), o1b=np.ones((1, 128), bfdt),
        o128b=np.ones((128, 128), bfdt), o128f=np.ones((128, 128), f),
        o128f1=np.ones((128, 1), f),
    )
    return consts


def kernel(**inputs):
    consts = _host_layout(inputs)
    u = np.asarray(inputs["L0_attn_u"], np.float32)
    if _NC_CACHE[0] is None:
        _NC_CACHE[0] = build_program()
    nc = _NC_CACHE[0]
    in_maps = []
    for b in range(8):
        m = dict(consts)
        m["u_fm"] = np.ascontiguousarray(u[b].T)
        in_maps.append(m)
    res = run_bass_kernel_spmd(nc, in_maps, core_ids=list(range(8)))
    out = np.stack([np.asarray(res.results[b]["out_fm"]).T for b in range(8)])
    return np.ascontiguousarray(out, dtype=np.float32)


if __name__ == "__main__":
    nc = build_program()
    print("build ok, instructions:", nc.n_instructions())


# version 22
# speedup vs baseline: 1.1625x; 1.1625x over previous
import numpy as np
import ml_dtypes
import concourse.bass as bass
import concourse.mybir as mybir
from concourse.bass_utils import run_bass_kernel_spmd

F32 = mybir.dt.float32
BF16 = mybir.dt.bfloat16
AF = mybir.ActivationFunctionType
ALU = mybir.AluOpType
AX = mybir.AxisListType
bfdt = ml_dtypes.bfloat16

V, A, T, G, S = "vector", "scalar", "tensor", "gpsimd", "sync"
sV, sA, sT, sG, sS = "sV", "sA", "sT", "sGc", "sS"

SEGL = 1024
NSEG = 2
DM = 2048
DI = 4096
HFF = 5632
N_ECH = 67
N_HCH = 44
FFN_R = 8
EPS_LN = 1e-5
EPS_RMS = 1e-5
GAMMA_EPS = 1e-8
NW1 = float(HFF * DM)


class Pipe:
    ENGINES = ("sync", "scalar", "vector", "tensor", "gpsimd")
    CHAINED = {"vector": "sV", "scalar": "sA", "gpsimd": "sGc"}

    def __init__(self, nc):
        self.nc = nc
        self.ops = {e: [] for e in self.ENGINES}
        self.sem_names = []
        self.counts = {}
        self._last_compute = {}

    def new_sem(self, name):
        if name in self.counts:
            return name
        self.sem_names.append(name)
        self.counts[name] = 0
        return name

    def add(self, engine, fn, waits=(), inc=None, amt=1, _is_dma=False):
        ev = None
        waits = [w for w in waits if w is not None]
        if not _is_dma and engine in self.CHAINED:
            if inc is None:
                inc = self.CHAINED[engine]
            prev = self._last_compute.get(engine)
            if prev is not None:
                waits = waits + [prev]
        if engine == "tensor" and inc is None:
            inc = "sT"
        if inc is not None:
            self.new_sem(inc)
            self.counts[inc] += amt
            ev = (inc, self.counts[inc])
        if not _is_dma and engine in self.CHAINED:
            self._last_compute[engine] = ev
        self.ops[engine].append((fn, tuple(waits), inc, amt))
        return ev

    def dma(self, engine, out, in_, waits=(), sem=None, **kw):
        assert sem is not None
        self.new_sem(sem)
        return self.add(engine,
                        lambda eng, o=out, i=in_: eng.dma_start(out=o, in_=i, **kw),
                        waits=waits, inc=sem, amt=16, _is_dma=True)

    def last(self, engine):
        if engine == "tensor":
            n = self.counts.get("sT", 0)
            return ("sT", n) if n else None
        return self._last_compute.get(engine)

    def barrier(self):
        return [e for e in (self.last(V), self.last(A), self.last(T), self.last(G))
                if e is not None]

    def check(self):
        idx = {e: 0 for e in self.ENGINES}
        cnt = {s: 0 for s in self.sem_names}
        progress = True
        while progress:
            progress = False
            for e in self.ENGINES:
                while idx[e] < len(self.ops[e]):
                    fn, waits, inc, amt = self.ops[e][idx[e]]
                    if all(cnt[s] >= v for s, v in waits):
                        if inc is not None:
                            cnt[inc] += amt
                        idx[e] += 1
                        progress = True
                    else:
                        break
        stuck = {e: (idx[e], len(self.ops[e])) for e in self.ENGINES
                 if idx[e] < len(self.ops[e])}
        if stuck:
            detail = {}
            for e in stuck:
                fn, waits, inc, amt = self.ops[e][idx[e]]
                detail[e] = [(s, v, cnt[s]) for s, v in waits if cnt[s] < v]
            raise RuntimeError(f"Pipe deadlock: stuck={stuck} waits={detail}")

    def emit(self):
        self.check()
        nc = self.nc
        sems = {}
        import contextlib
        with contextlib.ExitStack() as stck:
            for s in self.sem_names:
                sems[s] = stck.enter_context(nc.semaphore(s))
            block = stck.enter_context(nc.Block())
            for e in self.ENGINES:
                if not self.ops[e]:
                    continue
                ops = self.ops[e]

                def section(eng, ops=ops, ename=e):
                    watermark = {}
                    for fn, waits, inc, amt in ops:
                        agg = {}
                        for s, v in waits:
                            if v > agg.get(s, -1):
                                agg[s] = v
                        for s, v in agg.items():
                            if watermark.get(s, -1) < v:
                                eng.wait_ge(sems[s], v)
                                watermark[s] = v
                        ins = fn(eng)
                        if inc is not None:
                            assert ins is not None, f"None instr on {ename}"
                            ins.then_inc(sems[inc], amt)
                getattr(block, e)(section)


class Overlay:
    def __init__(self, nc, base, size, tag):
        self.nc, self.base, self.size, self.tag = nc, base, size, tag
        self.off = 0
        self.n = 0

    def reset(self):
        self.off = 0

    def alloc(self, name, shape, dtype=F32):
        per_part = int(np.prod(shape[1:])) * mybir.dt.size(dtype)
        per_part = (per_part + 31) // 32 * 32
        assert self.off + per_part <= self.size, \
            f"overlay {self.tag} overflow: {name} {self.off}+{per_part}>{self.size}"
        self.n += 1
        t = self.nc.alloc_sbuf_tensor_at(f"{self.tag}{self.n}_{name}", shape, dtype,
                                         offset=self.base + self.off)
        self.off += per_part
        return t



def ssd_init(P, B, ev_consts):
    ev_dd = P.add(V, lambda v: v.tensor_copy(B["Ddiag"][:, :, :], B["Ddiag32"][:, :, :]),
                  waits=list(ev_consts))
    ev_s0 = P.add(V, lambda v: v.memset(B["STATES_bf"][:, :], 0.0))
    st = {
        "stage_ok": [ev_dd],
        "states_bf_ready": [ev_s0],
        "psh_free": None, "pY_free": [None, None], "pdiff_free": [None, None],
        "stage_free": [None, None], "dmin_free": [None, None], "E_free": [None, None],
        "ST_free": [None] * 4, "CE_free": [None] * 4,
        "pxT_free": None, "psmall_free": None, "cdrow_free": None,
        "xdt_freeT": None, "xdtd_freeT": None, "dtbf_freeT": None,
        "decaybf_freeT": None, "Btm_freeT": None, "cdb_freeT": None,
        "ST_CE_init": ev_dd,
    }
    return st


def emit_ssd_chunk(P, B, c, ev_ready, st):
    lo = c * 128
    sl = slice(lo, lo + 128)

    P.add(V, lambda v: v.tensor_scalar_mul(B["dA"][:, :], B["dt_hm"][:, sl], B["A_col"][:, 0:1]),
          waits=list(ev_ready["dt"]))
    ev_acs = P.add(V, lambda v: v.tensor_tensor_scan(B["Acs"][:, :], B["dA"][:, :], B["z64"][:, :], 0.0, ALU.add, ALU.add))
    ev_negacs = P.add(V, lambda v: v.tensor_scalar_mul(B["negAcs"][:, :], B["Acs"][:, :], -1.0))
    ev_tdec = P.add(V, lambda v: v.tensor_scalar(B["tdec"][:, :], B["Acs"][:, :], B["Acs"][:, 127:128], None, ALU.subtract))
    ev_dtbf = P.add(V, lambda v: v.tensor_copy(B["dtbf"][:, :], B["dt_hm"][:, sl]),
                    waits=[st["dtbf_freeT"]])
    ev_dec = P.add(A, lambda a: a.activation(B["decay"][:, :], B["tdec"][:, :], AF.Exp, scale=-1.0),
                   waits=[ev_tdec])
    ev_eacs = P.add(A, lambda a: a.activation(B["eacs"][:, :], B["Acs"][:, :], AF.Exp),
                    waits=[ev_acs])
    ev_cd = P.add(A, lambda a: a.activation(B["cd"][:, 0:1], B["Acs"][:, 127:128], AF.Exp),
                  waits=[ev_acs])
    ev_decbf = P.add(V, lambda v: v.tensor_copy(B["decaybf"][:, :], B["decay"][:, :]),
                     waits=[ev_dec, st["decaybf_freeT"]])

    ev_t1 = P.add(T, lambda t: t.transpose(B["p_xT"][:, 0:64], B["dtbf"][:, :], B["idn"][0:64, 0:64]),
                  waits=[st["pxT_free"], ev_dtbf])
    P.add(T, lambda t: t.transpose(B["p_xT"][:, 64:128], B["decaybf"][:, :], B["idn"][0:64, 0:64]),
          waits=[ev_decbf])
    ev_t3 = P.add(T, lambda t: t.transpose(B["p_xT"][:, 128:256], B["B_fm"][:, sl], B["idn"][:, :]),
                  waits=list(ev_ready["BC"]) + [st["Btm_freeT"]])
    ev_dtT = P.add(V, lambda v: v.tensor_copy(B["dt_tm"][:, :], B["p_xT"][:, 0:64]), waits=[ev_t3])
    ev_decT = P.add(V, lambda v: v.tensor_copy(B["decay_tm"][:, :], B["p_xT"][:, 64:128]))
    ev_BT = P.add(V, lambda v: v.tensor_copy(B["B_tm"][:, :], B["p_xT"][:, 128:256]))

    ev_g = P.add(T, lambda t: t.matmul(B["p_small"][:, 0:128], B["B_fm"][:, sl], B["C_fm"][:, sl],
                                       start=True, stop=True),
                 waits=list(ev_ready["BC"]) + [st["psmall_free"]])
    ev_Gm = P.add(V, lambda v: v.tensor_tensor(B["Gm"][:, :], B["p_small"][:, 0:128], B["tril"][:, :], ALU.mult),
                  waits=[ev_g, st["ST_CE_init"]])

    ev_cdrow = P.dma(G, B["cd_row"][0:1, :], B["cd"][:, 0:1],
                     waits=[ev_cd, st.get("cdrow_free")], sem="cdr")
    ev_cdmm = P.add(T, lambda t: t.matmul(B["p_small"][:, 192:256], B["ones1_f32"][0:1, :], B["cd_row"][0:1, :],
                                          start=True, stop=True),
                    waits=[ev_cdrow, ev_Gm])
    ev_cdb = P.add(V, lambda v: v.tensor_copy(B["cdb"][:, :], B["p_small"][:, 192:256]),
                   waits=[ev_cdmm, st["cdb_freeT"]])

    prev_evac = [ev_dtT, ev_decT, ev_BT]
    ev_x1 = ev_x2 = None
    prev_round_tt = {0: None, 1: st.get("pw3_free")}
    for r in range(4):
        pxb = B["p_xT"] if r % 2 == 0 else B["p_w3"].bitcast(BF16)
        ev_tr = None
        for k in range(8):
            pc = 8 * r + k
            w = (list(ev_ready["x"]) + prev_evac + [prev_round_tt[r % 2]]) if k == 0 else ()
            ev_tr = P.add(
                T, lambda t, pc=pc, k=k, pxb=pxb: t.transpose(
                    pxb[:, 128 * k:128 * k + 128], B["xy"][:, pc, sl], B["idn"][:, :]),
                waits=w)

        def xdt_fn(v, r=r, pxb=pxb):
            dt_ap = B["dt_tm"][:, 16 * r:16 * r + 16]
            dt_ap = dt_ap.rearrange("p (pc par) -> p pc par", par=2).unsqueeze(3).broadcast_to([128, 8, 2, 64])
            return v.tensor_tensor(B["xdt"][:, 1024 * r:1024 * r + 1024], pxb[:, 0:1024], dt_ap, ALU.mult)
        w = [ev_tr, ev_dtT] + ([st["xdt_freeT"]] if r == 0 else [])
        ev_x1 = P.add(V, xdt_fn, waits=w)
        prev_round_tt[r % 2] = ev_x1

        def xdtd_fn(v, r=r):
            de_ap = B["decay_tm"][:, 16 * r:16 * r + 16]
            de_ap = de_ap.rearrange("p (pc par) -> p pc par", par=2).unsqueeze(3).broadcast_to([128, 8, 2, 64])
            return v.tensor_tensor(B["xdtd"][:, 1024 * r:1024 * r + 1024], B["xdt"][:, 1024 * r:1024 * r + 1024], de_ap, ALU.mult)
        w = [ev_decT] + ([st["xdtd_freeT"]] if r == 0 else [])
        ev_x2 = P.add(V, xdtd_fn, waits=w)
        prev_evac = []
    ev_xdt, ev_xdtd = ev_x1, ev_x2
    st["pxT_free"] = prev_round_tt[0]
    st["pw3_free"] = prev_round_tt[1]

    ev_ST = [None] * 16
    ev_CE = [None] * 16
    ev_y_evacs = []
    last_pair_mm = None
    st["eacsb_free"] = [st["psh_free"], st["psmall_free"]]

    def emit_group(g):
        p = g % 2
        q4 = g % 4
        rs = slice(4 * g, 4 * g + 4)
        wst = [st["stage_free"][p], ev_negacs, ev_eacs] + (st["stage_ok"] if g < 2 else [])
        P.dma(G, B["stageL"][p][1:2, :], B["negAcs"][rs, :], waits=wst, sem=f"stg{p}")
        P.dma(G, B["stageR"][p][0:1, :], B["Acs"][rs, :], sem=f"stg{p}")
        e3 = P.dma(G, B["eacsrow"][p][0:1, :], B["eacs"][rs, :], sem=f"stg{p}")
        evd = None
        for j in range(4):
            w = [st["pdiff_free"][p], e3] if j == 0 else ()
            evd = P.add(
                T, lambda t, p=p, j=j: t.matmul(
                    B["p_diff"][p][:, 128 * j:128 * j + 128],
                    B["stageL"][p][0:2, 128 * j:128 * j + 128],
                    B["stageR"][p][0:2, 128 * j:128 * j + 128],
                    start=True, stop=True),
                waits=w)
        ebank = B["p_sh"] if (g % 2 == 0) else B["p_small"]
        wb = [st["eacsb_free"][g % 2]]
        if g == 1:
            wb.append(ev_cdb)
        ev_eb = P.add(T, lambda t, p=p, ebank=ebank: t.matmul(ebank[:, :], B["ones1_bf"][0:1, :], B["eacsrow"][p][0:1, :],
                                                 start=True, stop=True),
                      waits=wb)
        ev_min = P.add(A, lambda a, p=p: a.activation(B["dmin"][p][:, :], B["p_diff"][p][:, :], AF.Relu, scale=-1.0),
                       waits=[st["dmin_free"][p], evd])
        st["pdiff_free"][p] = ev_min
        st["stage_free"][p] = ev_eb
        ev_E = P.add(A, lambda a, p=p: a.activation(B["dmin"][p][:, :], B["dmin"][p][:, :], AF.Exp, scale=-1.0))

        def st_fn(v, p=p, q4=q4):
            gm_ap = B["Gm"][:, :].unsqueeze(1).broadcast_to([128, 4, 128])
            return v.tensor_tensor(B["ST"][q4][:, :], B["dmin"][p][:, :], gm_ap, ALU.mult)
        ev_st_ = P.add(V, st_fn, waits=[st["ST_free"][q4], ev_E, ev_Gm])
        st["dmin_free"][p] = ev_st_
        ev_ST[g] = ev_st_

        def ce_fn(v, q4=q4, sl=sl, ebank=ebank):
            c_ap = B["C_fm"][:, sl].unsqueeze(1).broadcast_to([128, 4, 128])
            return v.tensor_tensor(B["CE"][q4][:, :], c_ap, ebank[:, :], ALU.mult)
        ev_ce = P.add(V, ce_fn, waits=[st["CE_free"][q4], ev_eb])
        st["eacsb_free"][g % 2] = ev_ce
        ev_CE[g] = ev_ce

    def emit_yfill(q):
        nonlocal last_pair_mm
        b = q % 2
        last = None
        for m in range(4):
            i = 4 * q + m
            h0, h1 = 2 * i, 2 * i + 1
            g0, j0 = h0 // 4, h0 % 4
            g1, j1 = h1 // 4, h1 % 4
            col = slice(128 * m, 128 * m + 128)
            w0 = [ev_ST[g0], ev_ST[g1], ev_CE[g0], ev_CE[g1], ev_xdt]
            if m == 0:
                w0 = w0 + [st["pY_free"][b]] + list(st["states_bf_ready"])
            P.add(T, lambda t, b=b, col=col, h0=h0, g0=g0, j0=j0: t.matmul(
                B["p_Y"][b][0:64, col], B["xdt"][:, 64 * h0:64 * h0 + 64],
                B["ST"][g0 % 4][:, 128 * j0:128 * j0 + 128],
                start=True, stop=False, tile_position=(0, 0)),
                waits=w0)
            P.add(T, lambda t, b=b, col=col, h0=h0, g0=g0, j0=j0: t.matmul(
                B["p_Y"][b][0:64, col], B["STATES_bf"][:, 64 * h0:64 * h0 + 64],
                B["CE"][g0 % 4][:, 128 * j0:128 * j0 + 128],
                start=False, stop=False, tile_position=(0, 0)))
            P.add(T, lambda t, b=b, col=col, i=i, sl=sl: t.matmul(
                B["p_Y"][b][0:64, col], B["Ddiag"][:, i, 0:64], B["xy"][:, i, sl],
                start=False, stop=True, tile_position=(0, 0)))
            P.add(T, lambda t, b=b, col=col, h1=h1, g1=g1, j1=j1: t.matmul(
                B["p_Y"][b][64:128, col], B["xdt"][:, 64 * h1:64 * h1 + 64],
                B["ST"][g1 % 4][:, 128 * j1:128 * j1 + 128],
                start=True, stop=False, tile_position=(0, 64)))
            P.add(T, lambda t, b=b, col=col, h1=h1, g1=g1, j1=j1: t.matmul(
                B["p_Y"][b][64:128, col], B["STATES_bf"][:, 64 * h1:64 * h1 + 64],
                B["CE"][g1 % 4][:, 128 * j1:128 * j1 + 128],
                start=False, stop=False, tile_position=(0, 64)))
            last = P.add(T, lambda t, b=b, col=col, i=i, sl=sl: t.matmul(
                B["p_Y"][b][64:128, col], B["Ddiag"][:, i, 64:128], B["xy"][:, i, sl],
                start=False, stop=True, tile_position=(0, 64)))
        last_pair_mm = last

        def yevac_fn(a, b=b, q=q, sl=sl):
            return a.copy(B["xy"][:, 4 * q:4 * q + 4, sl], B["p_Y"][b][:, :])
        ev_ye = P.add(A, yevac_fn, waits=[last])
        st["pY_free"][b] = ev_ye
        ev_y_evacs.append(ev_ye)
        for g in (2 * q, 2 * q + 1):
            st["ST_free"][g % 4] = last
            st["CE_free"][g % 4] = last

    for q in range(9):
        if q < 8:
            emit_group(2 * q)
            emit_group(2 * q + 1)
        if q >= 1:
            emit_yfill(q - 1)

    ev_states = []
    sbf_ready = []
    psh_free = st["eacsb_free"][0]
    last_smm = None
    for j in range(8):
        colj = slice(512 * j, 512 * j + 512)
        wmm = [ev_xdtd, ev_BT, psh_free] + ([last_pair_mm] if j == 0 else [])
        ev_smm = P.add(T, lambda t, colj=colj: t.matmul(B["p_sh"][:, :], B["B_tm"][:, :], B["xdtd"][:, colj],
                                                        start=True, stop=True),
                       waits=wmm)
        last_smm = ev_smm

        def tmul_fn(v, j=j, colj=colj):
            cdb_ap = B["cdb"][:, 8 * j:8 * j + 8].unsqueeze(2).broadcast_to([128, 8, 64])
            return v.tensor_tensor(B["tscr"][:, :], B["STATES_bf"][:, colj], cdb_ap, ALU.mult)
        ev_tm = P.add(V, tmul_fn, waits=[ev_cdb, last_pair_mm])
        ev_up = P.add(V, lambda v, colj=colj: v.tensor_tensor(B["STATES_bf"][:, colj], B["tscr"][:, :], B["p_sh"][:, :], ALU.add),
                      waits=[ev_smm, ev_tm])
        sbf_ready.append(ev_up)
        psh_free = ev_up
        ev_states.append(ev_up)

    st["states_bf_ready"] = sbf_ready
    st["psh_free"] = psh_free
    st["psmall_free"] = st["eacsb_free"][1]
    st["cdrow_free"] = ev_cdmm
    st["xdt_freeT"] = last_pair_mm
    st["xdtd_freeT"] = last_smm
    st["dtbf_freeT"] = ev_t1
    st["decaybf_freeT"] = ev_t1
    st["Btm_freeT"] = last_smm
    st["cdb_freeT"] = ev_states[-1]
    return {"y": ev_y_evacs, "states": ev_states}



def build_program():
    nc = bass.Bass("TRN2", target_bir_lowering=False)

    d_u = nc.dram_tensor("u_fm", [DM, 2048], F32, kind="ExternalInput").ap()
    d_win = nc.dram_tensor("winT_t", [N_ECH, 128, 16, 128], F32, kind="ExternalInput").ap()
    d_wout = nc.dram_tensor("woutT_t", [16, 128, 32, 128], F32, kind="ExternalInput").ap()
    d_w1 = nc.dram_tensor("w1T_t", [N_HCH, 128, 16, 128], F32, kind="ExternalInput").ap()
    d_w2 = nc.dram_tensor("w2T_t", [N_HCH, 128, 16, 128], F32, kind="ExternalInput").ap()
    d_w3 = nc.dram_tensor("w3T_t", [N_HCH, 128, 2048], F32, kind="ExternalInput").ap()
    d_cw = nc.dram_tensor("cw_pp", [128, 34, 4], F32, kind="ExternalInput").ap()
    d_cb = nc.dram_tensor("cb_pp", [128, 34], F32, kind="ExternalInput").ap()
    d_nw = nc.dram_tensor("nw_pp", [128, 32], F32, kind="ExternalInput").ap()
    d_dtb = nc.dram_tensor("dtb", [64, 1], F32, kind="ExternalInput").ap()
    d_alog = nc.dram_tensor("alog", [64, 1], F32, kind="ExternalInput").ap()
    d_dd = nc.dram_tensor("dd32", [128, 32, 128], F32, kind="ExternalInput").ap()
    d_idn = nc.dram_tensor("idn", [128, 128], BF16, kind="ExternalInput").ap()
    d_tril = nc.dram_tensor("trilT", [128, 128], BF16, kind="ExternalInput").ap()
    d_o1f = nc.dram_tensor("o1f", [1, 128], F32, kind="ExternalInput").ap()
    d_o1b = nc.dram_tensor("o1b", [1, 128], BF16, kind="ExternalInput").ap()
    d_o128b = nc.dram_tensor("o128b", [128, 128], BF16, kind="ExternalInput").ap()
    d_o128f = nc.dram_tensor("o128f", [128, 128], F32, kind="ExternalInput").ap()
    d_o128f1 = nc.dram_tensor("o128f1", [128, 1], F32, kind="ExternalInput").ap()
    d_out = nc.dram_tensor("out_fm", [DM, 2048], F32, kind="ExternalOutput").ap()
    d_q1s = nc.dram_tensor("q1s", [N_HCH, 128, 2048], BF16, kind="Internal").ap()
    d_q2s = nc.dram_tensor("q2s", [N_HCH, 128, 2048], BF16, kind="Internal").ap()
    d_q3s = nc.dram_tensor("q3s", [N_HCH, 128, 2048], BF16, kind="Internal").ap()

    TOTAL = 212800
    BASE = 16512
    AR_A_SZ = 65536
    AR_H_SZ = 32768
    AR_X_SZ = 32768
    AR_W_SZ = 53696
    top = BASE + TOTAL
    a_base = top - AR_A_SZ
    h_base = a_base - AR_H_SZ
    x_base = h_base - AR_X_SZ
    w_base = x_base - AR_W_SZ

    Al = lambda n, s, d=F32: nc.alloc_sbuf_tensor(n, s, d)
    B = {}
    B["STATES_bf"] = Al("STATES_bf", [128, 4096], BF16)
    B["Ddiag"] = Al("Ddiag", [128, 32, 128], BF16)
    B["idn"] = Al("c_idn", [128, 128], BF16)
    B["tril"] = Al("c_tril", [128, 128], BF16)
    B["ones1_f32"] = Al("c_o1f", [1, 128])
    B["ones1_bf"] = Al("c_o1b", [1, 128], BF16)
    B["o128b"] = Al("c_o128b", [128, 128], BF16)
    B["o128f"] = Al("c_o128f", [128, 128])
    B["o128f1"] = Al("c_o128f1", [128, 1])
    B["nw"] = Al("c_nw", [128, 32])
    B["cw"] = Al("c_cw", [128, 34, 4])
    B["cb"] = Al("c_cb", [128, 34])
    B["dtbias"] = Al("c_dtb", [64, 1])
    B["A_col"] = Al("A_col", [64, 1])
    B["stash"] = Al("stash", [128, 34, 3])
    B["tpp"] = Al("tpp", [128, 4])
    B["tneg"] = Al("tneg", [128, 4])
    B["B_fm"] = Al("B_fm", [128, SEGL], BF16)
    B["C_fm"] = Al("C_fm", [128, SEGL], BF16)
    B["dt_hm"] = Al("dt_hm", [64, SEGL])
    used = TOTAL - nc.sbuf_bytes_remaining
    assert BASE + used <= w_base, f"auto overflow: used={used}, limit={w_base - BASE}"

    arA = Overlay(nc, a_base, AR_A_SZ, "aA")
    arH = Overlay(nc, h_base, AR_H_SZ, "aH")
    arX = Overlay(nc, x_base, AR_X_SZ, "aX")
    arW = Overlay(nc, w_base, AR_W_SZ, "aW")

    B["Ddiag32"] = arA.alloc("dd32", [128, 32, 128], F32)
    arA.reset()
    B["u32"] = arA.alloc("u32", [128, 16, SEGL], F32)
    arA.reset()
    B["xy"] = arA.alloc("xy", [128, 32, SEGL], BF16)
    arA.reset()
    B["hid"] = arA.alloc("hid", [128, FFN_R, SEGL], BF16)
    B["q3l"] = arA.alloc("q3l", [128, FFN_R, 2048], BF16)

    B["h1"] = arH.alloc("h", [128, 16, SEGL], BF16)

    B["X1"] = arX.alloc("X1", [128, 16, SEGL], BF16)
    arX.reset()
    B["gslab"] = [arX.alloc(f"gslab{i}", [128, 2048], F32) for i in range(2)]
    B["gscr"] = arX.alloc("gscr", [128, 264])
    B["g3"] = arX.alloc("g3", [128, 4])
    B["grow"] = arX.alloc("grow", [1, 4])
    B["wslab3"] = arX.alloc("wslab3", [128, 2048], BF16)
    B["sqbX"] = arX.alloc("sqbX", [128, SEGL], BF16)

    B["dA"] = arW.alloc("dA", [64, 128])
    B["Acs"] = arW.alloc("Acs", [64, 128])
    B["negAcs"] = arW.alloc("negAcs", [64, 128])
    B["tdec"] = arW.alloc("tdec", [64, 128])
    B["decay"] = arW.alloc("decay", [64, 128])
    B["eacs"] = arW.alloc("eacs", [64, 128])
    B["cd"] = arW.alloc("cd", [64, 1])
    B["dtbf"] = arW.alloc("dtbf", [64, 128], BF16)
    B["decaybf"] = arW.alloc("decaybf", [64, 128], BF16)
    B["z64"] = arW.alloc("z64", [64, 128])
    B["ones512"] = arW.alloc("ones512", [1, 512])
    B["dt_tm"] = arW.alloc("dt_tm", [128, 64], BF16)
    B["decay_tm"] = arW.alloc("decay_tm", [128, 64], BF16)
    B["B_tm"] = arW.alloc("B_tm", [128, 128], BF16)
    B["Gm"] = arW.alloc("Gm", [128, 128], BF16)
    B["cdb"] = arW.alloc("cdb", [128, 64])
    B["cd_row"] = arW.alloc("cd_row", [1, 64])
    B["xdt"] = arW.alloc("xdt", [128, 4096], BF16)
    B["xdtd"] = arW.alloc("xdtd", [128, 4096], BF16)
    B["tscr"] = arW.alloc("tscr", [128, 512])
    B["stageL"] = [arW.alloc(f"stageL{i}", [2, 512]) for i in range(2)]
    B["stageR"] = [arW.alloc(f"stageR{i}", [2, 512]) for i in range(2)]
    B["eacsrow"] = [arW.alloc(f"eacsrow{i}", [1, 512], BF16) for i in range(2)]
    B["dmin"] = [arW.alloc(f"dmin{i}", [128, 512]) for i in range(2)]
    B["ST"] = [arW.alloc(f"ST{i}", [128, 512], BF16) for i in range(4)]
    B["CE"] = [arW.alloc(f"CE{i}", [128, 512], BF16) for i in range(4)]
    assert arW.off <= AR_W_SZ

    arW.reset()
    B["wslab@m1"] = [arW.alloc(f"wslab{i}", [128, 2048], BF16) for i in range(2)]
    B["dtraw"] = arW.alloc("dtraw", [64, SEGL])
    B["dts1"] = arW.alloc("dts1", [64, SEGL])
    B["dts2"] = B["dtraw"]
    B["convtmp"] = arW.alloc("convtmp", [128, SEGL])
    B["xbcraw"] = [arW.alloc(f"xbcraw{i}", [128, 1028]) for i in range(2)]
    B["convo@m0"] = [arW.alloc(f"convo{i}", [128, SEGL]) for i in range(2)]
    B["sqb@m0"] = arW.alloc("sqb0", [128, SEGL], BF16)
    B["sz@m0"] = arW.alloc("sz0", [128, SEGL])
    B["stat1@m0"] = arW.alloc("stat10", [128, SEGL])
    B["stat2@m0"] = arW.alloc("stat20", [128, SEGL])
    assert arW.off <= AR_W_SZ, f"W2 {arW.off}"

    arW.reset()
    B["wslab@m3"] = [arW.alloc(f"wslab3_{i}", [128, 2048], BF16) for i in range(2)]
    B["sz@m3"] = arW.alloc("sz3", [128, SEGL])
    assert arW.off <= AR_W_SZ - 6272
    arW.off = AR_W_SZ - 6272
    B["sqb@m3"] = arW.alloc("sqb3", [128, SEGL], BF16)
    B["stat2@m3"] = arW.alloc("stat23", [128, SEGL])

    arW.reset()
    B["woslab"] = [arW.alloc(f"woslab{i}", [128, 4096], BF16) for i in range(2)]
    B["uscr"] = [arW.alloc(f"uscr{i}", [128, 1024]) for i in range(2)]
    B["tmpX"] = arW.alloc("tmpX", [128, 1024])
    assert arW.off <= AR_W_SZ - 4160

    arW.reset()
    B["sqb@m5"] = arW.alloc("sqb5", [128, SEGL], BF16)
    B["sz@m5"] = arW.alloc("sz5", [128, SEGL])
    B["stat1@m5"] = arW.alloc("stat15", [128, SEGL])
    B["stat2@m5"] = arW.alloc("stat25", [128, SEGL])
    B["convo@m5"] = [arW.alloc(f"convo5_{i}", [128, SEGL]) for i in range(2)]
    assert arW.off <= AR_W_SZ

    arW.reset()
    _w1 = arW.alloc("w1slab", [128, 2048], F32)
    B["w1slab"] = [_w1, _w1]
    _w2 = arW.alloc("w2slab", [128, 2048], F32)
    B["w2slab"] = [_w2, _w2]
    B["q1"] = [arW.alloc(f"q1_{i}", [128, 2048], BF16) for i in range(2)]
    B["q2"] = [arW.alloc(f"q2_{i}", [128, 2048], BF16) for i in range(2)]
    B["qtmp"] = arW.alloc("qtmp", [128, 2048], BF16)
    B["sig"] = arW.alloc("sig", [128, SEGL])
    B["outstg"] = [arW.alloc(f"outstg{i}", [128, 512]) for i in range(2)]
    B["w3slab"] = B["w2slab"]
    assert arW.off <= AR_W_SZ - 6272, f"W6 {arW.off}"

    B["p_xT"] = nc.alloc_psum_tensor("p_xT", [128, 1024], BF16)
    p_ip = [nc.alloc_psum_tensor(f"p_ip{i}", [128, 1024], F32) for i in range(3)]
    B["p_ip"] = p_ip
    B["p_Y"] = [p_ip[0][:, 0:512], p_ip[0][:, 512:1024]]
    B["p_diff"] = [p_ip[1][:, 0:512], p_ip[1][:, 512:1024]]
    B["p_sh"] = p_ip[2][:, 0:512]
    B["p_small"] = p_ip[2][:, 512:1024]
    B["p_w3"] = nc.alloc_psum_tensor("p_w3", [128, 512], F32)

    P = Pipe(nc)
    for s in (sV, sA, "sT", "sGc", "stg0", "stg1", "cdr"):
        P.new_sem(s)

    ld = []
    for dst, src in ((B["idn"], d_idn), (B["tril"], d_tril), (B["ones1_f32"], d_o1f),
                     (B["ones1_bf"], d_o1b), (B["o128b"], d_o128b), (B["o128f"], d_o128f),
                     (B["o128f1"], d_o128f1),
                     (B["nw"], d_nw), (B["cw"], d_cw), (B["cb"], d_cb),
                     (B["dtbias"], d_dtb), (B["A_col"], d_alog), (B["Ddiag32"], d_dd)):
        ld.append(P.dma(S, dst[:], src, sem="dini"))
    ev_ini = [ld[-1]]
    e = P.add(A, lambda a: a.activation(B["A_col"][:, 0:1], B["A_col"][:, 0:1], AF.Exp),
              waits=ev_ini)
    P.add(V, lambda v: v.tensor_scalar_mul(B["A_col"][:, 0:1], B["A_col"][:, 0:1], -1.0), waits=[e])
    ev_stash0 = P.add(V, lambda v: v.memset(B["stash"][:, :, :], 0.0))

    sst = ssd_init(P, B, ev_consts=ev_ini)

    pfree = {"p_Y0": None, "p_Y1": None, "p_diff0": None, "p_diff1": None,
             "p_sh": None, "p_small": None, "p_w3": None}

    def pf_pair(names):
        return [pfree[n] for n in names]

    ev_gamma_done = [None]
    qcache_done = {}
    wfree3 = [None, None, None]
    out_dma_evs = []

    gamma_state = {"col": 0, "gfree": [None, None], "last": None}

    def emit_gamma_batch(n):
        srcs = []
        for dw in (d_w1, d_w2, d_w3):
            for hc in range(N_HCH):
                srcs.append((dw, hc))
        while n > 0 and gamma_state["col"] < len(srcs):
            col = gamma_state["col"]
            dw, hc = srcs[col]
            par = col % 2
            if dw is d_w3:
                s_ = dw[hc, :, :]
            else:
                s_ = dw[hc, :, :, :].rearrange("p a b -> p (a b)")
            gf = gamma_state["gfree"][par]
            evd = P.dma(S, B["gslab"][par][:, :], s_,
                        waits=list(gf) if isinstance(gf, tuple) else [gf],
                        sem=f"dg{par}")
            ev_ra = P.add(A, lambda a, par=par, col=col: a.activation(
                B["gslab"][par][:, 0:1024], B["gslab"][par][:, 0:1024], AF.Abs,
                accum_out=B["gscr"][:, 2 * col:2 * col + 1]), waits=[evd])
            ev_rb = P.add(A, lambda a, par=par, col=col: a.activation(
                B["gslab"][par][:, 1024:2048], B["gslab"][par][:, 1024:2048], AF.Abs,
                accum_out=B["gscr"][:, 2 * col + 1:2 * col + 2]))
            gamma_state["gfree"][par] = (ev_ra, ev_rb)
            gamma_state["last"] = (ev_ra, ev_rb)
            gamma_state["col"] = col + 1
            n -= 1

    def emit_gamma_fold():
        last_red = list(gamma_state["last"])
        assert gamma_state["col"] == 3 * N_HCH
        evs = []
        for idx in range(3):
            evs.append(P.add(V, lambda v, idx=idx: v.reduce_sum(
                B["g3"][:, idx:idx + 1], B["gscr"][:, 88 * idx:88 * idx + 88], AX.X),
                waits=last_red))
        ev_z = P.add(V, lambda v: v.memset(B["g3"][:, 3:4], 0.0))
        ev_gmm = P.add(T, lambda t: t.matmul(B["p_w3"][0:1, 0:4], B["o128f1"][:, 0:1], B["g3"][:, 0:4],
                                             start=True, stop=True),
                       waits=[ev_z] + evs + [pfree["p_w3"]])
        ev_gr = P.add(V, lambda v: v.tensor_copy(B["grow"][0:1, 0:4], B["p_w3"][0:1, 0:4]),
                      waits=[ev_gmm])
        ev_gb = P.add(T, lambda t: t.matmul(B["p_w3"][:, 4:8], B["ones1_f32"][0:1, :], B["grow"][0:1, 0:4],
                                            start=True, stop=True), waits=[ev_gr])
        ev_tpp = P.add(V, lambda v: v.tensor_scalar(B["tpp"][:, 0:4], B["p_w3"][:, 4:8],
                                                    0.5 / NW1, 0.5 * GAMMA_EPS, ALU.mult, ALU.add),
                       waits=[ev_gb])
        ev_tn = P.add(V, lambda v: v.tensor_scalar_mul(B["tneg"][:, 0:4], B["tpp"][:, 0:4], -1.0))
        pfree["p_w3"] = ev_tpp
        ev_gamma_done[0] = ev_tn

    for seg in range(NSEG):
        l0 = seg * SEGL
        wu = (P.barrier() + list(qcache_done.values()) + out_dma_evs[-2:]) if seg else [sst["ST_CE_init"]]
        evu = []
        for half in range(2):
            src = d_u[1024 * half:1024 * (half + 1), l0:l0 + SEGL]
            src = src.rearrange("(k p) l -> p k l", p=128)
            evu.append(P.dma(S, B["u32"][:, 8 * half:8 * (half + 1), :], src,
                             waits=wu, sem="du"))
        ev_smm = ev_umm = None
        smm_by_k = []
        for k in range(16):
            sqbuf = B["sqb@m0"] if k % 2 == 0 else B["sqbX"]
            evq = P.add(A, lambda a, k=k, sqbuf=sqbuf: a.activation(sqbuf[:, :], B["u32"][:, k, :], AF.Square),
                        waits=evu + (wu if k < 2 else []) + ([smm_by_k[k - 2]] if k >= 2 else []))
            for lh in range(2):
                ev_smm = P.add(T, lambda t, k=k, lh=lh, sqbuf=sqbuf: t.matmul(
                    B["p_diff"][lh][:, :], B["o128b"][:, :], sqbuf[:, 512 * lh:512 * lh + 512],
                    start=(k == 0), stop=(k == 15)),
                    waits=[evq] + ([pfree[f"p_diff{lh}"]] if k == 0 else []))
            smm_by_k.append(ev_smm)
        for k in range(16):
            for lh in range(2):
                ev_umm = P.add(T, lambda t, k=k, lh=lh: t.matmul(
                    B["p_Y"][lh][:, :], B["o128f"][:, :], B["u32"][:, k, 512 * lh:512 * lh + 512],
                    start=(k == 0), stop=(k == 15)),
                    waits=(evu + [pfree["p_Y0"], pfree["p_Y1"]]) if k == 0 else ())
        P.add(A, lambda a: a.mul(B["stat1@m0"][:, :], B["p_ip"][0][:, :], 1.0 / DM),
              waits=[ev_umm, ev_smm] + (wu if seg else []))
        ev_v2 = P.add(A, lambda a: a.mul(B["stat2@m0"][:, :], B["p_ip"][1][:, :], 1.0 / DM))
        pfree["p_Y0"] = pfree["p_Y1"] = pfree["p_diff0"] = pfree["p_diff1"] = ev_v2
        evm = P.add(A, lambda a: a.activation(B["sz@m0"][:, :], B["stat1@m0"][:, :], AF.Square),
                    waits=[ev_v2])
        evv = P.add(V, lambda v: v.scalar_tensor_tensor(B["stat2@m0"][:, :], B["stat2@m0"][:, :], float(EPS_LN), B["sz@m0"][:, :], ALU.add, ALU.subtract),
                    waits=[evm])
        evl = P.add(A, lambda a: a.activation(B["stat2@m0"][:, :], B["stat2@m0"][:, :], AF.Ln),
                    waits=[evv])
        ev_rstd = P.add(A, lambda a: a.activation(B["stat2@m0"][:, :], B["stat2@m0"][:, :], AF.Exp, scale=-0.5))
        ev_h1k = []
        for k in range(16):
            evs_ = P.add(V, lambda v, k=k: v.tensor_tensor(B["convo@m0"][k % 2][:, :], B["u32"][:, k, :], B["stat1@m0"][:, :], ALU.subtract),
                         waits=[ev_rstd])
            ev_h1k.append(P.add(V, lambda v, k=k: v.tensor_tensor(B["h1"][:, k, :], B["convo@m0"][k % 2][:, :], B["stat2@m0"][:, :], ALU.mult)))
        ev_h1done = [ev_h1k[-1]]


        wfree = wfree3 if seg else [None, None, None]
        slabs3 = [B["wslab@m1"][0], B["wslab@m1"][1], B["wslab3"]]
        psum3 = [(B["p_Y"][0], B["p_Y"][1]), (B["p_diff"][0], B["p_diff"][1]),
                 (B["p_sh"], B["p_small"])]
        slot_free = {0: [pfree["p_Y0"], pfree["p_Y1"]],
                     1: [pfree["p_diff0"], pfree["p_diff1"]],
                     2: [pfree["p_sh"], pfree["p_small"]]}
        xbf = [None, None]
        cvf = [None, None]
        ev_x_ready = []
        ev_dt_ready = []
        for ci, ec in enumerate(range(32, 67)):
            par = ci % 3
            p3 = ci % 3
            if seg == 0:
                emit_gamma_batch(2)
            src = d_win[ec, :, :, :].rearrange("p a b -> p (a b)")
            evd = P.dma(G, slabs3[par][:, :], src, waits=[wfree[par]] + (P.barrier() if ci < 3 and seg else []),
                        sem=f"dwin{par}")
            pb = psum3[p3]
            ev_mm = None
            for lh in range(2):
                for k in range(16):
                    w = [ev_h1k[k]]
                    if lh == 0 and k == 0:
                        w = w + [evd] + slot_free[p3]
                    ev_mm = P.add(T, lambda t, sl3=slabs3[par], lh=lh, k=k, pb=pb: t.matmul(
                        pb[lh][:, :], sl3[:, 128 * k:128 * k + 128],
                        B["h1"][:, k, 512 * lh:512 * lh + 512],
                        start=(k == 0), stop=(k == 15)), waits=w)
            wfree[par] = ev_mm
            if ec < 66:
                m = ec - 32
                par = ci % 2
                pfull = p_ip[p3]
                co = B["convo@m0"][par]
                e1_ = P.add(V, lambda v, co=co, m=m: v.tensor_scalar(
                    co[:, 0:3], B["stash"][:, m, 0:3], B["cw"][:, m, 0:1], None, ALU.mult),
                    waits=[ev_mm, ev_stash0, cvf[par]])
                e2_ = P.add(V, lambda v, co=co, m=m, pfull=pfull: v.tensor_scalar(
                    co[:, 3:1024], pfull[:, 0:1021], B["cw"][:, m, 0:1], None, ALU.mult))
                P.add(V, lambda v, co=co, m=m: v.scalar_tensor_tensor(
                    co[:, 0:2], B["stash"][:, m, 1:3], B["cw"][:, m, 1:2], co[:, 0:2],
                    ALU.mult, ALU.add))
                P.add(V, lambda v, co=co, m=m, pfull=pfull: v.scalar_tensor_tensor(
                    co[:, 2:1024], pfull[:, 0:1022], B["cw"][:, m, 1:2], co[:, 2:1024],
                    ALU.mult, ALU.add))
                P.add(V, lambda v, co=co, m=m: v.scalar_tensor_tensor(
                    co[:, 0:1], B["stash"][:, m, 2:3], B["cw"][:, m, 2:3], co[:, 0:1],
                    ALU.mult, ALU.add))
                P.add(V, lambda v, co=co, m=m, pfull=pfull: v.scalar_tensor_tensor(
                    co[:, 1:1024], pfull[:, 0:1023], B["cw"][:, m, 2:3], co[:, 1:1024],
                    ALU.mult, ALU.add))
                evc = P.add(V, lambda v, co=co, m=m, pfull=pfull: v.scalar_tensor_tensor(
                    co[:, 0:1024], pfull[:, 0:1024], B["cw"][:, m, 3:4], co[:, 0:1024],
                    ALU.mult, ALU.add))
                ev_sth = P.add(V, lambda v, m=m, pfull=pfull: v.tensor_copy(
                    B["stash"][:, m, :], pfull[:, 1021:1024]))
                slot_free[p3] = [evc, ev_sth]
                if m < 32:
                    dst = B["xy"][:, m, :]
                elif m == 32:
                    dst = B["B_fm"][:, :]
                else:
                    dst = B["C_fm"][:, :]
                evx = P.add(A, lambda a, dst=dst, m=m, co=co: a.activation(
                    dst, co[:, :], AF.Silu, bias=B["cb"][:, m:m + 1]),
                    waits=[evc])
                cvf[par] = evx
                ev_x_ready = [evx]
            else:
                eva = P.add(A, lambda a, pb=pb: a.copy(B["dtraw"][0:64, 0:512], pb[0][0:64, :]),
                            waits=[ev_mm])
                eva2 = P.add(A, lambda a, pb=pb: a.copy(B["dtraw"][0:64, 512:1024], pb[1][0:64, :]))
                slot_free[p3] = [eva2]
                ev1 = P.add(V, lambda v: v.tensor_scalar(B["dts1"][0:64, :], B["dtraw"][0:64, :],
                                                         B["dtbias"][:, 0:1], None, ALU.add),
                            waits=[eva2])
                P.add(A, lambda a: a.activation(B["dts2"][0:64, :], B["dts1"][0:64, :], AF.Abs),
                      waits=[ev1])
                P.add(A, lambda a: a.activation(B["dts2"][0:64, :], B["dts2"][0:64, :], AF.Exp, scale=-1.0))
                ev4 = P.add(A, lambda a: a.activation(B["dts2"][0:64, :], B["dts2"][0:64, :], AF.Ln, bias=1.0))
                P.add(V, lambda v: v.tensor_scalar_max(B["dts1"][0:64, :], B["dts1"][0:64, :], 0.0),
                      waits=[ev4])
                ev6 = P.add(V, lambda v: v.tensor_tensor(B["dt_hm"][0:64, :], B["dts1"][0:64, :], B["dts2"][0:64, :], ALU.add))
                ev_dt_ready = [ev6]

        m1bar = P.barrier()
        P.add(V, lambda v: v.memset(B["z64"][:, :], 0.0), waits=m1bar)
        sevs = []
        for i2 in range(2):
            sevs.append(P.add(V, lambda v, i2=i2: v.memset(B["stageL"][i2][0:1, :], 1.0)))
        ev_o2 = P.add(V, lambda v: v.memset(B["ones512"][0:1, :], 1.0))
        for i2 in range(2):
            sevs.append(P.dma(G, B["stageR"][i2][1:2, :], B["ones512"][0:1, :],
                              waits=[ev_o2, sst["stage_free"][i2]], sem=f"stg{i2}"))
        sst["stage_ok"] = sevs + [P.last(V)]
        sst["stage_free"] = [sevs[-2], sevs[-1]]
        sst["pY_free"] = [slot_free[0][0], slot_free[0][0]]
        sst["pdiff_free"] = [slot_free[1][0], slot_free[1][0]]
        sst["psh_free"] = slot_free[2][0]
        sst["psmall_free"] = slot_free[2][0]
        r = None
        sst["pw3_free"] = pfree["p_w3"]
        for c in range(8):
            if seg == 0:
                emit_gamma_batch(2)
            r = emit_ssd_chunk(P, B, c, {"x": ev_x_ready, "BC": ev_x_ready,
                                         "dt": ev_dt_ready}, sst)
        pfree["p_w3"] = sst["pw3_free"]
        ev_ssd_done = r["y"] + r["states"]

        m2bar = P.barrier()
        slot_free = {0: [sst["pY_free"][0]], 1: [sst["pdiff_free"][0]]}
        ev_rms_mm = None
        ev_yg_prev = None
        ev_rms_prev = None
        ev_ygs = []
        ev_sqs = []
        slabs3b = [B["wslab@m3"][0], B["wslab@m3"][1], B["wslab3"]]
        for ec in range(32):
            par = ec % 3
            par2 = ec % 2
            if seg == 0:
                emit_gamma_batch(2)
            src = d_win[ec, :, :, :].rearrange("p a b -> p (a b)")
            evd = P.dma(G, slabs3b[par][:, :], src,
                        waits=[wfree[par]] + (m2bar if ec < 3 else []), sem=f"dwin{par}")
            pb = B["p_Y"] if par2 == 0 else B["p_diff"]
            ev_mm = None
            for lh in range(2):
                for k in range(16):
                    w = []
                    if lh == 0 and k == 0:
                        w = [evd] + slot_free[par2] + (ev_ssd_done if ec < 2 else [])
                    ev_mm = P.add(T, lambda t, sl3=slabs3b[par], lh=lh, k=k, pb=pb: t.matmul(
                        pb[lh][:, :], sl3[:, 128 * k:128 * k + 128],
                        B["h1"][:, k, 512 * lh:512 * lh + 512],
                        start=(k == 0), stop=(k == 15)), waits=w)
            wfree[par] = ev_mm
            pfull = B["p_ip"][par2]
            evs2 = P.add(A, lambda a, pfull=pfull: a.activation(B["sz@m3"][:, :], pfull[:, :], AF.Silu),
                         waits=[ev_mm, ev_yg_prev])
            slot_free[par2] = [evs2]
            evyg = P.add(V, lambda v, ec=ec: v.tensor_tensor(B["xy"][:, ec, :], B["xy"][:, ec, :], B["sz@m3"][:, :], ALU.mult),
                         waits=[evs2] + (ev_ssd_done if ec < 2 else []))
            ev_yg_prev = evyg
            ev_ygs.append(evyg)
            if ec >= 2:
                ecl = ec - 2
                sqb_l = B["sqb@m3"] if ecl % 2 == 0 else B["sqb@m0"]
                for lh in range(2):
                    ev_rms_mm = P.add(T, lambda t, ecl=ecl, lh=lh, sqb_l=sqb_l: t.matmul(
                        (B["p_sh"] if lh == 0 else B["p_small"])[:, :], B["o128b"][:, :],
                        sqb_l[:, 512 * lh:512 * lh + 512],
                        start=(ecl == 0), stop=False),
                        waits=[ev_sqs[ecl]] + ([sst["psh_free"], sst["psmall_free"]] if ecl == 0 else []))
                ev_rms_prev = ev_rms_mm
            sqbuf = B["sqb@m3"] if ec % 2 == 0 else B["sqb@m0"]
            ev_sqs.append(P.add(A, lambda a, ec=ec, sqbuf=sqbuf: a.activation(
                sqbuf[:, :], B["xy"][:, ec, :], AF.Square),
                waits=[evyg, ev_rms_prev]))
        m3bar = P.barrier()
        for ecl in (30, 31):
            sqb_l = B["sqb@m3"] if ecl % 2 == 0 else B["sqb@m0"]
            for lh in range(2):
                ev_rms_mm = P.add(T, lambda t, ecl=ecl, lh=lh, sqb_l=sqb_l: t.matmul(
                    (B["p_sh"] if lh == 0 else B["p_small"])[:, :], B["o128b"][:, :],
                    sqb_l[:, 512 * lh:512 * lh + 512],
                    start=False, stop=(ecl == 31)),
                    waits=[ev_sqs[ecl]])
        ev1 = P.add(V, lambda v: v.tensor_scalar(B["stat2@m3"][:, 0:512], B["p_sh"][:, :], 1.0 / DI, float(EPS_RMS), ALU.mult, ALU.add),
                    waits=[ev_rms_mm])
        ev2 = P.add(V, lambda v: v.tensor_scalar(B["stat2@m3"][:, 512:1024], B["p_small"][:, :], 1.0 / DI, float(EPS_RMS), ALU.mult, ALU.add))
        pfree["p_sh"] = ev2
        pfree["p_small"] = ev2
        P.add(A, lambda a: a.activation(B["stat2@m3"][:, :], B["stat2@m3"][:, :], AF.Ln), waits=[ev2])
        ev_rms = P.add(A, lambda a: a.activation(B["stat2@m3"][:, :], B["stat2@m3"][:, :], AF.Exp, scale=-0.5))
        ev_yn_done = [ev_ygs[-1]]

        wfree3 = list(wfree)
        if seg == 0:
            emit_gamma_fold()

        wofree = [None, None]
        ufree = [None, None]
        ev_x1k_m4 = []
        slot_free = {0: [slot_free[0][0]], 1: [slot_free[1][0]],
                     2: [pfree["p_sh"], pfree["p_small"]]}
        ev_x1w = None
        for dc in range(16):
            par = dc % 2
            src = d_wout[dc, :, :, :].rearrange("p a b -> p (a b)")
            evd = P.dma(G, B["woslab"][par][:, :], src,
                        waits=[wofree[par]] + (m3bar if dc < 2 else []), sem=f"dwo{par}")
            nw_ap = B["nw"][:, :].unsqueeze(2).broadcast_to([128, 32, 128])
            evd = P.add(V, lambda v, par=par, nw_ap=nw_ap: v.tensor_tensor(
                B["woslab"][par][:, :].rearrange("p (a b) -> p a b", b=128), B["woslab"][par][:, :].rearrange("p (a b) -> p a b", b=128),
                nw_ap, ALU.mult), waits=[evd])
            uev = P.dma(S, B["uscr"][par][:, :], d_u[128 * dc:128 * dc + 128, l0:l0 + SEGL],
                        waits=[ufree[par]] + (m3bar if dc < 2 else [])
                        + ([ev_gamma_done[0]] if dc < 2 and seg == 0 else []),
                        sem=f"dus{par}")
            p3w = dc % 3
            pfullw = B["p_ip"][p3w]
            pb = [pfullw[:, 0:512], pfullw[:, 512:1024]]
            ev_mm = None
            for lh in range(2):
                for ec in range(32):
                    w = []
                    if lh == 0 and ec == 0:
                        w = [evd] + slot_free[p3w] + ev_yn_done
                    ev_mm = P.add(T, lambda t, par=par, lh=lh, ec=ec, pb=pb: t.matmul(
                        pb[lh][:, :], B["woslab"][par][:, 128 * ec:128 * ec + 128],
                        B["xy"][:, ec, 512 * lh:512 * lh + 512],
                        start=(ec == 0), stop=(ec == 31)), waits=w)
            wofree[par] = ev_mm
            et1 = P.add(V, lambda v, pfullw=pfullw: v.tensor_tensor(
                B["tmpX"][:, :], pfullw[:, :], B["stat2@m3"][:, :], ALU.mult),
                waits=[ev_mm, ev_rms] + ([ev_gamma_done[0]] if dc < 2 and seg == 0 else []))
            e2 = P.add(V, lambda v, dc=dc, par=par: v.tensor_tensor(
                B["X1"][:, dc, :], B["tmpX"][:, :], B["uscr"][par][:, :], ALU.add),
                waits=[uev])
            slot_free[p3w] = [e2]
            ufree[par] = e2
            ev_x1k_m4.append(e2)
            ev_x1w = e2
        pfree["p_Y0"] = pfree["p_Y1"] = slot_free[0][0]
        pfree["p_diff0"] = pfree["p_diff1"] = slot_free[1][0]
        pfree["p_sh"] = pfree["p_small"] = slot_free[2][0]

        ev_rms_mm_guard = [ev_rms_mm] + P.barrier()
        ev_smm = ev_umm = None
        smm_by_k = []
        for k in range(16):
            sqbuf = B["sqb@m5"] if k % 2 == 0 else B["sqb@m0"]
            evq = P.add(A, lambda a, k=k, sqbuf=sqbuf: a.activation(sqbuf[:, :], B["X1"][:, k, :], AF.Square),
                        waits=[ev_x1k_m4[k]] + ([smm_by_k[k - 2]] if k >= 2 else [])
                        + (ev_rms_mm_guard if k < 2 else []))
            for lh in range(2):
                ev_smm = P.add(T, lambda t, k=k, lh=lh, sqbuf=sqbuf: t.matmul(
                    B["p_diff"][lh][:, :], B["o128b"][:, :], sqbuf[:, 512 * lh:512 * lh + 512],
                    start=(k == 0), stop=(k == 15)),
                    waits=[evq] + ([pfree[f"p_diff{lh}"]] if k == 0 else []))
            smm_by_k.append(ev_smm)
        for k in range(16):
            for lh in range(2):
                ev_umm = P.add(T, lambda t, k=k, lh=lh: t.matmul(
                    B["p_Y"][lh][:, :], B["o128b"][:, :], B["X1"][:, k, 512 * lh:512 * lh + 512],
                    start=(k == 0), stop=(k == 15)),
                    waits=[ev_x1k_m4[k]] + ([pfree["p_Y0"], pfree["p_Y1"]] if k == 0 else []))
        P.add(A, lambda a: a.mul(B["stat1@m5"][:, :], B["p_ip"][0][:, :], 1.0 / DM),
              waits=[ev_umm, ev_smm])
        ev_v2 = P.add(A, lambda a: a.mul(B["stat2@m5"][:, :], B["p_ip"][1][:, :], 1.0 / DM))
        pfree["p_Y0"] = pfree["p_Y1"] = pfree["p_diff0"] = pfree["p_diff1"] = ev_v2
        evm = P.add(A, lambda a: a.activation(B["sz@m5"][:, :], B["stat1@m5"][:, :], AF.Square),
                    waits=[ev_v2])
        evv = P.add(V, lambda v: v.scalar_tensor_tensor(B["stat2@m5"][:, :], B["stat2@m5"][:, :], float(EPS_LN), B["sz@m5"][:, :], ALU.add, ALU.subtract),
                    waits=[evm])
        P.add(A, lambda a: a.activation(B["stat2@m5"][:, :], B["stat2@m5"][:, :], AF.Ln),
              waits=[evv])
        ev_rstd = P.add(A, lambda a: a.activation(B["stat2@m5"][:, :], B["stat2@m5"][:, :], AF.Exp, scale=-0.5))
        wh2 = list(wfree) + [wofree[0], wofree[1]]
        ev_h2k = []
        for k in range(16):
            evs_ = P.add(V, lambda v, k=k: v.tensor_tensor(B["convo@m5"][k % 2][:, :], B["X1"][:, k, :], B["stat1@m5"][:, :], ALU.subtract),
                         waits=[ev_rstd] + (wh2 if k == 0 else []))
            ev_h2k.append(P.add(V, lambda v, k=k: v.tensor_tensor(B["h1"][:, k, :], B["convo@m5"][k % 2][:, :], B["stat2@m5"][:, :], ALU.mult)))
        ev_h2done = [ev_h2k[-1]]

        m5bar = P.barrier()
        rounds = [list(range(i, min(i + FFN_R, N_HCH))) for i in range(0, N_HCH, FFN_R)]
        w1f = [None, None]
        w2f = [None, None]
        w3f = [None, None]
        q1f = [None, None]
        q2f = [None, None]
        q3lf = [None] * FFN_R
        hidf = [None] * FFN_R
        x1pf = {0: pfree["p_Y0"], 1: pfree["p_Y1"]}
        x2pf = {0: pfree["p_diff0"], 1: pfree["p_diff1"]}
        w3pf = [pfree["p_w3"], pfree["p_sh"]]
        sigf = None
        qtmpf = [None]
        outf = [None, None]
        mam_bar = P.barrier()
        ev_last_w3 = None
        for ri, rnd in enumerate(rounds):
            hid_ready = [None] * len(rnd)
            q3_ready = [None] * len(rnd)
            for li, hc in enumerate(rnd):
                par = li % 2

                def quant(dslab, wslabs, qbufs, tcol, wf, qf, extra, dcache):
                    if seg != 0:
                        dq = P.dma(S, qbufs[par][:, :], dcache[hc, :, :],
                                   waits=[qf[0], qf[1], qcache_done.get(tcol)] + extra,
                                   sem=f"dq{tcol}{par}")
                        wf[par] = dq
                        return dq
                    d_ = P.dma(S, wslabs[par][:, :], dslab,
                               waits=[wf[0], wf[1]] + extra, sem=f"dwf{tcol}{par}")
                    if tcol == 1:
                        eb = P.add(V, lambda v, q=qbufs[par], w=wslabs[par], tc=tcol: v.tensor_scalar(
                            q[:, :], w[:, :], B["tneg"][:, tc:tc + 1], None, ALU.is_lt),
                            waits=[d_, qf[0], qf[1], qcache_done.get(tcol), ev_gamma_done[0]])
                        ea2 = P.add(V, lambda v, w=wslabs[par], tc=tcol: v.tensor_scalar(
                            B["qtmp"][:, :], w[:, :], B["tpp"][:, tc:tc + 1], None, ALU.is_gt),
                            waits=[qtmpf[0]])
                        eq = P.add(V, lambda v, q=qbufs[par]: v.tensor_tensor(
                            q[:, :], B["qtmp"][:, :], q[:, :], ALU.subtract))
                    else:
                        ea = P.add(A, lambda a, w=wslabs[par], tc=tcol: a.activation(
                            B["qtmp"][:, :], w[:, :], AF.Sign, bias=B["tneg"][:, tc:tc + 1]),
                            waits=[d_, qtmpf[0], ev_gamma_done[0]])
                        eb = P.add(A, lambda a, q=qbufs[par], w=wslabs[par], tc=tcol: a.activation(
                            q[:, :], w[:, :], AF.Sign, bias=B["tpp"][:, tc:tc + 1]),
                            waits=[qf[0], qf[1], qcache_done.get(tcol)])
                        eq = P.add(V, lambda v, q=qbufs[par]: v.tensor_tensor(
                            q[:, :], q[:, :], B["qtmp"][:, :], ALU.add),
                            waits=[ea, eb])
                    qtmpf[0] = eq
                    wf[par] = eq
                    qcache_done[tcol] = P.dma(G, dcache[hc, :, :], qbufs[par][:, :],
                                              waits=[eq], sem=f"dqc{tcol}")
                    return eq
                eq1 = quant(d_w1[hc, :, :, :].rearrange("p a b -> p (a b)"), B["w1slab"],
                            B["q1"], 0, w1f, q1f, (m5bar if ri == 0 and li < 2 else []),
                            d_q1s)
                eq2 = quant(d_w2[hc, :, :, :].rearrange("p a b -> p (a b)"), B["w2slab"],
                            B["q2"], 1, w2f, q2f, [w3f[0], w3f[1]] + (m5bar if ri == 0 and li < 2 else []),
                            d_q2s)
                ex1 = ex2 = None
                for lh in range(2):
                    for k in range(16):
                        w = [ev_h2k[k]]
                        if lh == 0 and k == 0:
                            w = w + [eq1, x1pf[0], x1pf[1]]
                        ex1 = P.add(T, lambda t, par=par, lh=lh, k=k: t.matmul(
                            B["p_Y"][lh][:, :], B["q1"][par][:, 128 * k:128 * k + 128],
                            B["h1"][:, k, 512 * lh:512 * lh + 512],
                            start=(k == 0), stop=(k == 15)), waits=w)
                for lh in range(2):
                    for k in range(16):
                        w = []
                        if lh == 0 and k == 0:
                            w = [eq2, x2pf[0], x2pf[1]]
                        ex2 = P.add(T, lambda t, par=par, lh=lh, k=k: t.matmul(
                            B["p_diff"][lh][:, :], B["q2"][par][:, 128 * k:128 * k + 128],
                            B["h1"][:, k, 512 * lh:512 * lh + 512],
                            start=(k == 0), stop=(k == 15)), waits=w)
                q1f[0] = q1f[1] = ex1
                q2f[0] = q2f[1] = ex2
                es2 = P.add(A, lambda a: a.activation(B["sig"][:, :], B["p_ip"][0][:, :], AF.Sigmoid, scale=0.5),
                            waits=[ex1, sigf])
                x1pf = {0: es2, 1: es2}
                eh2 = P.add(V, lambda v, li=li: v.tensor_tensor(
                    B["hid"][:, li, :], B["sig"][:, :], B["p_ip"][1][:, :], ALU.mult),
                    waits=[es2, ex2, hidf[li]] + (mam_bar if ri == 0 and li < 2 else []))
                x2pf = {0: eh2, 1: eh2}
                sigf = eh2
                hid_ready[li] = eh2
                if seg == 0:
                    d3 = P.dma(S, B["w3slab"][par][:, :], d_w3[hc, :, :],
                               waits=[w3f[0], w3f[1], eq2], sem=f"dw3{par}")
                    ec_ = P.add(A, lambda a, par=par: a.activation(
                        B["qtmp"][:, :], B["w3slab"][par][:, :], AF.Sign, bias=B["tneg"][:, 2:3]),
                        waits=[d3, qtmpf[0]])
                    eb3 = P.add(A, lambda a, par=par, li=li: a.activation(
                        B["q3l"][:, li, :], B["w3slab"][par][:, :], AF.Sign, bias=B["tpp"][:, 2:3]),
                        waits=[q3lf[li], qcache_done.get(2)] + (mam_bar if ri == 0 and li < 2 else []))
                    eq3 = P.add(V, lambda v, li=li: v.tensor_tensor(
                        B["q3l"][:, li, :], B["q3l"][:, li, :], B["qtmp"][:, :], ALU.add),
                        waits=[ec_, eb3])
                    qtmpf[0] = eq3
                    w3f[par] = eq3
                    qcache_done[2] = P.dma(G, d_q3s[hc, :, :], B["q3l"][:, li, :],
                                           waits=[eq3], sem="dqc2")
                else:
                    eq3 = P.dma(S, B["q3l"][:, li, :], d_q3s[hc, :, :],
                                waits=[q3lf[li], qcache_done.get(2)] + (mam_bar if ri == 0 and li < 2 else []),
                                sem=f"dq3l{par}")
                q3_ready[li] = eq3
            for dc in range(16):
                for lh in range(2):
                    pwb = B["p_w3"] if (dc * 2 + lh) % 2 == 0 else B["p_sh"]
                    pw_i = (dc * 2 + lh) % 2
                    ev_mm = None
                    for li in range(len(rnd)):
                        w = []
                        if li == 0:
                            w = [w3pf[pw_i], q3_ready[-1], hid_ready[-1]] + (
                                [q3_ready[-2]] if len(rnd) > 1 else [])
                        ev_mm = P.add(T, lambda t, dc=dc, lh=lh, li=li, pwb=pwb,
                                      st_=(li == 0), sp_=(li == len(rnd) - 1): t.matmul(
                            pwb[:, :], B["q3l"][:, li, 128 * dc:128 * dc + 128],
                            B["hid"][:, li, 512 * lh:512 * lh + 512],
                            start=st_, stop=sp_), waits=w)
                    ev_last_w3 = ev_mm
                    if ri < len(rounds) - 1:
                        evac = P.add(V, lambda v, dc=dc, lh=lh, pwb=pwb: v.scalar_tensor_tensor(
                            B["X1"][:, dc, 512 * lh:512 * lh + 512], pwb[:, :], 0.5,
                            B["X1"][:, dc, 512 * lh:512 * lh + 512], ALU.mult, ALU.add),
                            waits=[ev_mm])
                    else:
                        sp = (dc * 2 + lh) % 2
                        evac = P.add(V, lambda v, dc=dc, lh=lh, pwb=pwb, sp=sp: v.scalar_tensor_tensor(
                            B["outstg"][sp][:, :], pwb[:, :], 0.5,
                            B["X1"][:, dc, 512 * lh:512 * lh + 512], ALU.mult, ALU.add),
                            waits=[ev_mm, outf[sp]])
                        odst = d_out[128 * dc:128 * dc + 128, l0 + 512 * lh:l0 + 512 * lh + 512]
                        oev = P.dma(S, odst, B["outstg"][sp][:, :], waits=[evac], sem=f"dout{sp}")
                        outf[sp] = oev
                        out_dma_evs.append(oev)
                    w3pf[pw_i] = evac
            for li in range(len(rnd)):
                hidf[li] = ev_last_w3
                q3lf[li] = ev_last_w3
        pfree["p_w3"] = w3pf[0]
        pfree["p_sh"] = w3pf[1]
        pfree["p_Y0"] = pfree["p_Y1"] = x1pf[0]
        pfree["p_diff0"] = pfree["p_diff1"] = x2pf[0]

    P.add(G, lambda g: g.engine_nop(), waits=out_dma_evs)
    P.emit()
    return nc


_NC_CACHE = [None]


def _host_layout(inputs):
    f = np.float32
    W_in = np.asarray(inputs["L0_attn_W_in"], f)
    W_out = np.asarray(inputs["L0_attn_W_out"], f)
    w1 = np.asarray(inputs["L0_ffn_w1"], f)
    w2 = np.asarray(inputs["L0_ffn_w2"], f)
    w3 = np.asarray(inputs["L0_ffn_w3"], f)
    conv_w = np.asarray(inputs["L0_attn_conv_w"], f)
    conv_b = np.asarray(inputs["L0_attn_conv_b"], f)
    nw = np.asarray(inputs["L0_attn_norm_weight"], f)
    dtb = np.asarray(inputs["L0_attn_dt_bias"], f)
    alog = np.asarray(inputs["L0_attn_A_log"], f)
    Dv = np.asarray(inputs["L0_attn_D"], f)

    WinT = np.ascontiguousarray(W_in.T)
    WinT_p = np.zeros((DM, N_ECH * 128), f)
    WinT_p[:, :8512] = WinT
    winT_t = np.ascontiguousarray(
        WinT_p.reshape(16, 128, N_ECH, 128).transpose(2, 1, 0, 3))
    woutT_t = np.ascontiguousarray(
        W_out.T.reshape(32, 128, 16, 128).transpose(2, 1, 0, 3))
    w1T_t = np.ascontiguousarray(
        w1.T.reshape(16, 128, N_HCH, 128).transpose(2, 1, 0, 3))
    w2T_t = np.ascontiguousarray(
        w2.T.reshape(16, 128, N_HCH, 128).transpose(2, 1, 0, 3))
    w3T_t = np.ascontiguousarray(w3.T.reshape(N_HCH, 128, 2048))
    cw_pp = np.ascontiguousarray(conv_w.reshape(34, 128, 4).transpose(1, 0, 2))
    cb_pp = np.ascontiguousarray(conv_b.reshape(34, 128).T)
    nw_pp = np.ascontiguousarray(nw.reshape(32, 128).T)
    dd32 = np.zeros((128, 32, 128), f)
    for i in range(32):
        for p_ in range(64):
            dd32[p_, i, p_] = Dv[2 * i]
            dd32[64 + p_, i, 64 + p_] = Dv[2 * i + 1]
    consts = dict(
        winT_t=winT_t, woutT_t=woutT_t, w1T_t=w1T_t, w2T_t=w2T_t, w3T_t=w3T_t,
        cw_pp=cw_pp, cb_pp=cb_pp, nw_pp=nw_pp,
        dtb=dtb.reshape(64, 1), alog=alog.reshape(64, 1), dd32=dd32,
        idn=np.eye(128, dtype=bfdt),
        trilT=(np.arange(128)[:, None] <= np.arange(128)[None, :]).astype(bfdt),
        o1f=np.ones((1, 128), f), o1b=np.ones((1, 128), bfdt),
        o128b=np.ones((128, 128), bfdt), o128f=np.ones((128, 128), f),
        o128f1=np.ones((128, 1), f),
    )
    return consts


def kernel(**inputs):
    consts = _host_layout(inputs)
    u = np.asarray(inputs["L0_attn_u"], np.float32)
    if _NC_CACHE[0] is None:
        _NC_CACHE[0] = build_program()
    nc = _NC_CACHE[0]
    in_maps = []
    for b in range(8):
        m = dict(consts)
        m["u_fm"] = np.ascontiguousarray(u[b].T)
        in_maps.append(m)
    res = run_bass_kernel_spmd(nc, in_maps, core_ids=list(range(8)))
    out = np.stack([np.asarray(res.results[b]["out_fm"]).T for b in range(8)])
    return np.ascontiguousarray(out, dtype=np.float32)


if __name__ == "__main__":
    nc = build_program()
    print("build ok, instructions:", nc.n_instructions())


# version 23 (final)
# speedup vs baseline: 1.1987x; 1.0312x over previous
import numpy as np
import ml_dtypes
import concourse.bass as bass
import concourse.mybir as mybir
from concourse.bass_utils import run_bass_kernel_spmd

F32 = mybir.dt.float32
BF16 = mybir.dt.bfloat16
AF = mybir.ActivationFunctionType
ALU = mybir.AluOpType
AX = mybir.AxisListType
bfdt = ml_dtypes.bfloat16

V, A, T, G, S = "vector", "scalar", "tensor", "gpsimd", "sync"
sV, sA, sT, sG, sS = "sV", "sA", "sT", "sGc", "sS"

SEGL = 1024
NSEG = 2
DM = 2048
DI = 4096
HFF = 5632
N_ECH = 67
N_HCH = 44
FFN_R = 8
EPS_LN = 1e-5
EPS_RMS = 1e-5
GAMMA_EPS = 1e-8
NW1 = float(HFF * DM)


class Pipe:
    ENGINES = ("sync", "scalar", "vector", "tensor", "gpsimd")
    CHAINED = {"vector": "sV", "scalar": "sA", "gpsimd": "sGc"}

    def __init__(self, nc):
        self.nc = nc
        self.ops = {e: [] for e in self.ENGINES}
        self.sem_names = []
        self.counts = {}
        self._last_compute = {}

    def new_sem(self, name):
        if name in self.counts:
            return name
        self.sem_names.append(name)
        self.counts[name] = 0
        return name

    def add(self, engine, fn, waits=(), inc=None, amt=1, _is_dma=False):
        ev = None
        waits = [w for w in waits if w is not None]
        if not _is_dma and engine in self.CHAINED:
            if inc is None:
                inc = self.CHAINED[engine]
            prev = self._last_compute.get(engine)
            if prev is not None:
                waits = waits + [prev]
        if engine == "tensor" and inc is None:
            inc = "sT"
        if inc is not None:
            self.new_sem(inc)
            self.counts[inc] += amt
            ev = (inc, self.counts[inc])
        if not _is_dma and engine in self.CHAINED:
            self._last_compute[engine] = ev
        self.ops[engine].append((fn, tuple(waits), inc, amt))
        return ev

    def dma(self, engine, out, in_, waits=(), sem=None, **kw):
        assert sem is not None
        self.new_sem(sem)
        return self.add(engine,
                        lambda eng, o=out, i=in_: eng.dma_start(out=o, in_=i, **kw),
                        waits=waits, inc=sem, amt=16, _is_dma=True)

    def last(self, engine):
        if engine == "tensor":
            n = self.counts.get("sT", 0)
            return ("sT", n) if n else None
        return self._last_compute.get(engine)

    def barrier(self):
        return [e for e in (self.last(V), self.last(A), self.last(T), self.last(G))
                if e is not None]

    def check(self):
        idx = {e: 0 for e in self.ENGINES}
        cnt = {s: 0 for s in self.sem_names}
        progress = True
        while progress:
            progress = False
            for e in self.ENGINES:
                while idx[e] < len(self.ops[e]):
                    fn, waits, inc, amt = self.ops[e][idx[e]]
                    if all(cnt[s] >= v for s, v in waits):
                        if inc is not None:
                            cnt[inc] += amt
                        idx[e] += 1
                        progress = True
                    else:
                        break
        stuck = {e: (idx[e], len(self.ops[e])) for e in self.ENGINES
                 if idx[e] < len(self.ops[e])}
        if stuck:
            detail = {}
            for e in stuck:
                fn, waits, inc, amt = self.ops[e][idx[e]]
                detail[e] = [(s, v, cnt[s]) for s, v in waits if cnt[s] < v]
            raise RuntimeError(f"Pipe deadlock: stuck={stuck} waits={detail}")

    def emit(self):
        self.check()
        nc = self.nc
        sems = {}
        import contextlib
        with contextlib.ExitStack() as stck:
            for s in self.sem_names:
                sems[s] = stck.enter_context(nc.semaphore(s))
            block = stck.enter_context(nc.Block())
            for e in self.ENGINES:
                if not self.ops[e]:
                    continue
                ops = self.ops[e]

                def section(eng, ops=ops, ename=e):
                    watermark = {}
                    for fn, waits, inc, amt in ops:
                        agg = {}
                        for s, v in waits:
                            if v > agg.get(s, -1):
                                agg[s] = v
                        for s, v in agg.items():
                            if watermark.get(s, -1) < v:
                                eng.wait_ge(sems[s], v)
                                watermark[s] = v
                        ins = fn(eng)
                        if inc is not None:
                            assert ins is not None, f"None instr on {ename}"
                            ins.then_inc(sems[inc], amt)
                getattr(block, e)(section)


class Overlay:
    def __init__(self, nc, base, size, tag):
        self.nc, self.base, self.size, self.tag = nc, base, size, tag
        self.off = 0
        self.n = 0

    def reset(self):
        self.off = 0

    def alloc(self, name, shape, dtype=F32):
        per_part = int(np.prod(shape[1:])) * mybir.dt.size(dtype)
        per_part = (per_part + 31) // 32 * 32
        assert self.off + per_part <= self.size, \
            f"overlay {self.tag} overflow: {name} {self.off}+{per_part}>{self.size}"
        self.n += 1
        t = self.nc.alloc_sbuf_tensor_at(f"{self.tag}{self.n}_{name}", shape, dtype,
                                         offset=self.base + self.off)
        self.off += per_part
        return t



def ssd_init(P, B, ev_consts):
    ev_dd = P.add(V, lambda v: v.tensor_copy(B["Ddiag"][:, :, :], B["Ddiag32"][:, :, :]),
                  waits=list(ev_consts))
    ev_s0 = P.add(V, lambda v: v.memset(B["STATES_bf"][:, :], 0.0))
    st = {
        "stage_ok": [ev_dd],
        "states_bf_ready": [ev_s0],
        "psh_free": None, "pY_free": [None, None], "pdiff_free": [None, None],
        "stage_free": [None, None], "dmin_free": [None, None], "E_free": [None, None],
        "ST_free": [None] * 4, "CE_free": [None] * 4,
        "pxT_free": None, "psmall_free": None, "cdrow_free": None,
        "xdt_freeT": None, "xdtd_freeT": None, "dtbf_freeT": None,
        "decaybf_freeT": None, "Btm_freeT": None, "cdb_freeT": None,
        "ST_CE_init": ev_dd,
    }
    return st


def emit_ssd_chunk(P, B, c, ev_ready, st):
    lo = c * 128
    sl = slice(lo, lo + 128)

    P.add(V, lambda v: v.tensor_scalar_mul(B["dA"][:, :], B["dt_hm"][:, sl], B["A_col"][:, 0:1]),
          waits=list(ev_ready["dt"]))
    ev_acs = P.add(V, lambda v: v.tensor_tensor_scan(B["Acs"][:, :], B["dA"][:, :], B["z64"][:, :], 0.0, ALU.add, ALU.add))
    ev_negacs = P.add(V, lambda v: v.tensor_scalar_mul(B["negAcs"][:, :], B["Acs"][:, :], -1.0))
    ev_tdec = P.add(V, lambda v: v.tensor_scalar(B["tdec"][:, :], B["Acs"][:, :], B["Acs"][:, 127:128], None, ALU.subtract))
    ev_dtbf = P.add(V, lambda v: v.tensor_copy(B["dtbf"][:, :], B["dt_hm"][:, sl]),
                    waits=[st["dtbf_freeT"]])
    ev_dec = P.add(A, lambda a: a.activation(B["decay"][:, :], B["tdec"][:, :], AF.Exp, scale=-1.0),
                   waits=[ev_tdec])
    ev_eacs = P.add(A, lambda a: a.activation(B["eacs"][:, :], B["Acs"][:, :], AF.Exp),
                    waits=[ev_acs])
    ev_cd = P.add(A, lambda a: a.activation(B["cd"][:, 0:1], B["Acs"][:, 127:128], AF.Exp),
                  waits=[ev_acs])
    ev_decbf = P.add(V, lambda v: v.tensor_copy(B["decaybf"][:, :], B["decay"][:, :]),
                     waits=[ev_dec, st["decaybf_freeT"]])

    ev_t1 = P.add(T, lambda t: t.transpose(B["p_xT"][:, 0:64], B["dtbf"][:, :], B["idn"][0:64, 0:64]),
                  waits=[st["pxT_free"], ev_dtbf])
    P.add(T, lambda t: t.transpose(B["p_xT"][:, 64:128], B["decaybf"][:, :], B["idn"][0:64, 0:64]),
          waits=[ev_decbf])
    ev_t3 = P.add(T, lambda t: t.transpose(B["p_xT"][:, 128:256], B["B_fm"][:, sl], B["idn"][:, :]),
                  waits=list(ev_ready["BC"]) + [st["Btm_freeT"]])
    ev_dtT = P.add(V, lambda v: v.tensor_copy(B["dt_tm"][:, :], B["p_xT"][:, 0:64]), waits=[ev_t3])
    ev_decT = P.add(V, lambda v: v.tensor_copy(B["decay_tm"][:, :], B["p_xT"][:, 64:128]))
    ev_BT = P.add(V, lambda v: v.tensor_copy(B["B_tm"][:, :], B["p_xT"][:, 128:256]))

    ev_g = P.add(T, lambda t: t.matmul(B["p_small"][:, 0:128], B["B_fm"][:, sl], B["C_fm"][:, sl],
                                       start=True, stop=True),
                 waits=list(ev_ready["BC"]) + [st["psmall_free"]])
    ev_Gm = P.add(V, lambda v: v.tensor_tensor(B["Gm"][:, :], B["p_small"][:, 0:128], B["tril"][:, :], ALU.mult),
                  waits=[ev_g, st["ST_CE_init"]])

    ev_cdrow = P.dma(G, B["cd_row"][0:1, :], B["cd"][:, 0:1],
                     waits=[ev_cd, st.get("cdrow_free")], sem="cdr")
    ev_cdmm = P.add(T, lambda t: t.matmul(B["p_small"][:, 192:256], B["ones1_f32"][0:1, :], B["cd_row"][0:1, :],
                                          start=True, stop=True),
                    waits=[ev_cdrow, ev_Gm])
    ev_cdb = P.add(V, lambda v: v.tensor_copy(B["cdb"][:, :], B["p_small"][:, 192:256]),
                   waits=[ev_cdmm, st["cdb_freeT"]])

    prev_evac = [ev_dtT, ev_decT, ev_BT]
    ev_x1 = ev_x2 = None
    prev_round_tt = {0: None, 1: st.get("pw3_free")}
    for r in range(4):
        pxb = B["p_xT"] if r % 2 == 0 else B["p_w3"].bitcast(BF16)
        ev_tr = None
        for k in range(8):
            pc = 8 * r + k
            w = (list(ev_ready["x"]) + prev_evac + [prev_round_tt[r % 2]]) if k == 0 else ()
            ev_tr = P.add(
                T, lambda t, pc=pc, k=k, pxb=pxb: t.transpose(
                    pxb[:, 128 * k:128 * k + 128], B["xy"][:, pc, sl], B["idn"][:, :]),
                waits=w)

        def xdt_fn(v, r=r, pxb=pxb):
            dt_ap = B["dt_tm"][:, 16 * r:16 * r + 16]
            dt_ap = dt_ap.rearrange("p (pc par) -> p pc par", par=2).unsqueeze(3).broadcast_to([128, 8, 2, 64])
            return v.tensor_tensor(B["xdt"][:, 1024 * r:1024 * r + 1024], pxb[:, 0:1024], dt_ap, ALU.mult)
        w = [ev_tr, ev_dtT] + ([st["xdt_freeT"]] if r == 0 else [])
        ev_x1 = P.add(V, xdt_fn, waits=w)
        prev_round_tt[r % 2] = ev_x1

        def xdtd_fn(v, r=r):
            de_ap = B["decay_tm"][:, 16 * r:16 * r + 16]
            de_ap = de_ap.rearrange("p (pc par) -> p pc par", par=2).unsqueeze(3).broadcast_to([128, 8, 2, 64])
            return v.tensor_tensor(B["xdtd"][:, 1024 * r:1024 * r + 1024], B["xdt"][:, 1024 * r:1024 * r + 1024], de_ap, ALU.mult)
        w = [ev_decT] + ([st["xdtd_freeT"]] if r == 0 else [])
        ev_x2 = P.add(V, xdtd_fn, waits=w)
        prev_evac = []
    ev_xdt, ev_xdtd = ev_x1, ev_x2
    st["pxT_free"] = prev_round_tt[0]
    st["pw3_free"] = prev_round_tt[1]

    ev_ST = [None] * 16
    ev_CE = [None] * 16
    ev_y_evacs = []
    last_pair_mm = None
    st["eacsb_free"] = [st["psh_free"], st["psmall_free"]]

    def emit_group(g):
        p = g % 2
        q4 = g % 4
        rs = slice(4 * g, 4 * g + 4)
        wst = [st["stage_free"][p], ev_negacs, ev_eacs] + (st["stage_ok"] if g < 2 else [])
        P.dma(G, B["stageL"][p][1:2, :], B["negAcs"][rs, :], waits=wst, sem=f"stg{p}")
        P.dma(G, B["stageR"][p][0:1, :], B["Acs"][rs, :], sem=f"stg{p}")
        e3 = P.dma(G, B["eacsrow"][p][0:1, :], B["eacs"][rs, :], sem=f"stg{p}")
        evd = None
        for j in range(4):
            w = [st["pdiff_free"][p], e3] if j == 0 else ()
            evd = P.add(
                T, lambda t, p=p, j=j: t.matmul(
                    B["p_diff"][p][:, 128 * j:128 * j + 128],
                    B["stageL"][p][0:2, 128 * j:128 * j + 128],
                    B["stageR"][p][0:2, 128 * j:128 * j + 128],
                    start=True, stop=True),
                waits=w)
        ebank = B["p_sh"] if (g % 2 == 0) else B["p_small"]
        wb = [st["eacsb_free"][g % 2]]
        if g == 1:
            wb.append(ev_cdb)
        ev_eb = P.add(T, lambda t, p=p, ebank=ebank: t.matmul(ebank[:, :], B["ones1_bf"][0:1, :], B["eacsrow"][p][0:1, :],
                                                 start=True, stop=True),
                      waits=wb)
        ev_min = P.add(A, lambda a, p=p: a.activation(B["dmin"][p][:, :], B["p_diff"][p][:, :], AF.Relu, scale=-1.0),
                       waits=[st["dmin_free"][p], evd])
        st["pdiff_free"][p] = ev_min
        st["stage_free"][p] = ev_eb
        ev_E = P.add(A, lambda a, p=p: a.activation(B["dmin"][p][:, :], B["dmin"][p][:, :], AF.Exp, scale=-1.0))

        def st_fn(v, p=p, q4=q4):
            gm_ap = B["Gm"][:, :].unsqueeze(1).broadcast_to([128, 4, 128])
            return v.tensor_tensor(B["ST"][q4][:, :], B["dmin"][p][:, :], gm_ap, ALU.mult)
        ev_st_ = P.add(V, st_fn, waits=[st["ST_free"][q4], ev_E, ev_Gm])
        st["dmin_free"][p] = ev_st_
        ev_ST[g] = ev_st_

        def ce_fn(v, q4=q4, sl=sl, ebank=ebank):
            c_ap = B["C_fm"][:, sl].unsqueeze(1).broadcast_to([128, 4, 128])
            return v.tensor_tensor(B["CE"][q4][:, :], c_ap, ebank[:, :], ALU.mult)
        ev_ce = P.add(V, ce_fn, waits=[st["CE_free"][q4], ev_eb])
        st["eacsb_free"][g % 2] = ev_ce
        ev_CE[g] = ev_ce

    def emit_yfill(q):
        nonlocal last_pair_mm
        b = q % 2
        last = None
        for m in range(4):
            i = 4 * q + m
            h0, h1 = 2 * i, 2 * i + 1
            g0, j0 = h0 // 4, h0 % 4
            g1, j1 = h1 // 4, h1 % 4
            col = slice(128 * m, 128 * m + 128)
            w0 = [ev_ST[g0], ev_ST[g1], ev_CE[g0], ev_CE[g1], ev_xdt]
            if m == 0:
                w0 = w0 + [st["pY_free"][b]] + list(st["states_bf_ready"])
            P.add(T, lambda t, b=b, col=col, h0=h0, g0=g0, j0=j0: t.matmul(
                B["p_Y"][b][0:64, col], B["xdt"][:, 64 * h0:64 * h0 + 64],
                B["ST"][g0 % 4][:, 128 * j0:128 * j0 + 128],
                start=True, stop=False, tile_position=(0, 0)),
                waits=w0)
            P.add(T, lambda t, b=b, col=col, h0=h0, g0=g0, j0=j0: t.matmul(
                B["p_Y"][b][0:64, col], B["STATES_bf"][:, 64 * h0:64 * h0 + 64],
                B["CE"][g0 % 4][:, 128 * j0:128 * j0 + 128],
                start=False, stop=False, tile_position=(0, 0)))
            P.add(T, lambda t, b=b, col=col, i=i, sl=sl: t.matmul(
                B["p_Y"][b][0:64, col], B["Ddiag"][:, i, 0:64], B["xy"][:, i, sl],
                start=False, stop=True, tile_position=(0, 0)))
            P.add(T, lambda t, b=b, col=col, h1=h1, g1=g1, j1=j1: t.matmul(
                B["p_Y"][b][64:128, col], B["xdt"][:, 64 * h1:64 * h1 + 64],
                B["ST"][g1 % 4][:, 128 * j1:128 * j1 + 128],
                start=True, stop=False, tile_position=(0, 64)))
            P.add(T, lambda t, b=b, col=col, h1=h1, g1=g1, j1=j1: t.matmul(
                B["p_Y"][b][64:128, col], B["STATES_bf"][:, 64 * h1:64 * h1 + 64],
                B["CE"][g1 % 4][:, 128 * j1:128 * j1 + 128],
                start=False, stop=False, tile_position=(0, 64)))
            last = P.add(T, lambda t, b=b, col=col, i=i, sl=sl: t.matmul(
                B["p_Y"][b][64:128, col], B["Ddiag"][:, i, 64:128], B["xy"][:, i, sl],
                start=False, stop=True, tile_position=(0, 64)))
        last_pair_mm = last

        def yevac_fn(a, b=b, q=q, sl=sl):
            return a.copy(B["xy"][:, 4 * q:4 * q + 4, sl], B["p_Y"][b][:, :])
        ev_ye = P.add(A, yevac_fn, waits=[last])
        st["pY_free"][b] = ev_ye
        ev_y_evacs.append(ev_ye)
        for g in (2 * q, 2 * q + 1):
            st["ST_free"][g % 4] = last
            st["CE_free"][g % 4] = last

    for q in range(9):
        if q < 8:
            emit_group(2 * q)
            emit_group(2 * q + 1)
        if q >= 1:
            emit_yfill(q - 1)

    ev_states = []
    sbf_ready = []
    psh_free = st["eacsb_free"][0]
    last_smm = None
    for j in range(8):
        colj = slice(512 * j, 512 * j + 512)
        wmm = [ev_xdtd, ev_BT, psh_free] + ([last_pair_mm] if j == 0 else [])
        ev_smm = P.add(T, lambda t, colj=colj: t.matmul(B["p_sh"][:, :], B["B_tm"][:, :], B["xdtd"][:, colj],
                                                        start=True, stop=True),
                       waits=wmm)
        last_smm = ev_smm

        def tmul_fn(v, j=j, colj=colj):
            cdb_ap = B["cdb"][:, 8 * j:8 * j + 8].unsqueeze(2).broadcast_to([128, 8, 64])
            return v.tensor_tensor(B["tscr"][:, :], B["STATES_bf"][:, colj], cdb_ap, ALU.mult)
        ev_tm = P.add(V, tmul_fn, waits=[ev_cdb, last_pair_mm])
        ev_up = P.add(V, lambda v, colj=colj: v.tensor_tensor(B["STATES_bf"][:, colj], B["tscr"][:, :], B["p_sh"][:, :], ALU.add),
                      waits=[ev_smm, ev_tm])
        sbf_ready.append(ev_up)
        psh_free = ev_up
        ev_states.append(ev_up)

    st["states_bf_ready"] = sbf_ready
    st["psh_free"] = psh_free
    st["psmall_free"] = st["eacsb_free"][1]
    st["cdrow_free"] = ev_cdmm
    st["xdt_freeT"] = last_pair_mm
    st["xdtd_freeT"] = last_smm
    st["dtbf_freeT"] = ev_t1
    st["decaybf_freeT"] = ev_t1
    st["Btm_freeT"] = last_smm
    st["cdb_freeT"] = ev_states[-1]
    return {"y": ev_y_evacs, "states": ev_states}



def build_program():
    nc = bass.Bass("TRN2", target_bir_lowering=False)

    d_u = nc.dram_tensor("u_fm", [DM, 2048], F32, kind="ExternalInput").ap()
    d_win = nc.dram_tensor("winT_t", [N_ECH, 128, 16, 128], F32, kind="ExternalInput").ap()
    d_wout = nc.dram_tensor("woutT_t", [16, 128, 32, 128], F32, kind="ExternalInput").ap()
    d_w1 = nc.dram_tensor("w1T_t", [N_HCH, 128, 16, 128], F32, kind="ExternalInput").ap()
    d_w2 = nc.dram_tensor("w2T_t", [N_HCH, 128, 16, 128], F32, kind="ExternalInput").ap()
    d_w3 = nc.dram_tensor("w3T_t", [N_HCH, 128, 2048], F32, kind="ExternalInput").ap()
    d_cw = nc.dram_tensor("cw_pp", [128, 34, 4], F32, kind="ExternalInput").ap()
    d_cb = nc.dram_tensor("cb_pp", [128, 34], F32, kind="ExternalInput").ap()
    d_nw = nc.dram_tensor("nw_pp", [128, 32], F32, kind="ExternalInput").ap()
    d_dtb = nc.dram_tensor("dtb", [64, 1], F32, kind="ExternalInput").ap()
    d_alog = nc.dram_tensor("alog", [64, 1], F32, kind="ExternalInput").ap()
    d_dd = nc.dram_tensor("dd32", [128, 32, 128], F32, kind="ExternalInput").ap()
    d_idn = nc.dram_tensor("idn", [128, 128], BF16, kind="ExternalInput").ap()
    d_tril = nc.dram_tensor("trilT", [128, 128], BF16, kind="ExternalInput").ap()
    d_o1f = nc.dram_tensor("o1f", [1, 128], F32, kind="ExternalInput").ap()
    d_o1b = nc.dram_tensor("o1b", [1, 128], BF16, kind="ExternalInput").ap()
    d_o128b = nc.dram_tensor("o128b", [128, 128], BF16, kind="ExternalInput").ap()
    d_o128f = nc.dram_tensor("o128f", [128, 128], F32, kind="ExternalInput").ap()
    d_o128f1 = nc.dram_tensor("o128f1", [128, 1], F32, kind="ExternalInput").ap()
    d_out = nc.dram_tensor("out_fm", [DM, 2048], F32, kind="ExternalOutput").ap()
    d_q1s = nc.dram_tensor("q1s", [N_HCH, 128, 2048], BF16, kind="Internal").ap()
    d_q2s = nc.dram_tensor("q2s", [N_HCH, 128, 2048], BF16, kind="Internal").ap()
    d_q3s = nc.dram_tensor("q3s", [N_HCH, 128, 2048], BF16, kind="Internal").ap()

    TOTAL = 212800
    BASE = 16512
    AR_A_SZ = 65536
    AR_H_SZ = 32768
    AR_X_SZ = 32768
    AR_W_SZ = 53696
    top = BASE + TOTAL
    a_base = top - AR_A_SZ
    h_base = a_base - AR_H_SZ
    x_base = h_base - AR_X_SZ
    w_base = x_base - AR_W_SZ

    Al = lambda n, s, d=F32: nc.alloc_sbuf_tensor(n, s, d)
    B = {}
    B["STATES_bf"] = Al("STATES_bf", [128, 4096], BF16)
    B["Ddiag"] = Al("Ddiag", [128, 32, 128], BF16)
    B["idn"] = Al("c_idn", [128, 128], BF16)
    B["tril"] = Al("c_tril", [128, 128], BF16)
    B["ones1_f32"] = Al("c_o1f", [1, 128])
    B["ones1_bf"] = Al("c_o1b", [1, 128], BF16)
    B["o128b"] = Al("c_o128b", [128, 128], BF16)
    B["o128f"] = Al("c_o128f", [128, 128])
    B["o128f1"] = Al("c_o128f1", [128, 1])
    B["nw"] = Al("c_nw", [128, 32])
    B["cw"] = Al("c_cw", [128, 34, 4])
    B["cb"] = Al("c_cb", [128, 34])
    B["dtbias"] = Al("c_dtb", [64, 1])
    B["A_col"] = Al("A_col", [64, 1])
    B["stash"] = Al("stash", [128, 34, 3])
    B["tpp"] = Al("tpp", [128, 4])
    B["tneg"] = Al("tneg", [128, 4])
    B["B_fm"] = Al("B_fm", [128, SEGL], BF16)
    B["C_fm"] = Al("C_fm", [128, SEGL], BF16)
    B["dt_hm"] = Al("dt_hm", [64, SEGL])
    used = TOTAL - nc.sbuf_bytes_remaining
    assert BASE + used <= w_base, f"auto overflow: used={used}, limit={w_base - BASE}"

    arA = Overlay(nc, a_base, AR_A_SZ, "aA")
    arH = Overlay(nc, h_base, AR_H_SZ, "aH")
    arX = Overlay(nc, x_base, AR_X_SZ, "aX")
    arW = Overlay(nc, w_base, AR_W_SZ, "aW")

    B["Ddiag32"] = arA.alloc("dd32", [128, 32, 128], F32)
    arA.reset()
    B["u32"] = arA.alloc("u32", [128, 16, SEGL], F32)
    arA.reset()
    B["xy"] = arA.alloc("xy", [128, 32, SEGL], BF16)
    arA.reset()
    B["hid"] = arA.alloc("hid", [128, FFN_R, SEGL], BF16)
    B["q3l"] = arA.alloc("q3l", [128, FFN_R, 2048], BF16)
    B["qtmp2"] = arA.alloc("qtmp2", [128, 2048], BF16)

    B["h1"] = arH.alloc("h", [128, 16, SEGL], BF16)

    B["X1"] = arX.alloc("X1", [128, 16, SEGL], BF16)
    arX.reset()
    B["gslab"] = [arX.alloc(f"gslab{i}", [128, 2048], F32) for i in range(2)]
    B["gscr"] = arX.alloc("gscr", [128, 264])
    B["g3"] = arX.alloc("g3", [128, 4])
    B["grow"] = arX.alloc("grow", [1, 4])
    B["wslab3"] = arX.alloc("wslab3", [128, 2048], BF16)
    B["sqbX"] = arX.alloc("sqbX", [128, SEGL], BF16)

    B["dA"] = arW.alloc("dA", [64, 128])
    B["Acs"] = arW.alloc("Acs", [64, 128])
    B["negAcs"] = arW.alloc("negAcs", [64, 128])
    B["tdec"] = arW.alloc("tdec", [64, 128])
    B["decay"] = arW.alloc("decay", [64, 128])
    B["eacs"] = arW.alloc("eacs", [64, 128])
    B["cd"] = arW.alloc("cd", [64, 1])
    B["dtbf"] = arW.alloc("dtbf", [64, 128], BF16)
    B["decaybf"] = arW.alloc("decaybf", [64, 128], BF16)
    B["z64"] = arW.alloc("z64", [64, 128])
    B["ones512"] = arW.alloc("ones512", [1, 512])
    B["dt_tm"] = arW.alloc("dt_tm", [128, 64], BF16)
    B["decay_tm"] = arW.alloc("decay_tm", [128, 64], BF16)
    B["B_tm"] = arW.alloc("B_tm", [128, 128], BF16)
    B["Gm"] = arW.alloc("Gm", [128, 128], BF16)
    B["cdb"] = arW.alloc("cdb", [128, 64])
    B["cd_row"] = arW.alloc("cd_row", [1, 64])
    B["xdt"] = arW.alloc("xdt", [128, 4096], BF16)
    B["xdtd"] = arW.alloc("xdtd", [128, 4096], BF16)
    B["tscr"] = arW.alloc("tscr", [128, 512])
    B["stageL"] = [arW.alloc(f"stageL{i}", [2, 512]) for i in range(2)]
    B["stageR"] = [arW.alloc(f"stageR{i}", [2, 512]) for i in range(2)]
    B["eacsrow"] = [arW.alloc(f"eacsrow{i}", [1, 512], BF16) for i in range(2)]
    B["dmin"] = [arW.alloc(f"dmin{i}", [128, 512]) for i in range(2)]
    B["ST"] = [arW.alloc(f"ST{i}", [128, 512], BF16) for i in range(4)]
    B["CE"] = [arW.alloc(f"CE{i}", [128, 512], BF16) for i in range(4)]
    assert arW.off <= AR_W_SZ

    arW.reset()
    B["wslab@m1"] = [arW.alloc(f"wslab{i}", [128, 2048], BF16) for i in range(2)]
    B["dtraw"] = arW.alloc("dtraw", [64, SEGL])
    B["dts1"] = arW.alloc("dts1", [64, SEGL])
    B["dts2"] = B["dtraw"]
    B["convtmp"] = arW.alloc("convtmp", [128, SEGL])
    B["xbcraw"] = [arW.alloc(f"xbcraw{i}", [128, 1028]) for i in range(2)]
    B["convo@m0"] = [arW.alloc(f"convo{i}", [128, SEGL]) for i in range(2)]
    B["sqb@m0"] = arW.alloc("sqb0", [128, SEGL], BF16)
    B["sz@m0"] = arW.alloc("sz0", [128, SEGL])
    B["stat1@m0"] = arW.alloc("stat10", [128, SEGL])
    B["stat2@m0"] = arW.alloc("stat20", [128, SEGL])
    assert arW.off <= AR_W_SZ, f"W2 {arW.off}"

    arW.reset()
    B["wslab@m3"] = [arW.alloc(f"wslab3_{i}", [128, 2048], BF16) for i in range(2)]
    B["sz@m3"] = arW.alloc("sz3", [128, SEGL])
    assert arW.off <= AR_W_SZ - 6272
    arW.off = AR_W_SZ - 6272
    B["sqb@m3"] = arW.alloc("sqb3", [128, SEGL], BF16)
    B["stat2@m3"] = arW.alloc("stat23", [128, SEGL])

    arW.reset()
    B["woslab"] = [arW.alloc(f"woslab{i}", [128, 4096], BF16) for i in range(2)]
    B["uscr"] = [arW.alloc(f"uscr{i}", [128, 1024]) for i in range(2)]
    B["tmpX"] = arW.alloc("tmpX", [128, 1024])
    assert arW.off <= AR_W_SZ - 4160

    arW.reset()
    B["sqb@m5"] = arW.alloc("sqb5", [128, SEGL], BF16)
    B["sz@m5"] = arW.alloc("sz5", [128, SEGL])
    B["stat1@m5"] = arW.alloc("stat15", [128, SEGL])
    B["stat2@m5"] = arW.alloc("stat25", [128, SEGL])
    B["convo@m5"] = [arW.alloc(f"convo5_{i}", [128, SEGL]) for i in range(2)]
    assert arW.off <= AR_W_SZ

    arW.reset()
    _w1 = arW.alloc("w1slab", [128, 2048], F32)
    B["w1slab"] = [_w1, _w1]
    _w2 = arW.alloc("w2slab", [128, 2048], F32)
    B["w2slab"] = [_w2, _w2]
    B["q1"] = [arW.alloc(f"q1_{i}", [128, 2048], BF16) for i in range(2)]
    B["q2"] = [arW.alloc(f"q2_{i}", [128, 2048], BF16) for i in range(2)]
    B["qtmp"] = arW.alloc("qtmp", [128, 2048], BF16)
    B["sig"] = arW.alloc("sig", [128, SEGL])
    B["outstg"] = [arW.alloc(f"outstg{i}", [128, 512]) for i in range(2)]
    B["w3slab"] = B["w2slab"]
    assert arW.off <= AR_W_SZ - 6272, f"W6 {arW.off}"

    B["p_xT"] = nc.alloc_psum_tensor("p_xT", [128, 1024], BF16)
    p_ip = [nc.alloc_psum_tensor(f"p_ip{i}", [128, 1024], F32) for i in range(3)]
    B["p_ip"] = p_ip
    B["p_Y"] = [p_ip[0][:, 0:512], p_ip[0][:, 512:1024]]
    B["p_diff"] = [p_ip[1][:, 0:512], p_ip[1][:, 512:1024]]
    B["p_sh"] = p_ip[2][:, 0:512]
    B["p_small"] = p_ip[2][:, 512:1024]
    B["p_w3"] = nc.alloc_psum_tensor("p_w3", [128, 512], F32)

    P = Pipe(nc)
    for s in (sV, sA, "sT", "sGc", "stg0", "stg1", "cdr"):
        P.new_sem(s)

    ld = []
    for dst, src in ((B["idn"], d_idn), (B["tril"], d_tril), (B["ones1_f32"], d_o1f),
                     (B["ones1_bf"], d_o1b), (B["o128b"], d_o128b), (B["o128f"], d_o128f),
                     (B["o128f1"], d_o128f1),
                     (B["nw"], d_nw), (B["cw"], d_cw), (B["cb"], d_cb),
                     (B["dtbias"], d_dtb), (B["A_col"], d_alog), (B["Ddiag32"], d_dd)):
        ld.append(P.dma(S, dst[:], src, sem="dini"))
    ev_ini = [ld[-1]]
    e = P.add(A, lambda a: a.activation(B["A_col"][:, 0:1], B["A_col"][:, 0:1], AF.Exp),
              waits=ev_ini)
    P.add(V, lambda v: v.tensor_scalar_mul(B["A_col"][:, 0:1], B["A_col"][:, 0:1], -1.0), waits=[e])
    ev_stash0 = P.add(V, lambda v: v.memset(B["stash"][:, :, :], 0.0))

    sst = ssd_init(P, B, ev_consts=ev_ini)

    pfree = {"p_Y0": None, "p_Y1": None, "p_diff0": None, "p_diff1": None,
             "p_sh": None, "p_small": None, "p_w3": None}

    def pf_pair(names):
        return [pfree[n] for n in names]

    ev_gamma_done = [None]
    qcache_done = {}
    wfree3 = [None, None, None]
    out_dma_evs = []

    gamma_state = {"col": 0, "gfree": [None, None], "last": None}

    def emit_gamma_batch(n):
        srcs = []
        for dw in (d_w1, d_w2, d_w3):
            for hc in range(N_HCH):
                srcs.append((dw, hc))
        while n > 0 and gamma_state["col"] < len(srcs):
            col = gamma_state["col"]
            dw, hc = srcs[col]
            par = col % 2
            if dw is d_w3:
                s_ = dw[hc, :, :]
            else:
                s_ = dw[hc, :, :, :].rearrange("p a b -> p (a b)")
            gf = gamma_state["gfree"][par]
            evd = P.dma(S, B["gslab"][par][:, :], s_,
                        waits=list(gf) if isinstance(gf, tuple) else [gf],
                        sem=f"dg{par}")
            ev_ra = P.add(A, lambda a, par=par, col=col: a.activation(
                B["gslab"][par][:, 0:1024], B["gslab"][par][:, 0:1024], AF.Abs,
                accum_out=B["gscr"][:, 2 * col:2 * col + 1]), waits=[evd])
            ev_rb = P.add(A, lambda a, par=par, col=col: a.activation(
                B["gslab"][par][:, 1024:2048], B["gslab"][par][:, 1024:2048], AF.Abs,
                accum_out=B["gscr"][:, 2 * col + 1:2 * col + 2]))
            gamma_state["gfree"][par] = (ev_ra, ev_rb)
            gamma_state["last"] = (ev_ra, ev_rb)
            gamma_state["col"] = col + 1
            n -= 1

    def emit_gamma_fold():
        last_red = list(gamma_state["last"])
        assert gamma_state["col"] == 3 * N_HCH
        evs = []
        for idx in range(3):
            evs.append(P.add(V, lambda v, idx=idx: v.reduce_sum(
                B["g3"][:, idx:idx + 1], B["gscr"][:, 88 * idx:88 * idx + 88], AX.X),
                waits=last_red))
        ev_z = P.add(V, lambda v: v.memset(B["g3"][:, 3:4], 0.0))
        ev_gmm = P.add(T, lambda t: t.matmul(B["p_w3"][0:1, 0:4], B["o128f1"][:, 0:1], B["g3"][:, 0:4],
                                             start=True, stop=True),
                       waits=[ev_z] + evs + [pfree["p_w3"]])
        ev_gr = P.add(V, lambda v: v.tensor_copy(B["grow"][0:1, 0:4], B["p_w3"][0:1, 0:4]),
                      waits=[ev_gmm])
        ev_gb = P.add(T, lambda t: t.matmul(B["p_w3"][:, 4:8], B["ones1_f32"][0:1, :], B["grow"][0:1, 0:4],
                                            start=True, stop=True), waits=[ev_gr])
        ev_tpp = P.add(V, lambda v: v.tensor_scalar(B["tpp"][:, 0:4], B["p_w3"][:, 4:8],
                                                    0.5 / NW1, 0.5 * GAMMA_EPS, ALU.mult, ALU.add),
                       waits=[ev_gb])
        ev_tn = P.add(V, lambda v: v.tensor_scalar_mul(B["tneg"][:, 0:4], B["tpp"][:, 0:4], -1.0))
        pfree["p_w3"] = ev_tpp
        ev_gamma_done[0] = ev_tn

    for seg in range(NSEG):
        l0 = seg * SEGL
        wu = (P.barrier() + list(qcache_done.values()) + out_dma_evs[-2:]) if seg else [sst["ST_CE_init"]]
        evu = []
        for half in range(2):
            src = d_u[1024 * half:1024 * (half + 1), l0:l0 + SEGL]
            src = src.rearrange("(k p) l -> p k l", p=128)
            evu.append(P.dma(S, B["u32"][:, 8 * half:8 * (half + 1), :], src,
                             waits=wu, sem="du"))
        ev_smm = ev_umm = None
        smm_by_k = []
        for k in range(16):
            sqbuf = B["sqb@m0"] if k % 2 == 0 else B["sqbX"]
            evq = P.add(A, lambda a, k=k, sqbuf=sqbuf: a.activation(sqbuf[:, :], B["u32"][:, k, :], AF.Square),
                        waits=evu + (wu if k < 2 else []) + ([smm_by_k[k - 2]] if k >= 2 else []))
            for lh in range(2):
                ev_smm = P.add(T, lambda t, k=k, lh=lh, sqbuf=sqbuf: t.matmul(
                    B["p_diff"][lh][:, :], B["o128b"][:, :], sqbuf[:, 512 * lh:512 * lh + 512],
                    start=(k == 0), stop=(k == 15)),
                    waits=[evq] + ([pfree[f"p_diff{lh}"]] if k == 0 else []))
            smm_by_k.append(ev_smm)
        for k in range(16):
            for lh in range(2):
                ev_umm = P.add(T, lambda t, k=k, lh=lh: t.matmul(
                    B["p_Y"][lh][:, :], B["o128f"][:, :], B["u32"][:, k, 512 * lh:512 * lh + 512],
                    start=(k == 0), stop=(k == 15)),
                    waits=(evu + [pfree["p_Y0"], pfree["p_Y1"]]) if k == 0 else ())
        P.add(A, lambda a: a.mul(B["stat1@m0"][:, :], B["p_ip"][0][:, :], 1.0 / DM),
              waits=[ev_umm, ev_smm] + (wu if seg else []))
        ev_v2 = P.add(A, lambda a: a.mul(B["stat2@m0"][:, :], B["p_ip"][1][:, :], 1.0 / DM))
        pfree["p_Y0"] = pfree["p_Y1"] = pfree["p_diff0"] = pfree["p_diff1"] = ev_v2
        evm = P.add(A, lambda a: a.activation(B["sz@m0"][:, :], B["stat1@m0"][:, :], AF.Square),
                    waits=[ev_v2])
        evv = P.add(V, lambda v: v.scalar_tensor_tensor(B["stat2@m0"][:, :], B["stat2@m0"][:, :], float(EPS_LN), B["sz@m0"][:, :], ALU.add, ALU.subtract),
                    waits=[evm])
        evl = P.add(A, lambda a: a.activation(B["stat2@m0"][:, :], B["stat2@m0"][:, :], AF.Ln),
                    waits=[evv])
        ev_rstd = P.add(A, lambda a: a.activation(B["stat2@m0"][:, :], B["stat2@m0"][:, :], AF.Exp, scale=-0.5))
        ev_h1k = []
        for k in range(16):
            evs_ = P.add(V, lambda v, k=k: v.tensor_tensor(B["convo@m0"][k % 2][:, :], B["u32"][:, k, :], B["stat1@m0"][:, :], ALU.subtract),
                         waits=[ev_rstd])
            ev_h1k.append(P.add(V, lambda v, k=k: v.tensor_tensor(B["h1"][:, k, :], B["convo@m0"][k % 2][:, :], B["stat2@m0"][:, :], ALU.mult)))
        ev_h1done = [ev_h1k[-1]]


        wfree = wfree3 if seg else [None, None, None]
        slabs3 = [B["wslab@m1"][0], B["wslab@m1"][1], B["wslab3"]]
        psum3 = [(B["p_Y"][0], B["p_Y"][1]), (B["p_diff"][0], B["p_diff"][1]),
                 (B["p_sh"], B["p_small"])]
        slot_free = {0: [pfree["p_Y0"], pfree["p_Y1"]],
                     1: [pfree["p_diff0"], pfree["p_diff1"]],
                     2: [pfree["p_sh"], pfree["p_small"]]}
        xbf = [None, None]
        cvf = [None, None]
        ev_x_ready = []
        ev_dt_ready = []
        for ci, ec in enumerate(range(32, 67)):
            par = ci % 3
            p3 = ci % 3
            if seg == 0:
                emit_gamma_batch(2)
            src = d_win[ec, :, :, :].rearrange("p a b -> p (a b)")
            evd = P.dma(G, slabs3[par][:, :], src, waits=[wfree[par]] + (P.barrier() if ci < 3 and seg else []),
                        sem=f"dwin{par}")
            pb = psum3[p3]
            ev_mm = None
            for lh in range(2):
                for k in range(16):
                    w = [ev_h1k[k]]
                    if lh == 0 and k == 0:
                        w = w + [evd] + slot_free[p3]
                    ev_mm = P.add(T, lambda t, sl3=slabs3[par], lh=lh, k=k, pb=pb: t.matmul(
                        pb[lh][:, :], sl3[:, 128 * k:128 * k + 128],
                        B["h1"][:, k, 512 * lh:512 * lh + 512],
                        start=(k == 0), stop=(k == 15)), waits=w)
            wfree[par] = ev_mm
            if ec < 66:
                m = ec - 32
                par = ci % 2
                pfull = p_ip[p3]
                co = B["convo@m0"][par]
                e1_ = P.add(V, lambda v, co=co, m=m: v.tensor_scalar(
                    co[:, 0:3], B["stash"][:, m, 0:3], B["cw"][:, m, 0:1], None, ALU.mult),
                    waits=[ev_mm, ev_stash0, cvf[par]])
                e2_ = P.add(V, lambda v, co=co, m=m, pfull=pfull: v.tensor_scalar(
                    co[:, 3:1024], pfull[:, 0:1021], B["cw"][:, m, 0:1], None, ALU.mult))
                P.add(V, lambda v, co=co, m=m: v.scalar_tensor_tensor(
                    co[:, 0:2], B["stash"][:, m, 1:3], B["cw"][:, m, 1:2], co[:, 0:2],
                    ALU.mult, ALU.add))
                P.add(V, lambda v, co=co, m=m, pfull=pfull: v.scalar_tensor_tensor(
                    co[:, 2:1024], pfull[:, 0:1022], B["cw"][:, m, 1:2], co[:, 2:1024],
                    ALU.mult, ALU.add))
                P.add(V, lambda v, co=co, m=m: v.scalar_tensor_tensor(
                    co[:, 0:1], B["stash"][:, m, 2:3], B["cw"][:, m, 2:3], co[:, 0:1],
                    ALU.mult, ALU.add))
                P.add(V, lambda v, co=co, m=m, pfull=pfull: v.scalar_tensor_tensor(
                    co[:, 1:1024], pfull[:, 0:1023], B["cw"][:, m, 2:3], co[:, 1:1024],
                    ALU.mult, ALU.add))
                evc = P.add(V, lambda v, co=co, m=m, pfull=pfull: v.scalar_tensor_tensor(
                    co[:, 0:1024], pfull[:, 0:1024], B["cw"][:, m, 3:4], co[:, 0:1024],
                    ALU.mult, ALU.add))
                ev_sth = P.add(V, lambda v, m=m, pfull=pfull: v.tensor_copy(
                    B["stash"][:, m, :], pfull[:, 1021:1024]))
                slot_free[p3] = [evc, ev_sth]
                if m < 32:
                    dst = B["xy"][:, m, :]
                elif m == 32:
                    dst = B["B_fm"][:, :]
                else:
                    dst = B["C_fm"][:, :]
                evx = P.add(A, lambda a, dst=dst, m=m, co=co: a.activation(
                    dst, co[:, :], AF.Silu, bias=B["cb"][:, m:m + 1]),
                    waits=[evc])
                cvf[par] = evx
                ev_x_ready = [evx]
            else:
                eva = P.add(A, lambda a, pb=pb: a.copy(B["dtraw"][0:64, 0:512], pb[0][0:64, :]),
                            waits=[ev_mm])
                eva2 = P.add(A, lambda a, pb=pb: a.copy(B["dtraw"][0:64, 512:1024], pb[1][0:64, :]))
                slot_free[p3] = [eva2]
                ev1 = P.add(V, lambda v: v.tensor_scalar(B["dts1"][0:64, :], B["dtraw"][0:64, :],
                                                         B["dtbias"][:, 0:1], None, ALU.add),
                            waits=[eva2])
                P.add(A, lambda a: a.activation(B["dts2"][0:64, :], B["dts1"][0:64, :], AF.Abs),
                      waits=[ev1])
                P.add(A, lambda a: a.activation(B["dts2"][0:64, :], B["dts2"][0:64, :], AF.Exp, scale=-1.0))
                ev4 = P.add(A, lambda a: a.activation(B["dts2"][0:64, :], B["dts2"][0:64, :], AF.Ln, bias=1.0))
                P.add(V, lambda v: v.tensor_scalar_max(B["dts1"][0:64, :], B["dts1"][0:64, :], 0.0),
                      waits=[ev4])
                ev6 = P.add(V, lambda v: v.tensor_tensor(B["dt_hm"][0:64, :], B["dts1"][0:64, :], B["dts2"][0:64, :], ALU.add))
                ev_dt_ready = [ev6]

        m1bar = P.barrier()
        P.add(V, lambda v: v.memset(B["z64"][:, :], 0.0), waits=m1bar)
        sevs = []
        for i2 in range(2):
            sevs.append(P.add(V, lambda v, i2=i2: v.memset(B["stageL"][i2][0:1, :], 1.0)))
        ev_o2 = P.add(V, lambda v: v.memset(B["ones512"][0:1, :], 1.0))
        for i2 in range(2):
            sevs.append(P.dma(G, B["stageR"][i2][1:2, :], B["ones512"][0:1, :],
                              waits=[ev_o2, sst["stage_free"][i2]], sem=f"stg{i2}"))
        sst["stage_ok"] = sevs + [P.last(V)]
        sst["stage_free"] = [sevs[-2], sevs[-1]]
        sst["pY_free"] = [slot_free[0][0], slot_free[0][0]]
        sst["pdiff_free"] = [slot_free[1][0], slot_free[1][0]]
        sst["psh_free"] = slot_free[2][0]
        sst["psmall_free"] = slot_free[2][0]
        r = None
        sst["pw3_free"] = pfree["p_w3"]
        for c in range(8):
            if seg == 0:
                emit_gamma_batch(2)
            r = emit_ssd_chunk(P, B, c, {"x": ev_x_ready, "BC": ev_x_ready,
                                         "dt": ev_dt_ready}, sst)
        pfree["p_w3"] = sst["pw3_free"]
        ev_ssd_done = r["y"] + r["states"]

        m2bar = P.barrier()
        slot_free = {0: [sst["pY_free"][0]], 1: [sst["pdiff_free"][0]]}
        ev_rms_mm = None
        ev_yg_prev = None
        ev_rms_prev = None
        ev_ygs = []
        ev_sqs = []
        slabs3b = [B["wslab@m3"][0], B["wslab@m3"][1], B["wslab3"]]
        for ec in range(32):
            par = ec % 3
            par2 = ec % 2
            if seg == 0:
                emit_gamma_batch(2)
            src = d_win[ec, :, :, :].rearrange("p a b -> p (a b)")
            evd = P.dma(G, slabs3b[par][:, :], src,
                        waits=[wfree[par]] + (m2bar if ec < 3 else []), sem=f"dwin{par}")
            pb = B["p_Y"] if par2 == 0 else B["p_diff"]
            ev_mm = None
            for lh in range(2):
                for k in range(16):
                    w = []
                    if lh == 0 and k == 0:
                        w = [evd] + slot_free[par2] + (ev_ssd_done if ec < 2 else [])
                    ev_mm = P.add(T, lambda t, sl3=slabs3b[par], lh=lh, k=k, pb=pb: t.matmul(
                        pb[lh][:, :], sl3[:, 128 * k:128 * k + 128],
                        B["h1"][:, k, 512 * lh:512 * lh + 512],
                        start=(k == 0), stop=(k == 15)), waits=w)
            wfree[par] = ev_mm
            pfull = B["p_ip"][par2]
            evs2 = P.add(A, lambda a, pfull=pfull: a.activation(B["sz@m3"][:, :], pfull[:, :], AF.Silu),
                         waits=[ev_mm, ev_yg_prev])
            slot_free[par2] = [evs2]
            evyg = P.add(V, lambda v, ec=ec: v.tensor_tensor(B["xy"][:, ec, :], B["xy"][:, ec, :], B["sz@m3"][:, :], ALU.mult),
                         waits=[evs2] + (ev_ssd_done if ec < 2 else []))
            ev_yg_prev = evyg
            ev_ygs.append(evyg)
            if ec >= 2:
                ecl = ec - 2
                sqb_l = B["sqb@m3"] if ecl % 2 == 0 else B["sqb@m0"]
                for lh in range(2):
                    ev_rms_mm = P.add(T, lambda t, ecl=ecl, lh=lh, sqb_l=sqb_l: t.matmul(
                        (B["p_sh"] if lh == 0 else B["p_small"])[:, :], B["o128b"][:, :],
                        sqb_l[:, 512 * lh:512 * lh + 512],
                        start=(ecl == 0), stop=False),
                        waits=[ev_sqs[ecl]] + ([sst["psh_free"], sst["psmall_free"]] if ecl == 0 else []))
                ev_rms_prev = ev_rms_mm
            sqbuf = B["sqb@m3"] if ec % 2 == 0 else B["sqb@m0"]
            ev_sqs.append(P.add(A, lambda a, ec=ec, sqbuf=sqbuf: a.activation(
                sqbuf[:, :], B["xy"][:, ec, :], AF.Square),
                waits=[evyg, ev_rms_prev]))
        m3bar = P.barrier()
        for ecl in (30, 31):
            sqb_l = B["sqb@m3"] if ecl % 2 == 0 else B["sqb@m0"]
            for lh in range(2):
                ev_rms_mm = P.add(T, lambda t, ecl=ecl, lh=lh, sqb_l=sqb_l: t.matmul(
                    (B["p_sh"] if lh == 0 else B["p_small"])[:, :], B["o128b"][:, :],
                    sqb_l[:, 512 * lh:512 * lh + 512],
                    start=False, stop=(ecl == 31)),
                    waits=[ev_sqs[ecl]])
        ev1 = P.add(V, lambda v: v.tensor_scalar(B["stat2@m3"][:, 0:512], B["p_sh"][:, :], 1.0 / DI, float(EPS_RMS), ALU.mult, ALU.add),
                    waits=[ev_rms_mm])
        ev2 = P.add(V, lambda v: v.tensor_scalar(B["stat2@m3"][:, 512:1024], B["p_small"][:, :], 1.0 / DI, float(EPS_RMS), ALU.mult, ALU.add))
        pfree["p_sh"] = ev2
        pfree["p_small"] = ev2
        P.add(A, lambda a: a.activation(B["stat2@m3"][:, :], B["stat2@m3"][:, :], AF.Ln), waits=[ev2])
        ev_rms = P.add(A, lambda a: a.activation(B["stat2@m3"][:, :], B["stat2@m3"][:, :], AF.Exp, scale=-0.5))
        ev_yn_done = [ev_ygs[-1]]

        wfree3 = list(wfree)
        if seg == 0:
            emit_gamma_fold()

        wofree = [None, None]
        ufree = [None, None]
        ev_x1k_m4 = []
        slot_free = {0: [slot_free[0][0]], 1: [slot_free[1][0]],
                     2: [pfree["p_sh"], pfree["p_small"]]}
        ev_x1w = None
        for dc in range(16):
            par = dc % 2
            src = d_wout[dc, :, :, :].rearrange("p a b -> p (a b)")
            evd = P.dma(G, B["woslab"][par][:, :], src,
                        waits=[wofree[par]] + (m3bar if dc < 2 else []), sem=f"dwo{par}")
            nw_ap = B["nw"][:, :].unsqueeze(2).broadcast_to([128, 32, 128])
            evd = P.add(V, lambda v, par=par, nw_ap=nw_ap: v.tensor_tensor(
                B["woslab"][par][:, :].rearrange("p (a b) -> p a b", b=128), B["woslab"][par][:, :].rearrange("p (a b) -> p a b", b=128),
                nw_ap, ALU.mult), waits=[evd])
            uev = P.dma(S, B["uscr"][par][:, :], d_u[128 * dc:128 * dc + 128, l0:l0 + SEGL],
                        waits=[ufree[par]] + (m3bar if dc < 2 else [])
                        + ([ev_gamma_done[0]] if dc < 2 and seg == 0 else []),
                        sem=f"dus{par}")
            p3w = dc % 3
            pfullw = B["p_ip"][p3w]
            pb = [pfullw[:, 0:512], pfullw[:, 512:1024]]
            ev_mm = None
            for lh in range(2):
                for ec in range(32):
                    w = []
                    if lh == 0 and ec == 0:
                        w = [evd] + slot_free[p3w] + ev_yn_done
                    ev_mm = P.add(T, lambda t, par=par, lh=lh, ec=ec, pb=pb: t.matmul(
                        pb[lh][:, :], B["woslab"][par][:, 128 * ec:128 * ec + 128],
                        B["xy"][:, ec, 512 * lh:512 * lh + 512],
                        start=(ec == 0), stop=(ec == 31)), waits=w)
            wofree[par] = ev_mm
            et1 = P.add(V, lambda v, pfullw=pfullw: v.tensor_tensor(
                B["tmpX"][:, :], pfullw[:, :], B["stat2@m3"][:, :], ALU.mult),
                waits=[ev_mm, ev_rms] + ([ev_gamma_done[0]] if dc < 2 and seg == 0 else []))
            e2 = P.add(V, lambda v, dc=dc, par=par: v.tensor_tensor(
                B["X1"][:, dc, :], B["tmpX"][:, :], B["uscr"][par][:, :], ALU.add),
                waits=[uev])
            slot_free[p3w] = [e2]
            ufree[par] = e2
            ev_x1k_m4.append(e2)
            ev_x1w = e2
        pfree["p_Y0"] = pfree["p_Y1"] = slot_free[0][0]
        pfree["p_diff0"] = pfree["p_diff1"] = slot_free[1][0]
        pfree["p_sh"] = pfree["p_small"] = slot_free[2][0]

        ev_rms_mm_guard = [ev_rms_mm] + P.barrier()
        ev_smm = ev_umm = None
        smm_by_k = []
        for k in range(16):
            sqbuf = B["sqb@m5"] if k % 2 == 0 else B["sqb@m0"]
            evq = P.add(A, lambda a, k=k, sqbuf=sqbuf: a.activation(sqbuf[:, :], B["X1"][:, k, :], AF.Square),
                        waits=[ev_x1k_m4[k]] + ([smm_by_k[k - 2]] if k >= 2 else [])
                        + (ev_rms_mm_guard if k < 2 else []))
            for lh in range(2):
                ev_smm = P.add(T, lambda t, k=k, lh=lh, sqbuf=sqbuf: t.matmul(
                    B["p_diff"][lh][:, :], B["o128b"][:, :], sqbuf[:, 512 * lh:512 * lh + 512],
                    start=(k == 0), stop=(k == 15)),
                    waits=[evq] + ([pfree[f"p_diff{lh}"]] if k == 0 else []))
            smm_by_k.append(ev_smm)
        for k in range(16):
            for lh in range(2):
                ev_umm = P.add(T, lambda t, k=k, lh=lh: t.matmul(
                    B["p_Y"][lh][:, :], B["o128b"][:, :], B["X1"][:, k, 512 * lh:512 * lh + 512],
                    start=(k == 0), stop=(k == 15)),
                    waits=[ev_x1k_m4[k]] + ([pfree["p_Y0"], pfree["p_Y1"]] if k == 0 else []))
        P.add(A, lambda a: a.mul(B["stat1@m5"][:, :], B["p_ip"][0][:, :], 1.0 / DM),
              waits=[ev_umm, ev_smm])
        ev_v2 = P.add(A, lambda a: a.mul(B["stat2@m5"][:, :], B["p_ip"][1][:, :], 1.0 / DM))
        pfree["p_Y0"] = pfree["p_Y1"] = pfree["p_diff0"] = pfree["p_diff1"] = ev_v2
        evm = P.add(A, lambda a: a.activation(B["sz@m5"][:, :], B["stat1@m5"][:, :], AF.Square),
                    waits=[ev_v2])
        evv = P.add(V, lambda v: v.scalar_tensor_tensor(B["stat2@m5"][:, :], B["stat2@m5"][:, :], float(EPS_LN), B["sz@m5"][:, :], ALU.add, ALU.subtract),
                    waits=[evm])
        P.add(A, lambda a: a.activation(B["stat2@m5"][:, :], B["stat2@m5"][:, :], AF.Ln),
              waits=[evv])
        ev_rstd = P.add(A, lambda a: a.activation(B["stat2@m5"][:, :], B["stat2@m5"][:, :], AF.Exp, scale=-0.5))
        wh2 = list(wfree) + [wofree[0], wofree[1]]
        ev_h2k = []
        for k in range(16):
            evs_ = P.add(V, lambda v, k=k: v.tensor_tensor(B["convo@m5"][k % 2][:, :], B["X1"][:, k, :], B["stat1@m5"][:, :], ALU.subtract),
                         waits=[ev_rstd] + (wh2 if k == 0 else []))
            ev_h2k.append(P.add(V, lambda v, k=k: v.tensor_tensor(B["h1"][:, k, :], B["convo@m5"][k % 2][:, :], B["stat2@m5"][:, :], ALU.mult)))
        ev_h2done = [ev_h2k[-1]]

        m5bar = P.barrier()
        rounds = [list(range(i, min(i + FFN_R, N_HCH))) for i in range(0, N_HCH, FFN_R)]
        w1f = [None, None]
        w2f = [None, None]
        w3f = [None, None]
        q1f = [None, None]
        q2f = [None, None]
        q3lf = [None] * FFN_R
        hidf = [None] * FFN_R
        x1pf = {0: pfree["p_Y0"], 1: pfree["p_Y1"]}
        x2pf = {0: pfree["p_diff0"], 1: pfree["p_diff1"]}
        w3pf = [pfree["p_w3"], pfree["p_sh"]]
        sigf = None
        qtmpf = [None, None]
        qti = [0]
        outf = [None, None]
        mam_bar = P.barrier()
        ev_last_w3 = None
        for ri, rnd in enumerate(rounds):
            hid_ready = [None] * len(rnd)
            q3_ready = [None] * len(rnd)
            for li, hc in enumerate(rnd):
                par = li % 2

                def quant(dslab, wslabs, qbufs, tcol, wf, qf, extra, dcache):
                    if seg != 0:
                        dq = P.dma(S, qbufs[par][:, :], dcache[hc, :, :],
                                   waits=[qf[0], qf[1], qcache_done.get(tcol)] + extra,
                                   sem=f"dq{tcol}{par}")
                        wf[par] = dq
                        return dq
                    d_ = P.dma(S, wslabs[par][:, :], dslab,
                               waits=[wf[0], wf[1]] + extra, sem=f"dwf{tcol}{par}")
                    qi = qti[0] % 2
                    qti[0] += 1
                    qb = B["qtmp"] if qi == 0 else B["qtmp2"]
                    if tcol == 1:
                        eb = P.add(V, lambda v, q=qbufs[par], w=wslabs[par], tc=tcol: v.tensor_scalar(
                            q[:, :], w[:, :], B["tneg"][:, tc:tc + 1], None, ALU.is_lt),
                            waits=[d_, qf[0], qf[1], qcache_done.get(tcol), ev_gamma_done[0]])
                        ea2 = P.add(V, lambda v, w=wslabs[par], qb=qb, tc=tcol: v.tensor_scalar(
                            qb[:, :], w[:, :], B["tpp"][:, tc:tc + 1], None, ALU.is_gt),
                            waits=[qtmpf[qi]])
                        eq = P.add(V, lambda v, q=qbufs[par], qb=qb: v.tensor_tensor(
                            q[:, :], qb[:, :], q[:, :], ALU.subtract))
                    else:
                        ea = P.add(A, lambda a, w=wslabs[par], qb=qb, tc=tcol: a.activation(
                            qb[:, :], w[:, :], AF.Sign, bias=B["tneg"][:, tc:tc + 1]),
                            waits=[d_, qtmpf[qi], ev_gamma_done[0]])
                        eb = P.add(A, lambda a, q=qbufs[par], w=wslabs[par], tc=tcol: a.activation(
                            q[:, :], w[:, :], AF.Sign, bias=B["tpp"][:, tc:tc + 1]),
                            waits=[qf[0], qf[1], qcache_done.get(tcol)])
                        eq = P.add(V, lambda v, q=qbufs[par], qb=qb: v.tensor_tensor(
                            q[:, :], q[:, :], qb[:, :], ALU.add),
                            waits=[ea, eb])
                    qtmpf[qi] = eq
                    wf[par] = eq
                    qcache_done[tcol] = P.dma(G, dcache[hc, :, :], qbufs[par][:, :],
                                              waits=[eq], sem=f"dqc{tcol}")
                    return eq
                eq1 = quant(d_w1[hc, :, :, :].rearrange("p a b -> p (a b)"), B["w1slab"],
                            B["q1"], 0, w1f, q1f, (m5bar if ri == 0 and li < 2 else []),
                            d_q1s)
                eq2 = quant(d_w2[hc, :, :, :].rearrange("p a b -> p (a b)"), B["w2slab"],
                            B["q2"], 1, w2f, q2f, [w3f[0], w3f[1]] + (m5bar if ri == 0 and li < 2 else []),
                            d_q2s)
                ex1 = ex2 = None
                for lh in range(2):
                    for k in range(16):
                        w = [ev_h2k[k]]
                        if lh == 0 and k == 0:
                            w = w + [eq1, x1pf[0], x1pf[1]]
                        ex1 = P.add(T, lambda t, par=par, lh=lh, k=k: t.matmul(
                            B["p_Y"][lh][:, :], B["q1"][par][:, 128 * k:128 * k + 128],
                            B["h1"][:, k, 512 * lh:512 * lh + 512],
                            start=(k == 0), stop=(k == 15)), waits=w)
                for lh in range(2):
                    for k in range(16):
                        w = []
                        if lh == 0 and k == 0:
                            w = [eq2, x2pf[0], x2pf[1]]
                        ex2 = P.add(T, lambda t, par=par, lh=lh, k=k: t.matmul(
                            B["p_diff"][lh][:, :], B["q2"][par][:, 128 * k:128 * k + 128],
                            B["h1"][:, k, 512 * lh:512 * lh + 512],
                            start=(k == 0), stop=(k == 15)), waits=w)
                q1f[0] = q1f[1] = ex1
                q2f[0] = q2f[1] = ex2
                es2 = P.add(A, lambda a: a.activation(B["sig"][:, :], B["p_ip"][0][:, :], AF.Sigmoid, scale=0.5),
                            waits=[ex1, sigf])
                x1pf = {0: es2, 1: es2}
                eh2 = P.add(V, lambda v, li=li: v.tensor_tensor(
                    B["hid"][:, li, :], B["sig"][:, :], B["p_ip"][1][:, :], ALU.mult),
                    waits=[es2, ex2, hidf[li]] + (mam_bar if ri == 0 and li < 2 else []))
                x2pf = {0: eh2, 1: eh2}
                sigf = eh2
                hid_ready[li] = eh2
                if seg == 0:
                    d3 = P.dma(S, B["w3slab"][par][:, :], d_w3[hc, :, :],
                               waits=[w3f[0], w3f[1], eq2], sem=f"dw3{par}")
                    qi = qti[0] % 2
                    qti[0] += 1
                    qb = B["qtmp"] if qi == 0 else B["qtmp2"]
                    ec_ = P.add(A, lambda a, par=par, qb=qb: a.activation(
                        qb[:, :], B["w3slab"][par][:, :], AF.Sign, bias=B["tneg"][:, 2:3]),
                        waits=[d3, qtmpf[qi]])
                    eb3 = P.add(A, lambda a, par=par, li=li: a.activation(
                        B["q3l"][:, li, :], B["w3slab"][par][:, :], AF.Sign, bias=B["tpp"][:, 2:3]),
                        waits=[q3lf[li], qcache_done.get(2)] + (mam_bar if ri == 0 and li < 2 else []))
                    eq3 = P.add(V, lambda v, li=li, qb=qb: v.tensor_tensor(
                        B["q3l"][:, li, :], B["q3l"][:, li, :], qb[:, :], ALU.add),
                        waits=[ec_, eb3])
                    qtmpf[qi] = eq3
                    w3f[par] = eq3
                    qcache_done[2] = P.dma(G, d_q3s[hc, :, :], B["q3l"][:, li, :],
                                           waits=[eq3], sem="dqc2")
                else:
                    eq3 = P.dma(S, B["q3l"][:, li, :], d_q3s[hc, :, :],
                                waits=[q3lf[li], qcache_done.get(2)] + (mam_bar if ri == 0 and li < 2 else []),
                                sem=f"dq3l{par}")
                q3_ready[li] = eq3
            for dc in range(16):
                for lh in range(2):
                    pwb = B["p_w3"] if (dc * 2 + lh) % 2 == 0 else B["p_sh"]
                    pw_i = (dc * 2 + lh) % 2
                    ev_mm = None
                    for li in range(len(rnd)):
                        w = []
                        if li == 0:
                            w = [w3pf[pw_i], q3_ready[-1], hid_ready[-1]] + (
                                [q3_ready[-2]] if len(rnd) > 1 else [])
                        ev_mm = P.add(T, lambda t, dc=dc, lh=lh, li=li, pwb=pwb,
                                      st_=(li == 0), sp_=(li == len(rnd) - 1): t.matmul(
                            pwb[:, :], B["q3l"][:, li, 128 * dc:128 * dc + 128],
                            B["hid"][:, li, 512 * lh:512 * lh + 512],
                            start=st_, stop=sp_), waits=w)
                    ev_last_w3 = ev_mm
                    if ri < len(rounds) - 1:
                        evac = P.add(V, lambda v, dc=dc, lh=lh, pwb=pwb: v.scalar_tensor_tensor(
                            B["X1"][:, dc, 512 * lh:512 * lh + 512], pwb[:, :], 0.5,
                            B["X1"][:, dc, 512 * lh:512 * lh + 512], ALU.mult, ALU.add),
                            waits=[ev_mm])
                    else:
                        sp = (dc * 2 + lh) % 2
                        evac = P.add(V, lambda v, dc=dc, lh=lh, pwb=pwb, sp=sp: v.scalar_tensor_tensor(
                            B["outstg"][sp][:, :], pwb[:, :], 0.5,
                            B["X1"][:, dc, 512 * lh:512 * lh + 512], ALU.mult, ALU.add),
                            waits=[ev_mm, outf[sp]])
                        odst = d_out[128 * dc:128 * dc + 128, l0 + 512 * lh:l0 + 512 * lh + 512]
                        oev = P.dma(S, odst, B["outstg"][sp][:, :], waits=[evac], sem=f"dout{sp}")
                        outf[sp] = oev
                        out_dma_evs.append(oev)
                    w3pf[pw_i] = evac
            for li in range(len(rnd)):
                hidf[li] = ev_last_w3
                q3lf[li] = ev_last_w3
        pfree["p_w3"] = w3pf[0]
        pfree["p_sh"] = w3pf[1]
        pfree["p_Y0"] = pfree["p_Y1"] = x1pf[0]
        pfree["p_diff0"] = pfree["p_diff1"] = x2pf[0]

    P.add(G, lambda g: g.engine_nop(), waits=out_dma_evs)
    P.emit()
    return nc


_NC_CACHE = [None]


def _host_layout(inputs):
    f = np.float32
    W_in = np.asarray(inputs["L0_attn_W_in"], f)
    W_out = np.asarray(inputs["L0_attn_W_out"], f)
    w1 = np.asarray(inputs["L0_ffn_w1"], f)
    w2 = np.asarray(inputs["L0_ffn_w2"], f)
    w3 = np.asarray(inputs["L0_ffn_w3"], f)
    conv_w = np.asarray(inputs["L0_attn_conv_w"], f)
    conv_b = np.asarray(inputs["L0_attn_conv_b"], f)
    nw = np.asarray(inputs["L0_attn_norm_weight"], f)
    dtb = np.asarray(inputs["L0_attn_dt_bias"], f)
    alog = np.asarray(inputs["L0_attn_A_log"], f)
    Dv = np.asarray(inputs["L0_attn_D"], f)

    WinT = np.ascontiguousarray(W_in.T)
    WinT_p = np.zeros((DM, N_ECH * 128), f)
    WinT_p[:, :8512] = WinT
    winT_t = np.ascontiguousarray(
        WinT_p.reshape(16, 128, N_ECH, 128).transpose(2, 1, 0, 3))
    woutT_t = np.ascontiguousarray(
        W_out.T.reshape(32, 128, 16, 128).transpose(2, 1, 0, 3))
    w1T_t = np.ascontiguousarray(
        w1.T.reshape(16, 128, N_HCH, 128).transpose(2, 1, 0, 3))
    w2T_t = np.ascontiguousarray(
        w2.T.reshape(16, 128, N_HCH, 128).transpose(2, 1, 0, 3))
    w3T_t = np.ascontiguousarray(w3.T.reshape(N_HCH, 128, 2048))
    cw_pp = np.ascontiguousarray(conv_w.reshape(34, 128, 4).transpose(1, 0, 2))
    cb_pp = np.ascontiguousarray(conv_b.reshape(34, 128).T)
    nw_pp = np.ascontiguousarray(nw.reshape(32, 128).T)
    dd32 = np.zeros((128, 32, 128), f)
    for i in range(32):
        for p_ in range(64):
            dd32[p_, i, p_] = Dv[2 * i]
            dd32[64 + p_, i, 64 + p_] = Dv[2 * i + 1]
    consts = dict(
        winT_t=winT_t, woutT_t=woutT_t, w1T_t=w1T_t, w2T_t=w2T_t, w3T_t=w3T_t,
        cw_pp=cw_pp, cb_pp=cb_pp, nw_pp=nw_pp,
        dtb=dtb.reshape(64, 1), alog=alog.reshape(64, 1), dd32=dd32,
        idn=np.eye(128, dtype=bfdt),
        trilT=(np.arange(128)[:, None] <= np.arange(128)[None, :]).astype(bfdt),
        o1f=np.ones((1, 128), f), o1b=np.ones((1, 128), bfdt),
        o128b=np.ones((128, 128), bfdt), o128f=np.ones((128, 128), f),
        o128f1=np.ones((128, 1), f),
    )
    return consts


def kernel(**inputs):
    consts = _host_layout(inputs)
    u = np.asarray(inputs["L0_attn_u"], np.float32)
    if _NC_CACHE[0] is None:
        _NC_CACHE[0] = build_program()
    nc = _NC_CACHE[0]
    in_maps = []
    for b in range(8):
        m = dict(consts)
        m["u_fm"] = np.ascontiguousarray(u[b].T)
        in_maps.append(m)
    res = run_bass_kernel_spmd(nc, in_maps, core_ids=list(range(8)))
    out = np.stack([np.asarray(res.results[b]["out_fm"]).T for b in range(8)])
    return np.ascontiguousarray(out, dtype=np.float32)


if __name__ == "__main__":
    nc = build_program()
    print("build ok, instructions:", nc.n_instructions())


# version 24
# speedup vs baseline: 1.2066x; 1.0065x over previous
import numpy as np
import ml_dtypes
import concourse.bass as bass
import concourse.mybir as mybir
from concourse.bass_utils import run_bass_kernel_spmd

F32 = mybir.dt.float32
BF16 = mybir.dt.bfloat16
AF = mybir.ActivationFunctionType
ALU = mybir.AluOpType
AX = mybir.AxisListType
bfdt = ml_dtypes.bfloat16

V, A, T, G, S = "vector", "scalar", "tensor", "gpsimd", "sync"
sV, sA, sT, sG, sS = "sV", "sA", "sT", "sGc", "sS"

SEGL = 1024
NSEG = 2
DM = 2048
DI = 4096
HFF = 5632
N_ECH = 67
N_HCH = 44
FFN_R = 8
EPS_LN = 1e-5
EPS_RMS = 1e-5
GAMMA_EPS = 1e-8
NW1 = float(HFF * DM)


class Pipe:
    ENGINES = ("sync", "scalar", "vector", "tensor", "gpsimd")
    CHAINED = {"vector": "sV", "scalar": "sA", "gpsimd": "sGc"}

    def __init__(self, nc):
        self.nc = nc
        self.ops = {e: [] for e in self.ENGINES}
        self.sem_names = []
        self.counts = {}
        self._last_compute = {}

    def new_sem(self, name):
        if name in self.counts:
            return name
        self.sem_names.append(name)
        self.counts[name] = 0
        return name

    def add(self, engine, fn, waits=(), inc=None, amt=1, _is_dma=False):
        ev = None
        waits = [w for w in waits if w is not None]
        if not _is_dma and engine in self.CHAINED:
            if inc is None:
                inc = self.CHAINED[engine]
            prev = self._last_compute.get(engine)
            if prev is not None:
                waits = waits + [prev]
        if engine == "tensor" and inc is None:
            inc = "sT"
        if inc is not None:
            self.new_sem(inc)
            self.counts[inc] += amt
            ev = (inc, self.counts[inc])
        if not _is_dma and engine in self.CHAINED:
            self._last_compute[engine] = ev
        self.ops[engine].append((fn, tuple(waits), inc, amt))
        return ev

    def dma(self, engine, out, in_, waits=(), sem=None, **kw):
        assert sem is not None
        self.new_sem(sem)
        return self.add(engine,
                        lambda eng, o=out, i=in_: eng.dma_start(out=o, in_=i, **kw),
                        waits=waits, inc=sem, amt=16, _is_dma=True)

    def last(self, engine):
        if engine == "tensor":
            n = self.counts.get("sT", 0)
            return ("sT", n) if n else None
        return self._last_compute.get(engine)

    def barrier(self):
        return [e for e in (self.last(V), self.last(A), self.last(T), self.last(G))
                if e is not None]

    def check(self):
        idx = {e: 0 for e in self.ENGINES}
        cnt = {s: 0 for s in self.sem_names}
        progress = True
        while progress:
            progress = False
            for e in self.ENGINES:
                while idx[e] < len(self.ops[e]):
                    fn, waits, inc, amt = self.ops[e][idx[e]]
                    if all(cnt[s] >= v for s, v in waits):
                        if inc is not None:
                            cnt[inc] += amt
                        idx[e] += 1
                        progress = True
                    else:
                        break
        stuck = {e: (idx[e], len(self.ops[e])) for e in self.ENGINES
                 if idx[e] < len(self.ops[e])}
        if stuck:
            detail = {}
            for e in stuck:
                fn, waits, inc, amt = self.ops[e][idx[e]]
                detail[e] = [(s, v, cnt[s]) for s, v in waits if cnt[s] < v]
            raise RuntimeError(f"Pipe deadlock: stuck={stuck} waits={detail}")

    def emit(self):
        self.check()
        nc = self.nc
        sems = {}
        import contextlib
        with contextlib.ExitStack() as stck:
            for s in self.sem_names:
                sems[s] = stck.enter_context(nc.semaphore(s))
            block = stck.enter_context(nc.Block())
            for e in self.ENGINES:
                if not self.ops[e]:
                    continue
                ops = self.ops[e]

                def section(eng, ops=ops, ename=e):
                    watermark = {}
                    for fn, waits, inc, amt in ops:
                        agg = {}
                        for s, v in waits:
                            if v > agg.get(s, -1):
                                agg[s] = v
                        for s, v in agg.items():
                            if watermark.get(s, -1) < v:
                                eng.wait_ge(sems[s], v)
                                watermark[s] = v
                        ins = fn(eng)
                        if inc is not None:
                            assert ins is not None, f"None instr on {ename}"
                            ins.then_inc(sems[inc], amt)
                getattr(block, e)(section)


class Overlay:
    def __init__(self, nc, base, size, tag):
        self.nc, self.base, self.size, self.tag = nc, base, size, tag
        self.off = 0
        self.n = 0

    def reset(self):
        self.off = 0

    def alloc(self, name, shape, dtype=F32):
        per_part = int(np.prod(shape[1:])) * mybir.dt.size(dtype)
        per_part = (per_part + 31) // 32 * 32
        assert self.off + per_part <= self.size, \
            f"overlay {self.tag} overflow: {name} {self.off}+{per_part}>{self.size}"
        self.n += 1
        t = self.nc.alloc_sbuf_tensor_at(f"{self.tag}{self.n}_{name}", shape, dtype,
                                         offset=self.base + self.off)
        self.off += per_part
        return t



def ssd_init(P, B, ev_consts):
    ev_dd = P.add(V, lambda v: v.tensor_copy(B["Ddiag"][:, :, :], B["Ddiag32"][:, :, :]),
                  waits=list(ev_consts))
    ev_s0 = P.add(V, lambda v: v.memset(B["STATES_bf"][:, :], 0.0))
    st = {
        "stage_ok": [ev_dd],
        "states_bf_ready": [ev_s0],
        "psh_free": None, "pY_free": [None, None], "pdiff_free": [None, None],
        "stage_free": [None, None], "dmin_free": [None, None], "E_free": [None, None],
        "ST_free": [None] * 4, "CE_free": [None] * 4,
        "pxT_free": None, "psmall_free": None, "cdrow_free": None,
        "xdt_freeT": None, "xdtd_freeT": None, "dtbf_freeT": None,
        "decaybf_freeT": None, "Btm_freeT": None, "cdb_freeT": None,
        "ST_CE_init": ev_dd,
    }
    return st


def emit_ssd_chunk(P, B, c, ev_ready, st):
    lo = c * 128
    sl = slice(lo, lo + 128)

    P.add(V, lambda v: v.tensor_scalar_mul(B["dA"][:, :], B["dt_hm"][:, sl], B["A_col"][:, 0:1]),
          waits=list(ev_ready["dt"]))
    ev_acs = P.add(V, lambda v: v.tensor_tensor_scan(B["Acs"][:, :], B["dA"][:, :], B["z64"][:, :], 0.0, ALU.add, ALU.add))
    ev_negacs = P.add(V, lambda v: v.tensor_scalar_mul(B["negAcs"][:, :], B["Acs"][:, :], -1.0))
    ev_tdec = P.add(V, lambda v: v.tensor_scalar(B["tdec"][:, :], B["Acs"][:, :], B["Acs"][:, 127:128], None, ALU.subtract))
    ev_dtbf = P.add(V, lambda v: v.tensor_copy(B["dtbf"][:, :], B["dt_hm"][:, sl]),
                    waits=[st["dtbf_freeT"]])
    ev_dec = P.add(A, lambda a: a.activation(B["decay"][:, :], B["tdec"][:, :], AF.Exp, scale=-1.0),
                   waits=[ev_tdec])
    ev_eacs = P.add(A, lambda a: a.activation(B["eacs"][:, :], B["Acs"][:, :], AF.Exp),
                    waits=[ev_acs])
    ev_cd = P.add(A, lambda a: a.activation(B["cd"][:, 0:1], B["Acs"][:, 127:128], AF.Exp),
                  waits=[ev_acs])
    ev_decbf = P.add(V, lambda v: v.tensor_copy(B["decaybf"][:, :], B["decay"][:, :]),
                     waits=[ev_dec, st["decaybf_freeT"]])

    ev_t1 = P.add(T, lambda t: t.transpose(B["p_xT"][:, 0:64], B["dtbf"][:, :], B["idn"][0:64, 0:64]),
                  waits=[st["pxT_free"], ev_dtbf])
    P.add(T, lambda t: t.transpose(B["p_xT"][:, 64:128], B["decaybf"][:, :], B["idn"][0:64, 0:64]),
          waits=[ev_decbf])
    ev_t3 = P.add(T, lambda t: t.transpose(B["p_xT"][:, 128:256], B["B_fm"][:, sl], B["idn"][:, :]),
                  waits=list(ev_ready["BC"]) + [st["Btm_freeT"]])
    ev_dtT = P.add(V, lambda v: v.tensor_copy(B["dt_tm"][:, :], B["p_xT"][:, 0:64]), waits=[ev_t3])
    ev_decT = P.add(V, lambda v: v.tensor_copy(B["decay_tm"][:, :], B["p_xT"][:, 64:128]))
    ev_BT = P.add(V, lambda v: v.tensor_copy(B["B_tm"][:, :], B["p_xT"][:, 128:256]))

    ev_g = P.add(T, lambda t: t.matmul(B["p_small"][:, 0:128], B["B_fm"][:, sl], B["C_fm"][:, sl],
                                       start=True, stop=True),
                 waits=list(ev_ready["BC"]) + [st["psmall_free"]])
    ev_Gm = P.add(V, lambda v: v.tensor_tensor(B["Gm"][:, :], B["p_small"][:, 0:128], B["tril"][:, :], ALU.mult),
                  waits=[ev_g, st["ST_CE_init"]])

    ev_cdrow = P.dma(G, B["cd_row"][0:1, :], B["cd"][:, 0:1],
                     waits=[ev_cd, st.get("cdrow_free")], sem="cdr")
    ev_cdmm = P.add(T, lambda t: t.matmul(B["p_small"][:, 192:256], B["ones1_f32"][0:1, :], B["cd_row"][0:1, :],
                                          start=True, stop=True),
                    waits=[ev_cdrow, ev_Gm])
    ev_cdb = P.add(V, lambda v: v.tensor_copy(B["cdb"][:, :], B["p_small"][:, 192:256]),
                   waits=[ev_cdmm, st["cdb_freeT"]])

    prev_evac = [ev_dtT, ev_decT, ev_BT]
    ev_x1 = ev_x2 = None
    prev_round_tt = {0: None, 1: st.get("pw3_free")}
    for r in range(4):
        pxb = B["p_xT"] if r % 2 == 0 else B["p_w3"].bitcast(BF16)
        ev_tr = None
        for k in range(8):
            pc = 8 * r + k
            w = (list(ev_ready["x"]) + prev_evac + [prev_round_tt[r % 2]]) if k == 0 else ()
            ev_tr = P.add(
                T, lambda t, pc=pc, k=k, pxb=pxb: t.transpose(
                    pxb[:, 128 * k:128 * k + 128], B["xy"][:, pc, sl], B["idn"][:, :]),
                waits=w)

        def xdt_fn(v, r=r, pxb=pxb):
            dt_ap = B["dt_tm"][:, 16 * r:16 * r + 16]
            dt_ap = dt_ap.rearrange("p (pc par) -> p pc par", par=2).unsqueeze(3).broadcast_to([128, 8, 2, 64])
            return v.tensor_tensor(B["xdt"][:, 1024 * r:1024 * r + 1024], pxb[:, 0:1024], dt_ap, ALU.mult)
        w = [ev_tr, ev_dtT] + ([st["xdt_freeT"]] if r == 0 else [])
        ev_x1 = P.add(V, xdt_fn, waits=w)
        prev_round_tt[r % 2] = ev_x1

        def xdtd_fn(v, r=r):
            de_ap = B["decay_tm"][:, 16 * r:16 * r + 16]
            de_ap = de_ap.rearrange("p (pc par) -> p pc par", par=2).unsqueeze(3).broadcast_to([128, 8, 2, 64])
            return v.tensor_tensor(B["xdtd"][:, 1024 * r:1024 * r + 1024], B["xdt"][:, 1024 * r:1024 * r + 1024], de_ap, ALU.mult)
        w = [ev_decT] + ([st["xdtd_freeT"]] if r == 0 else [])
        ev_x2 = P.add(V, xdtd_fn, waits=w)
        prev_evac = []
    ev_xdt, ev_xdtd = ev_x1, ev_x2
    st["pxT_free"] = prev_round_tt[0]
    st["pw3_free"] = prev_round_tt[1]

    ev_ST = [None] * 16
    ev_CE = [None] * 16
    ev_y_evacs = []
    last_pair_mm = None
    st["eacsb_free"] = [st["psh_free"], st["psmall_free"]]

    def emit_group(g):
        p = g % 2
        q4 = g % 4
        rs = slice(4 * g, 4 * g + 4)
        wst = [st["stage_free"][p], ev_negacs, ev_eacs] + (st["stage_ok"] if g < 2 else [])
        P.dma(G, B["stageL"][p][1:2, :], B["negAcs"][rs, :], waits=wst, sem=f"stg{p}")
        P.dma(G, B["stageR"][p][0:1, :], B["Acs"][rs, :], sem=f"stg{p}")
        e3 = P.dma(G, B["eacsrow"][p][0:1, :], B["eacs"][rs, :], sem=f"stg{p}")
        evd = None
        for j in range(4):
            w = [st["pdiff_free"][p], e3] if j == 0 else ()
            evd = P.add(
                T, lambda t, p=p, j=j: t.matmul(
                    B["p_diff"][p][:, 128 * j:128 * j + 128],
                    B["stageL"][p][0:2, 128 * j:128 * j + 128],
                    B["stageR"][p][0:2, 128 * j:128 * j + 128],
                    start=True, stop=True),
                waits=w)
        ebank = B["p_sh"] if (g % 2 == 0) else B["p_small"]
        wb = [st["eacsb_free"][g % 2]]
        if g == 1:
            wb.append(ev_cdb)
        ev_eb = P.add(T, lambda t, p=p, ebank=ebank: t.matmul(ebank[:, :], B["ones1_bf"][0:1, :], B["eacsrow"][p][0:1, :],
                                                 start=True, stop=True),
                      waits=wb)
        ev_min = P.add(A, lambda a, p=p: a.activation(B["dmin"][p][:, :], B["p_diff"][p][:, :], AF.Relu, scale=-1.0),
                       waits=[st["dmin_free"][p], evd])
        st["pdiff_free"][p] = ev_min
        st["stage_free"][p] = ev_eb
        ev_E = P.add(A, lambda a, p=p: a.activation(B["dmin"][p][:, :], B["dmin"][p][:, :], AF.Exp, scale=-1.0))

        def st_fn(v, p=p, q4=q4):
            gm_ap = B["Gm"][:, :].unsqueeze(1).broadcast_to([128, 4, 128])
            return v.tensor_tensor(B["ST"][q4][:, :], B["dmin"][p][:, :], gm_ap, ALU.mult)
        ev_st_ = P.add(V, st_fn, waits=[st["ST_free"][q4], ev_E, ev_Gm])
        st["dmin_free"][p] = ev_st_
        ev_ST[g] = ev_st_

        def ce_fn(v, q4=q4, sl=sl, ebank=ebank):
            c_ap = B["C_fm"][:, sl].unsqueeze(1).broadcast_to([128, 4, 128])
            return v.tensor_tensor(B["CE"][q4][:, :], c_ap, ebank[:, :], ALU.mult)
        ev_ce = P.add(V, ce_fn, waits=[st["CE_free"][q4], ev_eb])
        st["eacsb_free"][g % 2] = ev_ce
        ev_CE[g] = ev_ce

    def emit_yfill(q):
        nonlocal last_pair_mm
        b = q % 2
        last = None
        for m in range(4):
            i = 4 * q + m
            h0, h1 = 2 * i, 2 * i + 1
            g0, j0 = h0 // 4, h0 % 4
            g1, j1 = h1 // 4, h1 % 4
            col = slice(128 * m, 128 * m + 128)
            w0 = [ev_ST[g0], ev_ST[g1], ev_CE[g0], ev_CE[g1], ev_xdt]
            if m == 0:
                w0 = w0 + [st["pY_free"][b]] + list(st["states_bf_ready"])
            P.add(T, lambda t, b=b, col=col, h0=h0, g0=g0, j0=j0: t.matmul(
                B["p_Y"][b][0:64, col], B["xdt"][:, 64 * h0:64 * h0 + 64],
                B["ST"][g0 % 4][:, 128 * j0:128 * j0 + 128],
                start=True, stop=False, tile_position=(0, 0)),
                waits=w0)
            P.add(T, lambda t, b=b, col=col, h0=h0, g0=g0, j0=j0: t.matmul(
                B["p_Y"][b][0:64, col], B["STATES_bf"][:, 64 * h0:64 * h0 + 64],
                B["CE"][g0 % 4][:, 128 * j0:128 * j0 + 128],
                start=False, stop=False, tile_position=(0, 0)))
            P.add(T, lambda t, b=b, col=col, i=i, sl=sl: t.matmul(
                B["p_Y"][b][0:64, col], B["Ddiag"][:, i, 0:64], B["xy"][:, i, sl],
                start=False, stop=True, tile_position=(0, 0)))
            P.add(T, lambda t, b=b, col=col, h1=h1, g1=g1, j1=j1: t.matmul(
                B["p_Y"][b][64:128, col], B["xdt"][:, 64 * h1:64 * h1 + 64],
                B["ST"][g1 % 4][:, 128 * j1:128 * j1 + 128],
                start=True, stop=False, tile_position=(0, 64)))
            P.add(T, lambda t, b=b, col=col, h1=h1, g1=g1, j1=j1: t.matmul(
                B["p_Y"][b][64:128, col], B["STATES_bf"][:, 64 * h1:64 * h1 + 64],
                B["CE"][g1 % 4][:, 128 * j1:128 * j1 + 128],
                start=False, stop=False, tile_position=(0, 64)))
            last = P.add(T, lambda t, b=b, col=col, i=i, sl=sl: t.matmul(
                B["p_Y"][b][64:128, col], B["Ddiag"][:, i, 64:128], B["xy"][:, i, sl],
                start=False, stop=True, tile_position=(0, 64)))
        last_pair_mm = last

        def yevac_fn(a, b=b, q=q, sl=sl):
            return a.copy(B["xy"][:, 4 * q:4 * q + 4, sl], B["p_Y"][b][:, :])
        ev_ye = P.add(A, yevac_fn, waits=[last])
        st["pY_free"][b] = ev_ye
        ev_y_evacs.append(ev_ye)
        for g in (2 * q, 2 * q + 1):
            st["ST_free"][g % 4] = last
            st["CE_free"][g % 4] = last

    for q in range(9):
        if q < 8:
            emit_group(2 * q)
            emit_group(2 * q + 1)
        if q >= 1:
            emit_yfill(q - 1)

    ev_states = []
    sbf_ready = []
    psh_free = st["eacsb_free"][0]
    last_smm = None
    for j in range(8):
        colj = slice(512 * j, 512 * j + 512)
        wmm = [ev_xdtd, ev_BT, psh_free] + ([last_pair_mm] if j == 0 else [])
        ev_smm = P.add(T, lambda t, colj=colj: t.matmul(B["p_sh"][:, :], B["B_tm"][:, :], B["xdtd"][:, colj],
                                                        start=True, stop=True),
                       waits=wmm)
        last_smm = ev_smm

        def tmul_fn(v, j=j, colj=colj):
            cdb_ap = B["cdb"][:, 8 * j:8 * j + 8].unsqueeze(2).broadcast_to([128, 8, 64])
            return v.tensor_tensor(B["tscr"][:, :], B["STATES_bf"][:, colj], cdb_ap, ALU.mult)
        ev_tm = P.add(V, tmul_fn, waits=[ev_cdb, last_pair_mm])
        ev_up = P.add(V, lambda v, colj=colj: v.tensor_tensor(B["STATES_bf"][:, colj], B["tscr"][:, :], B["p_sh"][:, :], ALU.add),
                      waits=[ev_smm, ev_tm])
        sbf_ready.append(ev_up)
        psh_free = ev_up
        ev_states.append(ev_up)

    st["states_bf_ready"] = sbf_ready
    st["psh_free"] = psh_free
    st["psmall_free"] = st["eacsb_free"][1]
    st["cdrow_free"] = ev_cdmm
    st["xdt_freeT"] = last_pair_mm
    st["xdtd_freeT"] = last_smm
    st["dtbf_freeT"] = ev_t1
    st["decaybf_freeT"] = ev_t1
    st["Btm_freeT"] = last_smm
    st["cdb_freeT"] = ev_states[-1]
    return {"y": ev_y_evacs, "states": ev_states}



def build_program():
    nc = bass.Bass("TRN2", target_bir_lowering=False)

    d_u = nc.dram_tensor("u_fm", [DM, 2048], F32, kind="ExternalInput").ap()
    d_win = nc.dram_tensor("winT_t", [N_ECH, 128, 16, 128], F32, kind="ExternalInput").ap()
    d_wout = nc.dram_tensor("woutT_t", [16, 128, 32, 128], F32, kind="ExternalInput").ap()
    d_w1 = nc.dram_tensor("w1T_t", [N_HCH, 128, 16, 128], F32, kind="ExternalInput").ap()
    d_w2 = nc.dram_tensor("w2T_t", [N_HCH, 128, 16, 128], F32, kind="ExternalInput").ap()
    d_w3 = nc.dram_tensor("w3T_t", [N_HCH, 128, 2048], F32, kind="ExternalInput").ap()
    d_cw = nc.dram_tensor("cw_pp", [128, 34, 4], F32, kind="ExternalInput").ap()
    d_cb = nc.dram_tensor("cb_pp", [128, 34], F32, kind="ExternalInput").ap()
    d_nw = nc.dram_tensor("nw_pp", [128, 32], F32, kind="ExternalInput").ap()
    d_dtb = nc.dram_tensor("dtb", [64, 1], F32, kind="ExternalInput").ap()
    d_alog = nc.dram_tensor("alog", [64, 1], F32, kind="ExternalInput").ap()
    d_dd = nc.dram_tensor("dd32", [128, 32, 128], F32, kind="ExternalInput").ap()
    d_idn = nc.dram_tensor("idn", [128, 128], BF16, kind="ExternalInput").ap()
    d_tril = nc.dram_tensor("trilT", [128, 128], BF16, kind="ExternalInput").ap()
    d_o1f = nc.dram_tensor("o1f", [1, 128], F32, kind="ExternalInput").ap()
    d_o1b = nc.dram_tensor("o1b", [1, 128], BF16, kind="ExternalInput").ap()
    d_o128b = nc.dram_tensor("o128b", [128, 128], BF16, kind="ExternalInput").ap()
    d_o128f = nc.dram_tensor("o128f", [128, 128], F32, kind="ExternalInput").ap()
    d_o128f1 = nc.dram_tensor("o128f1", [128, 1], F32, kind="ExternalInput").ap()
    d_out = nc.dram_tensor("out_fm", [DM, 2048], F32, kind="ExternalOutput").ap()
    d_q1s = nc.dram_tensor("q1s", [N_HCH, 128, 2048], BF16, kind="Internal").ap()
    d_q2s = nc.dram_tensor("q2s", [N_HCH, 128, 2048], BF16, kind="Internal").ap()
    d_q3s = nc.dram_tensor("q3s", [N_HCH, 128, 2048], BF16, kind="Internal").ap()

    TOTAL = 212800
    BASE = 16512
    AR_A_SZ = 65536
    AR_H_SZ = 32768
    AR_X_SZ = 32768
    AR_W_SZ = 53696
    top = BASE + TOTAL
    a_base = top - AR_A_SZ
    h_base = a_base - AR_H_SZ
    x_base = h_base - AR_X_SZ
    w_base = x_base - AR_W_SZ

    Al = lambda n, s, d=F32: nc.alloc_sbuf_tensor(n, s, d)
    B = {}
    B["STATES_bf"] = Al("STATES_bf", [128, 4096], BF16)
    B["Ddiag"] = Al("Ddiag", [128, 32, 128], BF16)
    B["idn"] = Al("c_idn", [128, 128], BF16)
    B["tril"] = Al("c_tril", [128, 128], BF16)
    B["ones1_f32"] = Al("c_o1f", [1, 128])
    B["ones1_bf"] = Al("c_o1b", [1, 128], BF16)
    B["o128b"] = Al("c_o128b", [128, 128], BF16)
    B["o128f"] = Al("c_o128f", [128, 128])
    B["o128f1"] = Al("c_o128f1", [128, 1])
    B["nw"] = Al("c_nw", [128, 32])
    B["cw"] = Al("c_cw", [128, 34, 4])
    B["cb"] = Al("c_cb", [128, 34])
    B["dtbias"] = Al("c_dtb", [64, 1])
    B["A_col"] = Al("A_col", [64, 1])
    B["stash"] = Al("stash", [128, 34, 3])
    B["tpp"] = Al("tpp", [128, 4])
    B["tneg"] = Al("tneg", [128, 4])
    B["B_fm"] = Al("B_fm", [128, SEGL], BF16)
    B["C_fm"] = Al("C_fm", [128, SEGL], BF16)
    B["dt_hm"] = Al("dt_hm", [64, SEGL])
    used = TOTAL - nc.sbuf_bytes_remaining
    assert BASE + used <= w_base, f"auto overflow: used={used}, limit={w_base - BASE}"

    arA = Overlay(nc, a_base, AR_A_SZ, "aA")
    arH = Overlay(nc, h_base, AR_H_SZ, "aH")
    arX = Overlay(nc, x_base, AR_X_SZ, "aX")
    arW = Overlay(nc, w_base, AR_W_SZ, "aW")

    B["Ddiag32"] = arA.alloc("dd32", [128, 32, 128], F32)
    arA.reset()
    B["u32"] = arA.alloc("u32", [128, 16, SEGL], F32)
    arA.reset()
    B["xy"] = arA.alloc("xy", [128, 32, SEGL], BF16)
    arA.reset()
    B["hid"] = arA.alloc("hid", [128, FFN_R, SEGL], BF16)
    B["q3l"] = arA.alloc("q3l", [128, FFN_R, 2048], BF16)
    B["qtmp2"] = arA.alloc("qtmp2", [128, 2048], BF16)
    B["sig2"] = arA.alloc("sig2", [128, SEGL])

    B["h1"] = arH.alloc("h", [128, 16, SEGL], BF16)

    B["X1"] = arX.alloc("X1", [128, 16, SEGL], BF16)
    arX.reset()
    B["gslab"] = [arX.alloc(f"gslab{i}", [128, 2048], F32) for i in range(2)]
    B["gscr"] = arX.alloc("gscr", [128, 264])
    B["g3"] = arX.alloc("g3", [128, 4])
    B["grow"] = arX.alloc("grow", [1, 4])
    B["wslab3"] = arX.alloc("wslab3", [128, 2048], BF16)
    B["sqbX"] = arX.alloc("sqbX", [128, SEGL], BF16)

    B["dA"] = arW.alloc("dA", [64, 128])
    B["Acs"] = arW.alloc("Acs", [64, 128])
    B["negAcs"] = arW.alloc("negAcs", [64, 128])
    B["tdec"] = arW.alloc("tdec", [64, 128])
    B["decay"] = arW.alloc("decay", [64, 128])
    B["eacs"] = arW.alloc("eacs", [64, 128])
    B["cd"] = arW.alloc("cd", [64, 1])
    B["dtbf"] = arW.alloc("dtbf", [64, 128], BF16)
    B["decaybf"] = arW.alloc("decaybf", [64, 128], BF16)
    B["z64"] = arW.alloc("z64", [64, 128])
    B["ones512"] = arW.alloc("ones512", [1, 512])
    B["dt_tm"] = arW.alloc("dt_tm", [128, 64], BF16)
    B["decay_tm"] = arW.alloc("decay_tm", [128, 64], BF16)
    B["B_tm"] = arW.alloc("B_tm", [128, 128], BF16)
    B["Gm"] = arW.alloc("Gm", [128, 128], BF16)
    B["cdb"] = arW.alloc("cdb", [128, 64])
    B["cd_row"] = arW.alloc("cd_row", [1, 64])
    B["xdt"] = arW.alloc("xdt", [128, 4096], BF16)
    B["xdtd"] = arW.alloc("xdtd", [128, 4096], BF16)
    B["tscr"] = arW.alloc("tscr", [128, 512])
    B["stageL"] = [arW.alloc(f"stageL{i}", [2, 512]) for i in range(2)]
    B["stageR"] = [arW.alloc(f"stageR{i}", [2, 512]) for i in range(2)]
    B["eacsrow"] = [arW.alloc(f"eacsrow{i}", [1, 512], BF16) for i in range(2)]
    B["dmin"] = [arW.alloc(f"dmin{i}", [128, 512]) for i in range(2)]
    B["ST"] = [arW.alloc(f"ST{i}", [128, 512], BF16) for i in range(4)]
    B["CE"] = [arW.alloc(f"CE{i}", [128, 512], BF16) for i in range(4)]
    assert arW.off <= AR_W_SZ

    arW.reset()
    B["wslab@m1"] = [arW.alloc(f"wslab{i}", [128, 2048], BF16) for i in range(2)]
    B["dtraw"] = arW.alloc("dtraw", [64, SEGL])
    B["dts1"] = arW.alloc("dts1", [64, SEGL])
    B["dts2"] = B["dtraw"]
    B["convtmp"] = arW.alloc("convtmp", [128, SEGL])
    B["xbcraw"] = [arW.alloc(f"xbcraw{i}", [128, 1028]) for i in range(2)]
    B["convo@m0"] = [arW.alloc(f"convo{i}", [128, SEGL]) for i in range(2)]
    B["sqb@m0"] = arW.alloc("sqb0", [128, SEGL], BF16)
    B["sz@m0"] = arW.alloc("sz0", [128, SEGL])
    B["stat1@m0"] = arW.alloc("stat10", [128, SEGL])
    B["stat2@m0"] = arW.alloc("stat20", [128, SEGL])
    assert arW.off <= AR_W_SZ, f"W2 {arW.off}"

    arW.reset()
    B["wslab@m3"] = [arW.alloc(f"wslab3_{i}", [128, 2048], BF16) for i in range(2)]
    B["sz@m3"] = arW.alloc("sz3", [128, SEGL])
    B["sz2@m3"] = arW.alloc("sz3b", [128, SEGL])
    assert arW.off <= AR_W_SZ - 6272
    arW.off = AR_W_SZ - 6272
    B["sqb@m3"] = arW.alloc("sqb3", [128, SEGL], BF16)
    B["stat2@m3"] = arW.alloc("stat23", [128, SEGL])

    arW.reset()
    B["woslab"] = [arW.alloc(f"woslab{i}", [128, 4096], BF16) for i in range(2)]
    B["uscr"] = [arW.alloc(f"uscr{i}", [128, 1024]) for i in range(2)]
    B["tmpX"] = arW.alloc("tmpX", [128, 1024])
    assert arW.off <= AR_W_SZ - 4160

    arW.reset()
    B["sqb@m5"] = arW.alloc("sqb5", [128, SEGL], BF16)
    B["sz@m5"] = arW.alloc("sz5", [128, SEGL])
    B["stat1@m5"] = arW.alloc("stat15", [128, SEGL])
    B["stat2@m5"] = arW.alloc("stat25", [128, SEGL])
    B["convo@m5"] = [arW.alloc(f"convo5_{i}", [128, SEGL]) for i in range(2)]
    assert arW.off <= AR_W_SZ

    arW.reset()
    _w1 = arW.alloc("w1slab", [128, 2048], F32)
    B["w1slab"] = [_w1, _w1]
    _w2 = arW.alloc("w2slab", [128, 2048], F32)
    B["w2slab"] = [_w2, _w2]
    B["q1"] = [arW.alloc(f"q1_{i}", [128, 2048], BF16) for i in range(2)]
    B["q2"] = [arW.alloc(f"q2_{i}", [128, 2048], BF16) for i in range(2)]
    B["qtmp"] = arW.alloc("qtmp", [128, 2048], BF16)
    B["sig"] = arW.alloc("sig", [128, SEGL])
    B["outstg"] = [arW.alloc(f"outstg{i}", [128, 512]) for i in range(2)]
    B["w3slab"] = B["w2slab"]
    assert arW.off <= AR_W_SZ - 6272, f"W6 {arW.off}"

    B["p_xT"] = nc.alloc_psum_tensor("p_xT", [128, 1024], BF16)
    p_ip = [nc.alloc_psum_tensor(f"p_ip{i}", [128, 1024], F32) for i in range(3)]
    B["p_ip"] = p_ip
    B["p_Y"] = [p_ip[0][:, 0:512], p_ip[0][:, 512:1024]]
    B["p_diff"] = [p_ip[1][:, 0:512], p_ip[1][:, 512:1024]]
    B["p_sh"] = p_ip[2][:, 0:512]
    B["p_small"] = p_ip[2][:, 512:1024]
    B["p_w3"] = nc.alloc_psum_tensor("p_w3", [128, 512], F32)

    P = Pipe(nc)
    for s in (sV, sA, "sT", "sGc", "stg0", "stg1", "cdr"):
        P.new_sem(s)

    ld = []
    for dst, src in ((B["idn"], d_idn), (B["tril"], d_tril), (B["ones1_f32"], d_o1f),
                     (B["ones1_bf"], d_o1b), (B["o128b"], d_o128b), (B["o128f"], d_o128f),
                     (B["o128f1"], d_o128f1),
                     (B["nw"], d_nw), (B["cw"], d_cw), (B["cb"], d_cb),
                     (B["dtbias"], d_dtb), (B["A_col"], d_alog), (B["Ddiag32"], d_dd)):
        ld.append(P.dma(S, dst[:], src, sem="dini"))
    ev_ini = [ld[-1]]
    e = P.add(A, lambda a: a.activation(B["A_col"][:, 0:1], B["A_col"][:, 0:1], AF.Exp),
              waits=ev_ini)
    P.add(V, lambda v: v.tensor_scalar_mul(B["A_col"][:, 0:1], B["A_col"][:, 0:1], -1.0), waits=[e])
    ev_stash0 = P.add(V, lambda v: v.memset(B["stash"][:, :, :], 0.0))

    sst = ssd_init(P, B, ev_consts=ev_ini)

    pfree = {"p_Y0": None, "p_Y1": None, "p_diff0": None, "p_diff1": None,
             "p_sh": None, "p_small": None, "p_w3": None}

    def pf_pair(names):
        return [pfree[n] for n in names]

    ev_gamma_done = [None]
    qcache_done = {}
    wfree3 = [None, None, None]
    out_dma_evs = []

    gamma_state = {"col": 0, "gfree": [None, None], "last": None}

    def emit_gamma_batch(n):
        srcs = []
        for dw in (d_w1, d_w2, d_w3):
            for hc in range(N_HCH):
                srcs.append((dw, hc))
        while n > 0 and gamma_state["col"] < len(srcs):
            col = gamma_state["col"]
            dw, hc = srcs[col]
            par = col % 2
            if dw is d_w3:
                s_ = dw[hc, :, :]
            else:
                s_ = dw[hc, :, :, :].rearrange("p a b -> p (a b)")
            gf = gamma_state["gfree"][par]
            evd = P.dma(S, B["gslab"][par][:, :], s_,
                        waits=list(gf) if isinstance(gf, tuple) else [gf],
                        sem=f"dg{par}")
            ev_ra = P.add(A, lambda a, par=par, col=col: a.activation(
                B["gslab"][par][:, 0:1024], B["gslab"][par][:, 0:1024], AF.Abs,
                accum_out=B["gscr"][:, 2 * col:2 * col + 1]), waits=[evd])
            ev_rb = P.add(A, lambda a, par=par, col=col: a.activation(
                B["gslab"][par][:, 1024:2048], B["gslab"][par][:, 1024:2048], AF.Abs,
                accum_out=B["gscr"][:, 2 * col + 1:2 * col + 2]))
            gamma_state["gfree"][par] = (ev_ra, ev_rb)
            gamma_state["last"] = (ev_ra, ev_rb)
            gamma_state["col"] = col + 1
            n -= 1

    def emit_gamma_fold():
        last_red = list(gamma_state["last"])
        assert gamma_state["col"] == 3 * N_HCH
        evs = []
        for idx in range(3):
            evs.append(P.add(V, lambda v, idx=idx: v.reduce_sum(
                B["g3"][:, idx:idx + 1], B["gscr"][:, 88 * idx:88 * idx + 88], AX.X),
                waits=last_red))
        ev_z = P.add(V, lambda v: v.memset(B["g3"][:, 3:4], 0.0))
        ev_gmm = P.add(T, lambda t: t.matmul(B["p_w3"][0:1, 0:4], B["o128f1"][:, 0:1], B["g3"][:, 0:4],
                                             start=True, stop=True),
                       waits=[ev_z] + evs + [pfree["p_w3"]])
        ev_gr = P.add(V, lambda v: v.tensor_copy(B["grow"][0:1, 0:4], B["p_w3"][0:1, 0:4]),
                      waits=[ev_gmm])
        ev_gb = P.add(T, lambda t: t.matmul(B["p_w3"][:, 4:8], B["ones1_f32"][0:1, :], B["grow"][0:1, 0:4],
                                            start=True, stop=True), waits=[ev_gr])
        ev_tpp = P.add(V, lambda v: v.tensor_scalar(B["tpp"][:, 0:4], B["p_w3"][:, 4:8],
                                                    0.5 / NW1, 0.5 * GAMMA_EPS, ALU.mult, ALU.add),
                       waits=[ev_gb])
        ev_tn = P.add(V, lambda v: v.tensor_scalar_mul(B["tneg"][:, 0:4], B["tpp"][:, 0:4], -1.0))
        pfree["p_w3"] = ev_tpp
        ev_gamma_done[0] = ev_tn

    for seg in range(NSEG):
        l0 = seg * SEGL
        wu = (P.barrier() + list(qcache_done.values()) + out_dma_evs[-2:]) if seg else [sst["ST_CE_init"]]
        evu = []
        for half in range(2):
            src = d_u[1024 * half:1024 * (half + 1), l0:l0 + SEGL]
            src = src.rearrange("(k p) l -> p k l", p=128)
            evu.append(P.dma(S, B["u32"][:, 8 * half:8 * (half + 1), :], src,
                             waits=wu, sem="du"))
        ev_smm = ev_umm = None
        smm_by_k = []
        for k in range(16):
            sqbuf = B["sqb@m0"] if k % 2 == 0 else B["sqbX"]
            evq = P.add(A, lambda a, k=k, sqbuf=sqbuf: a.activation(sqbuf[:, :], B["u32"][:, k, :], AF.Square),
                        waits=evu + (wu if k < 2 else []) + ([smm_by_k[k - 2]] if k >= 2 else []))
            for lh in range(2):
                ev_smm = P.add(T, lambda t, k=k, lh=lh, sqbuf=sqbuf: t.matmul(
                    B["p_diff"][lh][:, :], B["o128b"][:, :], sqbuf[:, 512 * lh:512 * lh + 512],
                    start=(k == 0), stop=(k == 15)),
                    waits=[evq] + ([pfree[f"p_diff{lh}"]] if k == 0 else []))
            smm_by_k.append(ev_smm)
        for k in range(16):
            for lh in range(2):
                ev_umm = P.add(T, lambda t, k=k, lh=lh: t.matmul(
                    B["p_Y"][lh][:, :], B["o128f"][:, :], B["u32"][:, k, 512 * lh:512 * lh + 512],
                    start=(k == 0), stop=(k == 15)),
                    waits=(evu + [pfree["p_Y0"], pfree["p_Y1"]]) if k == 0 else ())
        P.add(A, lambda a: a.mul(B["stat1@m0"][:, :], B["p_ip"][0][:, :], 1.0 / DM),
              waits=[ev_umm, ev_smm] + (wu if seg else []))
        ev_v2 = P.add(A, lambda a: a.mul(B["stat2@m0"][:, :], B["p_ip"][1][:, :], 1.0 / DM))
        pfree["p_Y0"] = pfree["p_Y1"] = pfree["p_diff0"] = pfree["p_diff1"] = ev_v2
        evm = P.add(A, lambda a: a.activation(B["sz@m0"][:, :], B["stat1@m0"][:, :], AF.Square),
                    waits=[ev_v2])
        evv = P.add(V, lambda v: v.scalar_tensor_tensor(B["stat2@m0"][:, :], B["stat2@m0"][:, :], float(EPS_LN), B["sz@m0"][:, :], ALU.add, ALU.subtract),
                    waits=[evm])
        evl = P.add(A, lambda a: a.activation(B["stat2@m0"][:, :], B["stat2@m0"][:, :], AF.Ln),
                    waits=[evv])
        ev_rstd = P.add(A, lambda a: a.activation(B["stat2@m0"][:, :], B["stat2@m0"][:, :], AF.Exp, scale=-0.5))
        ev_h1k = []
        for k in range(16):
            evs_ = P.add(V, lambda v, k=k: v.tensor_tensor(B["convo@m0"][k % 2][:, :], B["u32"][:, k, :], B["stat1@m0"][:, :], ALU.subtract),
                         waits=[ev_rstd])
            ev_h1k.append(P.add(V, lambda v, k=k: v.tensor_tensor(B["h1"][:, k, :], B["convo@m0"][k % 2][:, :], B["stat2@m0"][:, :], ALU.mult)))
        ev_h1done = [ev_h1k[-1]]


        wfree = wfree3 if seg else [None, None, None]
        slabs3 = [B["wslab@m1"][0], B["wslab@m1"][1], B["wslab3"]]
        psum3 = [(B["p_Y"][0], B["p_Y"][1]), (B["p_diff"][0], B["p_diff"][1]),
                 (B["p_sh"], B["p_small"])]
        slot_free = {0: [pfree["p_Y0"], pfree["p_Y1"]],
                     1: [pfree["p_diff0"], pfree["p_diff1"]],
                     2: [pfree["p_sh"], pfree["p_small"]]}
        xbf = [None, None]
        cvf = [None, None]
        ev_x_ready = []
        ev_dt_ready = []
        for ci, ec in enumerate(range(32, 67)):
            par = ci % 3
            p3 = ci % 3
            if seg == 0:
                emit_gamma_batch(2)
            src = d_win[ec, :, :, :].rearrange("p a b -> p (a b)")
            evd = P.dma(G, slabs3[par][:, :], src, waits=[wfree[par]] + (P.barrier() if ci < 3 and seg else []),
                        sem=f"dwin{par}")
            pb = psum3[p3]
            ev_mm = None
            for lh in range(2):
                for k in range(16):
                    w = [ev_h1k[k]]
                    if lh == 0 and k == 0:
                        w = w + [evd] + slot_free[p3]
                    ev_mm = P.add(T, lambda t, sl3=slabs3[par], lh=lh, k=k, pb=pb: t.matmul(
                        pb[lh][:, :], sl3[:, 128 * k:128 * k + 128],
                        B["h1"][:, k, 512 * lh:512 * lh + 512],
                        start=(k == 0), stop=(k == 15)), waits=w)
            wfree[par] = ev_mm
            if ec < 66:
                m = ec - 32
                par = ci % 2
                pfull = p_ip[p3]
                co = B["convo@m0"][par]
                e1_ = P.add(V, lambda v, co=co, m=m: v.tensor_scalar(
                    co[:, 0:3], B["stash"][:, m, 0:3], B["cw"][:, m, 0:1], None, ALU.mult),
                    waits=[ev_mm, ev_stash0, cvf[par]])
                e2_ = P.add(V, lambda v, co=co, m=m, pfull=pfull: v.tensor_scalar(
                    co[:, 3:1024], pfull[:, 0:1021], B["cw"][:, m, 0:1], None, ALU.mult))
                P.add(V, lambda v, co=co, m=m: v.scalar_tensor_tensor(
                    co[:, 0:2], B["stash"][:, m, 1:3], B["cw"][:, m, 1:2], co[:, 0:2],
                    ALU.mult, ALU.add))
                P.add(V, lambda v, co=co, m=m, pfull=pfull: v.scalar_tensor_tensor(
                    co[:, 2:1024], pfull[:, 0:1022], B["cw"][:, m, 1:2], co[:, 2:1024],
                    ALU.mult, ALU.add))
                P.add(V, lambda v, co=co, m=m: v.scalar_tensor_tensor(
                    co[:, 0:1], B["stash"][:, m, 2:3], B["cw"][:, m, 2:3], co[:, 0:1],
                    ALU.mult, ALU.add))
                P.add(V, lambda v, co=co, m=m, pfull=pfull: v.scalar_tensor_tensor(
                    co[:, 1:1024], pfull[:, 0:1023], B["cw"][:, m, 2:3], co[:, 1:1024],
                    ALU.mult, ALU.add))
                evc = P.add(V, lambda v, co=co, m=m, pfull=pfull: v.scalar_tensor_tensor(
                    co[:, 0:1024], pfull[:, 0:1024], B["cw"][:, m, 3:4], co[:, 0:1024],
                    ALU.mult, ALU.add))
                ev_sth = P.add(V, lambda v, m=m, pfull=pfull: v.tensor_copy(
                    B["stash"][:, m, :], pfull[:, 1021:1024]))
                slot_free[p3] = [evc, ev_sth]
                if m < 32:
                    dst = B["xy"][:, m, :]
                elif m == 32:
                    dst = B["B_fm"][:, :]
                else:
                    dst = B["C_fm"][:, :]
                evx = P.add(A, lambda a, dst=dst, m=m, co=co: a.activation(
                    dst, co[:, :], AF.Silu, bias=B["cb"][:, m:m + 1]),
                    waits=[evc])
                cvf[par] = evx
                ev_x_ready = [evx]
            else:
                eva = P.add(A, lambda a, pb=pb: a.copy(B["dtraw"][0:64, 0:512], pb[0][0:64, :]),
                            waits=[ev_mm])
                eva2 = P.add(A, lambda a, pb=pb: a.copy(B["dtraw"][0:64, 512:1024], pb[1][0:64, :]))
                slot_free[p3] = [eva2]
                ev1 = P.add(V, lambda v: v.tensor_scalar(B["dts1"][0:64, :], B["dtraw"][0:64, :],
                                                         B["dtbias"][:, 0:1], None, ALU.add),
                            waits=[eva2])
                P.add(A, lambda a: a.activation(B["dts2"][0:64, :], B["dts1"][0:64, :], AF.Abs),
                      waits=[ev1])
                P.add(A, lambda a: a.activation(B["dts2"][0:64, :], B["dts2"][0:64, :], AF.Exp, scale=-1.0))
                ev4 = P.add(A, lambda a: a.activation(B["dts2"][0:64, :], B["dts2"][0:64, :], AF.Ln, bias=1.0))
                P.add(V, lambda v: v.tensor_scalar_max(B["dts1"][0:64, :], B["dts1"][0:64, :], 0.0),
                      waits=[ev4])
                ev6 = P.add(V, lambda v: v.tensor_tensor(B["dt_hm"][0:64, :], B["dts1"][0:64, :], B["dts2"][0:64, :], ALU.add))
                ev_dt_ready = [ev6]

        m1bar = P.barrier()
        P.add(V, lambda v: v.memset(B["z64"][:, :], 0.0), waits=m1bar)
        sevs = []
        for i2 in range(2):
            sevs.append(P.add(V, lambda v, i2=i2: v.memset(B["stageL"][i2][0:1, :], 1.0)))
        ev_o2 = P.add(V, lambda v: v.memset(B["ones512"][0:1, :], 1.0))
        for i2 in range(2):
            sevs.append(P.dma(G, B["stageR"][i2][1:2, :], B["ones512"][0:1, :],
                              waits=[ev_o2, sst["stage_free"][i2]], sem=f"stg{i2}"))
        sst["stage_ok"] = sevs + [P.last(V)]
        sst["stage_free"] = [sevs[-2], sevs[-1]]
        sst["pY_free"] = [slot_free[0][0], slot_free[0][0]]
        sst["pdiff_free"] = [slot_free[1][0], slot_free[1][0]]
        sst["psh_free"] = slot_free[2][0]
        sst["psmall_free"] = slot_free[2][0]
        r = None
        sst["pw3_free"] = pfree["p_w3"]
        for c in range(8):
            if seg == 0:
                emit_gamma_batch(2)
            r = emit_ssd_chunk(P, B, c, {"x": ev_x_ready, "BC": ev_x_ready,
                                         "dt": ev_dt_ready}, sst)
        pfree["p_w3"] = sst["pw3_free"]
        ev_ssd_done = r["y"] + r["states"]

        m2bar = P.barrier()
        slot_free = {0: [sst["pY_free"][0]], 1: [sst["pdiff_free"][0]]}
        ev_rms_mm = None
        ev_yg_prev = None
        ev_rms_prev = None
        ev_ygs = []
        ev_sqs = []
        slabs3b = [B["wslab@m3"][0], B["wslab@m3"][1], B["wslab3"]]
        for ec in range(32):
            par = ec % 3
            par2 = ec % 2
            if seg == 0:
                emit_gamma_batch(2)
            src = d_win[ec, :, :, :].rearrange("p a b -> p (a b)")
            evd = P.dma(G, slabs3b[par][:, :], src,
                        waits=[wfree[par]] + (m2bar if ec < 3 else []), sem=f"dwin{par}")
            pb = B["p_Y"] if par2 == 0 else B["p_diff"]
            ev_mm = None
            for lh in range(2):
                for k in range(16):
                    w = []
                    if lh == 0 and k == 0:
                        w = [evd] + slot_free[par2] + (ev_ssd_done if ec < 2 else [])
                    ev_mm = P.add(T, lambda t, sl3=slabs3b[par], lh=lh, k=k, pb=pb: t.matmul(
                        pb[lh][:, :], sl3[:, 128 * k:128 * k + 128],
                        B["h1"][:, k, 512 * lh:512 * lh + 512],
                        start=(k == 0), stop=(k == 15)), waits=w)
            wfree[par] = ev_mm
            pfull = B["p_ip"][par2]
            szb = B["sz@m3"] if par2 == 0 else B["sz2@m3"]
            evs2 = P.add(A, lambda a, pfull=pfull, szb=szb: a.activation(szb[:, :], pfull[:, :], AF.Silu),
                         waits=[ev_mm, ev_ygs[ec - 2] if ec >= 2 else None])
            slot_free[par2] = [evs2]
            evyg = P.add(V, lambda v, ec=ec, szb=szb: v.tensor_tensor(B["xy"][:, ec, :], B["xy"][:, ec, :], szb[:, :], ALU.mult),
                         waits=[evs2] + (ev_ssd_done if ec < 2 else []))
            ev_yg_prev = evyg
            ev_ygs.append(evyg)
            if ec >= 2:
                ecl = ec - 2
                sqb_l = B["sqb@m3"] if ecl % 2 == 0 else B["sqb@m0"]
                for lh in range(2):
                    ev_rms_mm = P.add(T, lambda t, ecl=ecl, lh=lh, sqb_l=sqb_l: t.matmul(
                        (B["p_sh"] if lh == 0 else B["p_small"])[:, :], B["o128b"][:, :],
                        sqb_l[:, 512 * lh:512 * lh + 512],
                        start=(ecl == 0), stop=False),
                        waits=[ev_sqs[ecl]] + ([sst["psh_free"], sst["psmall_free"]] if ecl == 0 else []))
                ev_rms_prev = ev_rms_mm
            sqbuf = B["sqb@m3"] if ec % 2 == 0 else B["sqb@m0"]
            ev_sqs.append(P.add(A, lambda a, ec=ec, sqbuf=sqbuf: a.activation(
                sqbuf[:, :], B["xy"][:, ec, :], AF.Square),
                waits=[evyg, ev_rms_prev]))
        m3bar = P.barrier()
        for ecl in (30, 31):
            sqb_l = B["sqb@m3"] if ecl % 2 == 0 else B["sqb@m0"]
            for lh in range(2):
                ev_rms_mm = P.add(T, lambda t, ecl=ecl, lh=lh, sqb_l=sqb_l: t.matmul(
                    (B["p_sh"] if lh == 0 else B["p_small"])[:, :], B["o128b"][:, :],
                    sqb_l[:, 512 * lh:512 * lh + 512],
                    start=False, stop=(ecl == 31)),
                    waits=[ev_sqs[ecl]])
        ev1 = P.add(V, lambda v: v.tensor_scalar(B["stat2@m3"][:, 0:512], B["p_sh"][:, :], 1.0 / DI, float(EPS_RMS), ALU.mult, ALU.add),
                    waits=[ev_rms_mm])
        ev2 = P.add(V, lambda v: v.tensor_scalar(B["stat2@m3"][:, 512:1024], B["p_small"][:, :], 1.0 / DI, float(EPS_RMS), ALU.mult, ALU.add))
        pfree["p_sh"] = ev2
        pfree["p_small"] = ev2
        P.add(A, lambda a: a.activation(B["stat2@m3"][:, :], B["stat2@m3"][:, :], AF.Ln), waits=[ev2])
        ev_rms = P.add(A, lambda a: a.activation(B["stat2@m3"][:, :], B["stat2@m3"][:, :], AF.Exp, scale=-0.5))
        ev_yn_done = [ev_ygs[-1]]

        wfree3 = list(wfree)
        if seg == 0:
            emit_gamma_fold()

        wofree = [None, None]
        ufree = [None, None]
        ev_x1k_m4 = []
        slot_free = {0: [slot_free[0][0]], 1: [slot_free[1][0]],
                     2: [pfree["p_sh"], pfree["p_small"]]}
        ev_x1w = None
        for dc in range(16):
            par = dc % 2
            src = d_wout[dc, :, :, :].rearrange("p a b -> p (a b)")
            evd = P.dma(G, B["woslab"][par][:, :], src,
                        waits=[wofree[par]] + (m3bar if dc < 2 else []), sem=f"dwo{par}")
            nw_ap = B["nw"][:, :].unsqueeze(2).broadcast_to([128, 32, 128])
            evd = P.add(V, lambda v, par=par, nw_ap=nw_ap: v.tensor_tensor(
                B["woslab"][par][:, :].rearrange("p (a b) -> p a b", b=128), B["woslab"][par][:, :].rearrange("p (a b) -> p a b", b=128),
                nw_ap, ALU.mult), waits=[evd])
            uev = P.dma(S, B["uscr"][par][:, :], d_u[128 * dc:128 * dc + 128, l0:l0 + SEGL],
                        waits=[ufree[par]] + (m3bar if dc < 2 else [])
                        + ([ev_gamma_done[0]] if dc < 2 and seg == 0 else []),
                        sem=f"dus{par}")
            p3w = dc % 3
            pfullw = B["p_ip"][p3w]
            pb = [pfullw[:, 0:512], pfullw[:, 512:1024]]
            ev_mm = None
            for lh in range(2):
                for ec in range(32):
                    w = []
                    if lh == 0 and ec == 0:
                        w = [evd] + slot_free[p3w] + ev_yn_done
                    ev_mm = P.add(T, lambda t, par=par, lh=lh, ec=ec, pb=pb: t.matmul(
                        pb[lh][:, :], B["woslab"][par][:, 128 * ec:128 * ec + 128],
                        B["xy"][:, ec, 512 * lh:512 * lh + 512],
                        start=(ec == 0), stop=(ec == 31)), waits=w)
            wofree[par] = ev_mm
            et1 = P.add(V, lambda v, pfullw=pfullw: v.tensor_tensor(
                B["tmpX"][:, :], pfullw[:, :], B["stat2@m3"][:, :], ALU.mult),
                waits=[ev_mm, ev_rms] + ([ev_gamma_done[0]] if dc < 2 and seg == 0 else []))
            e2 = P.add(V, lambda v, dc=dc, par=par: v.tensor_tensor(
                B["X1"][:, dc, :], B["tmpX"][:, :], B["uscr"][par][:, :], ALU.add),
                waits=[uev])
            slot_free[p3w] = [e2]
            ufree[par] = e2
            ev_x1k_m4.append(e2)
            ev_x1w = e2
        pfree["p_Y0"] = pfree["p_Y1"] = slot_free[0][0]
        pfree["p_diff0"] = pfree["p_diff1"] = slot_free[1][0]
        pfree["p_sh"] = pfree["p_small"] = slot_free[2][0]

        ev_rms_mm_guard = [ev_rms_mm] + P.barrier()
        ev_smm = ev_umm = None
        smm_by_k = []
        for k in range(16):
            sqbuf = B["sqb@m5"] if k % 2 == 0 else B["sqb@m0"]
            evq = P.add(A, lambda a, k=k, sqbuf=sqbuf: a.activation(sqbuf[:, :], B["X1"][:, k, :], AF.Square),
                        waits=[ev_x1k_m4[k]] + ([smm_by_k[k - 2]] if k >= 2 else [])
                        + (ev_rms_mm_guard if k < 2 else []))
            for lh in range(2):
                ev_smm = P.add(T, lambda t, k=k, lh=lh, sqbuf=sqbuf: t.matmul(
                    B["p_diff"][lh][:, :], B["o128b"][:, :], sqbuf[:, 512 * lh:512 * lh + 512],
                    start=(k == 0), stop=(k == 15)),
                    waits=[evq] + ([pfree[f"p_diff{lh}"]] if k == 0 else []))
            smm_by_k.append(ev_smm)
        for k in range(16):
            for lh in range(2):
                ev_umm = P.add(T, lambda t, k=k, lh=lh: t.matmul(
                    B["p_Y"][lh][:, :], B["o128b"][:, :], B["X1"][:, k, 512 * lh:512 * lh + 512],
                    start=(k == 0), stop=(k == 15)),
                    waits=[ev_x1k_m4[k]] + ([pfree["p_Y0"], pfree["p_Y1"]] if k == 0 else []))
        P.add(A, lambda a: a.mul(B["stat1@m5"][:, :], B["p_ip"][0][:, :], 1.0 / DM),
              waits=[ev_umm, ev_smm])
        ev_v2 = P.add(A, lambda a: a.mul(B["stat2@m5"][:, :], B["p_ip"][1][:, :], 1.0 / DM))
        pfree["p_Y0"] = pfree["p_Y1"] = pfree["p_diff0"] = pfree["p_diff1"] = ev_v2
        evm = P.add(A, lambda a: a.activation(B["sz@m5"][:, :], B["stat1@m5"][:, :], AF.Square),
                    waits=[ev_v2])
        evv = P.add(V, lambda v: v.scalar_tensor_tensor(B["stat2@m5"][:, :], B["stat2@m5"][:, :], float(EPS_LN), B["sz@m5"][:, :], ALU.add, ALU.subtract),
                    waits=[evm])
        P.add(A, lambda a: a.activation(B["stat2@m5"][:, :], B["stat2@m5"][:, :], AF.Ln),
              waits=[evv])
        ev_rstd = P.add(A, lambda a: a.activation(B["stat2@m5"][:, :], B["stat2@m5"][:, :], AF.Exp, scale=-0.5))
        wh2 = list(wfree) + [wofree[0], wofree[1]]
        ev_h2k = []
        for k in range(16):
            evs_ = P.add(V, lambda v, k=k: v.tensor_tensor(B["convo@m5"][k % 2][:, :], B["X1"][:, k, :], B["stat1@m5"][:, :], ALU.subtract),
                         waits=[ev_rstd] + (wh2 if k == 0 else []))
            ev_h2k.append(P.add(V, lambda v, k=k: v.tensor_tensor(B["h1"][:, k, :], B["convo@m5"][k % 2][:, :], B["stat2@m5"][:, :], ALU.mult)))
        ev_h2done = [ev_h2k[-1]]

        m5bar = P.barrier()
        rounds = [list(range(i, min(i + FFN_R, N_HCH))) for i in range(0, N_HCH, FFN_R)]
        w1f = [None, None]
        w2f = [None, None]
        w3f = [None, None]
        q1f = [None, None]
        q2f = [None, None]
        q3lf = [None] * FFN_R
        hidf = [None] * FFN_R
        x1pf = {0: pfree["p_Y0"], 1: pfree["p_Y1"]}
        x2pf = {0: pfree["p_diff0"], 1: pfree["p_diff1"]}
        w3pf = [pfree["p_w3"], pfree["p_sh"]]
        sigf = [None, None]
        qtmpf = [None, None]
        qti = [0]
        outf = [None, None]
        mam_bar = P.barrier()
        ev_last_w3 = None
        for ri, rnd in enumerate(rounds):
            hid_ready = [None] * len(rnd)
            q3_ready = [None] * len(rnd)
            for li, hc in enumerate(rnd):
                par = li % 2

                def quant(dslab, wslabs, qbufs, tcol, wf, qf, extra, dcache):
                    if seg != 0:
                        dq = P.dma(S, qbufs[par][:, :], dcache[hc, :, :],
                                   waits=[qf[0], qf[1], qcache_done.get(tcol)] + extra,
                                   sem=f"dq{tcol}{par}")
                        wf[par] = dq
                        return dq
                    d_ = P.dma(S, wslabs[par][:, :], dslab,
                               waits=[wf[0], wf[1]] + extra, sem=f"dwf{tcol}{par}")
                    qi = qti[0] % 2
                    qti[0] += 1
                    qb = B["qtmp"] if qi == 0 else B["qtmp2"]
                    if tcol == 1:
                        eb = P.add(V, lambda v, q=qbufs[par], w=wslabs[par], tc=tcol: v.tensor_scalar(
                            q[:, :], w[:, :], B["tneg"][:, tc:tc + 1], None, ALU.is_lt),
                            waits=[d_, qf[0], qf[1], qcache_done.get(tcol), ev_gamma_done[0]])
                        ea2 = P.add(V, lambda v, w=wslabs[par], qb=qb, tc=tcol: v.tensor_scalar(
                            qb[:, :], w[:, :], B["tpp"][:, tc:tc + 1], None, ALU.is_gt),
                            waits=[qtmpf[qi]])
                        eq = P.add(V, lambda v, q=qbufs[par], qb=qb: v.tensor_tensor(
                            q[:, :], qb[:, :], q[:, :], ALU.subtract))
                    else:
                        ea = P.add(A, lambda a, w=wslabs[par], qb=qb, tc=tcol: a.activation(
                            qb[:, :], w[:, :], AF.Sign, bias=B["tneg"][:, tc:tc + 1]),
                            waits=[d_, qtmpf[qi], ev_gamma_done[0]])
                        eb = P.add(A, lambda a, q=qbufs[par], w=wslabs[par], tc=tcol: a.activation(
                            q[:, :], w[:, :], AF.Sign, bias=B["tpp"][:, tc:tc + 1]),
                            waits=[qf[0], qf[1], qcache_done.get(tcol)])
                        eq = P.add(V, lambda v, q=qbufs[par], qb=qb: v.tensor_tensor(
                            q[:, :], q[:, :], qb[:, :], ALU.add),
                            waits=[ea, eb])
                    qtmpf[qi] = eq
                    wf[par] = eq
                    qcache_done[tcol] = P.dma(G, dcache[hc, :, :], qbufs[par][:, :],
                                              waits=[eq], sem=f"dqc{tcol}")
                    return eq
                eq1 = quant(d_w1[hc, :, :, :].rearrange("p a b -> p (a b)"), B["w1slab"],
                            B["q1"], 0, w1f, q1f, (m5bar if ri == 0 and li < 2 else []),
                            d_q1s)
                eq2 = quant(d_w2[hc, :, :, :].rearrange("p a b -> p (a b)"), B["w2slab"],
                            B["q2"], 1, w2f, q2f, [w3f[0], w3f[1]] + (m5bar if ri == 0 and li < 2 else []),
                            d_q2s)
                ex1 = ex2 = None
                for lh in range(2):
                    for k in range(16):
                        w = [ev_h2k[k]]
                        if lh == 0 and k == 0:
                            w = w + [eq1, x1pf[0], x1pf[1]]
                        ex1 = P.add(T, lambda t, par=par, lh=lh, k=k: t.matmul(
                            B["p_Y"][lh][:, :], B["q1"][par][:, 128 * k:128 * k + 128],
                            B["h1"][:, k, 512 * lh:512 * lh + 512],
                            start=(k == 0), stop=(k == 15)), waits=w)
                for lh in range(2):
                    for k in range(16):
                        w = []
                        if lh == 0 and k == 0:
                            w = [eq2, x2pf[0], x2pf[1]]
                        ex2 = P.add(T, lambda t, par=par, lh=lh, k=k: t.matmul(
                            B["p_diff"][lh][:, :], B["q2"][par][:, 128 * k:128 * k + 128],
                            B["h1"][:, k, 512 * lh:512 * lh + 512],
                            start=(k == 0), stop=(k == 15)), waits=w)
                q1f[0] = q1f[1] = ex1
                q2f[0] = q2f[1] = ex2
                sgb = B["sig"] if li % 2 == 0 else B["sig2"]
                es2 = P.add(A, lambda a, sgb=sgb: a.activation(sgb[:, :], B["p_ip"][0][:, :], AF.Sigmoid, scale=0.5),
                            waits=[ex1, sigf[li % 2]])
                x1pf = {0: es2, 1: es2}
                eh2 = P.add(V, lambda v, li=li, sgb=sgb: v.tensor_tensor(
                    B["hid"][:, li, :], sgb[:, :], B["p_ip"][1][:, :], ALU.mult),
                    waits=[es2, ex2, hidf[li]] + (mam_bar if ri == 0 and li < 2 else []))
                x2pf = {0: eh2, 1: eh2}
                sigf[li % 2] = eh2
                hid_ready[li] = eh2
                if seg == 0:
                    d3 = P.dma(S, B["w3slab"][par][:, :], d_w3[hc, :, :],
                               waits=[w3f[0], w3f[1], eq2], sem=f"dw3{par}")
                    qi = qti[0] % 2
                    qti[0] += 1
                    qb = B["qtmp"] if qi == 0 else B["qtmp2"]
                    ec_ = P.add(A, lambda a, par=par, qb=qb: a.activation(
                        qb[:, :], B["w3slab"][par][:, :], AF.Sign, bias=B["tneg"][:, 2:3]),
                        waits=[d3, qtmpf[qi]])
                    eb3 = P.add(A, lambda a, par=par, li=li: a.activation(
                        B["q3l"][:, li, :], B["w3slab"][par][:, :], AF.Sign, bias=B["tpp"][:, 2:3]),
                        waits=[q3lf[li], qcache_done.get(2)] + (mam_bar if ri == 0 and li < 2 else []))
                    eq3 = P.add(V, lambda v, li=li, qb=qb: v.tensor_tensor(
                        B["q3l"][:, li, :], B["q3l"][:, li, :], qb[:, :], ALU.add),
                        waits=[ec_, eb3])
                    qtmpf[qi] = eq3
                    w3f[par] = eq3
                    qcache_done[2] = P.dma(G, d_q3s[hc, :, :], B["q3l"][:, li, :],
                                           waits=[eq3], sem="dqc2")
                else:
                    eq3 = P.dma(S, B["q3l"][:, li, :], d_q3s[hc, :, :],
                                waits=[q3lf[li], qcache_done.get(2)] + (mam_bar if ri == 0 and li < 2 else []),
                                sem=f"dq3l{par}")
                q3_ready[li] = eq3
            for dc in range(16):
                for lh in range(2):
                    pwb = B["p_w3"] if (dc * 2 + lh) % 2 == 0 else B["p_sh"]
                    pw_i = (dc * 2 + lh) % 2
                    ev_mm = None
                    for li in range(len(rnd)):
                        w = []
                        if li == 0:
                            w = [w3pf[pw_i], q3_ready[-1], hid_ready[-1]] + (
                                [q3_ready[-2]] if len(rnd) > 1 else [])
                        ev_mm = P.add(T, lambda t, dc=dc, lh=lh, li=li, pwb=pwb,
                                      st_=(li == 0), sp_=(li == len(rnd) - 1): t.matmul(
                            pwb[:, :], B["q3l"][:, li, 128 * dc:128 * dc + 128],
                            B["hid"][:, li, 512 * lh:512 * lh + 512],
                            start=st_, stop=sp_), waits=w)
                    ev_last_w3 = ev_mm
                    if ri < len(rounds) - 1:
                        evac = P.add(V, lambda v, dc=dc, lh=lh, pwb=pwb: v.scalar_tensor_tensor(
                            B["X1"][:, dc, 512 * lh:512 * lh + 512], pwb[:, :], 0.5,
                            B["X1"][:, dc, 512 * lh:512 * lh + 512], ALU.mult, ALU.add),
                            waits=[ev_mm])
                    else:
                        sp = (dc * 2 + lh) % 2
                        evac = P.add(V, lambda v, dc=dc, lh=lh, pwb=pwb, sp=sp: v.scalar_tensor_tensor(
                            B["outstg"][sp][:, :], pwb[:, :], 0.5,
                            B["X1"][:, dc, 512 * lh:512 * lh + 512], ALU.mult, ALU.add),
                            waits=[ev_mm, outf[sp]])
                        odst = d_out[128 * dc:128 * dc + 128, l0 + 512 * lh:l0 + 512 * lh + 512]
                        oev = P.dma(S, odst, B["outstg"][sp][:, :], waits=[evac], sem=f"dout{sp}")
                        outf[sp] = oev
                        out_dma_evs.append(oev)
                    w3pf[pw_i] = evac
            for li in range(len(rnd)):
                hidf[li] = ev_last_w3
                q3lf[li] = ev_last_w3
        pfree["p_w3"] = w3pf[0]
        pfree["p_sh"] = w3pf[1]
        pfree["p_Y0"] = pfree["p_Y1"] = x1pf[0]
        pfree["p_diff0"] = pfree["p_diff1"] = x2pf[0]

    P.add(G, lambda g: g.engine_nop(), waits=out_dma_evs)
    P.emit()
    return nc


_NC_CACHE = [None]


def _host_layout(inputs):
    f = np.float32
    W_in = np.asarray(inputs["L0_attn_W_in"], f)
    W_out = np.asarray(inputs["L0_attn_W_out"], f)
    w1 = np.asarray(inputs["L0_ffn_w1"], f)
    w2 = np.asarray(inputs["L0_ffn_w2"], f)
    w3 = np.asarray(inputs["L0_ffn_w3"], f)
    conv_w = np.asarray(inputs["L0_attn_conv_w"], f)
    conv_b = np.asarray(inputs["L0_attn_conv_b"], f)
    nw = np.asarray(inputs["L0_attn_norm_weight"], f)
    dtb = np.asarray(inputs["L0_attn_dt_bias"], f)
    alog = np.asarray(inputs["L0_attn_A_log"], f)
    Dv = np.asarray(inputs["L0_attn_D"], f)

    WinT = np.ascontiguousarray(W_in.T)
    WinT_p = np.zeros((DM, N_ECH * 128), f)
    WinT_p[:, :8512] = WinT
    winT_t = np.ascontiguousarray(
        WinT_p.reshape(16, 128, N_ECH, 128).transpose(2, 1, 0, 3))
    woutT_t = np.ascontiguousarray(
        W_out.T.reshape(32, 128, 16, 128).transpose(2, 1, 0, 3))
    w1T_t = np.ascontiguousarray(
        w1.T.reshape(16, 128, N_HCH, 128).transpose(2, 1, 0, 3))
    w2T_t = np.ascontiguousarray(
        w2.T.reshape(16, 128, N_HCH, 128).transpose(2, 1, 0, 3))
    w3T_t = np.ascontiguousarray(w3.T.reshape(N_HCH, 128, 2048))
    cw_pp = np.ascontiguousarray(conv_w.reshape(34, 128, 4).transpose(1, 0, 2))
    cb_pp = np.ascontiguousarray(conv_b.reshape(34, 128).T)
    nw_pp = np.ascontiguousarray(nw.reshape(32, 128).T)
    dd32 = np.zeros((128, 32, 128), f)
    for i in range(32):
        for p_ in range(64):
            dd32[p_, i, p_] = Dv[2 * i]
            dd32[64 + p_, i, 64 + p_] = Dv[2 * i + 1]
    consts = dict(
        winT_t=winT_t, woutT_t=woutT_t, w1T_t=w1T_t, w2T_t=w2T_t, w3T_t=w3T_t,
        cw_pp=cw_pp, cb_pp=cb_pp, nw_pp=nw_pp,
        dtb=dtb.reshape(64, 1), alog=alog.reshape(64, 1), dd32=dd32,
        idn=np.eye(128, dtype=bfdt),
        trilT=(np.arange(128)[:, None] <= np.arange(128)[None, :]).astype(bfdt),
        o1f=np.ones((1, 128), f), o1b=np.ones((1, 128), bfdt),
        o128b=np.ones((128, 128), bfdt), o128f=np.ones((128, 128), f),
        o128f1=np.ones((128, 1), f),
    )
    return consts


def kernel(**inputs):
    consts = _host_layout(inputs)
    u = np.asarray(inputs["L0_attn_u"], np.float32)
    if _NC_CACHE[0] is None:
        _NC_CACHE[0] = build_program()
    nc = _NC_CACHE[0]
    in_maps = []
    for b in range(8):
        m = dict(consts)
        m["u_fm"] = np.ascontiguousarray(u[b].T)
        in_maps.append(m)
    res = run_bass_kernel_spmd(nc, in_maps, core_ids=list(range(8)))
    out = np.stack([np.asarray(res.results[b]["out_fm"]).T for b in range(8)])
    return np.ascontiguousarray(out, dtype=np.float32)


if __name__ == "__main__":
    nc = build_program()
    print("build ok, instructions:", nc.n_instructions())
